# version 03
# baseline (speedup 1.0000x reference)
import contextlib
import math
import os

import numpy as np
import ml_dtypes

import concourse.bass as bass
import concourse.mybir as mybir
from concourse.bass_utils import run_bass_kernel_spmd
from concourse.aot_env import allow_ici_all_to_all

F32 = mybir.dt.float32
BF16 = mybir.dt.bfloat16
AF = mybir.ActivationFunctionType
ALU = mybir.AluOpType
AX = mybir.AxisListType
BF = ml_dtypes.bfloat16

ENGINES = ("sync", "vector", "scalar", "tensor", "gpsimd")

B, T, D, H, HD = 2, 8192, 1024, 16, 64
NC = 8
TOK = 2048
NT = TOK // 128
NBLK = 16
NQB = 16
QBW = 512
CKEEP = 30.0
EXPSHIFT = -30.0
DEBUG = bool(int(os.environ.get("MK_DEBUG", "0")))
MIDDUMPS = bool(int(os.environ.get("MK_MIDDUMPS", "0")))
PHASES = int(os.environ.get("MK_PHASES", "4"))
NOCC = bool(int(os.environ.get("MK_NOCC", "0")))
NOQAUG = bool(int(os.environ.get("MK_NOQAUG", "0")))
NOLN = bool(int(os.environ.get("MK_NOLN", "0")))
NOKQ = bool(int(os.environ.get("MK_NOKQ", "0")))


def alibi_slopes(n_heads=H):
    start = 2.0 ** (-2.0 ** (-(math.log2(n_heads) - 3)))
    return np.array([start * start ** i for i in range(n_heads)], dtype=np.float32)


def head_slots():
    sl = alibi_slopes()
    ranks = list(np.argsort(sl))
    slots = [ranks[4 * s: 4 * s + 4] for s in range(4)]
    Ls = [int(math.ceil(CKEEP / float(sl[slots[s][0]]))) for s in range(4)]
    return slots, Ls, sl


def chunk_sched(L):
    out = []
    for qb in range(NQB):
        qb0 = QBW * qb
        jmax = (qb0 + QBW - 1) // 128
        jmin = max(0, int(math.ceil((qb0 - L - 127) / 128.0)))
        out.append(list(range(jmin, jmax + 1)))
    return out


class Sem:
    __slots__ = ("h", "n", "name")

    def __init__(self, h, name):
        self.h = h
        self.n = 0
        self.name = name


class Plan:
    def __init__(self):
        self.q = {e: [] for e in ENGINES}

    def em(self, eng, fn):
        self.q[eng].append(fn)

    def w(self, eng, sem, val=None):
        v = sem.n if val is None else val
        if v > 0:
            self.em(eng, lambda e, s=sem.h, vv=v: e.wait_ge(s, vv))

    def op(self, eng, fn, sem=None, k=1):
        if sem is not None:
            sem.n += k
            self.em(eng, lambda e, f=fn, s=sem.h, kk=k: f(e).then_inc(s, kk))
        else:
            self.em(eng, fn)


def build_nc():
    slots, Ls, sl = head_slots()
    sched = [chunk_sched(Ls[s]) for s in range(4)]

    nc = bass.Bass("TRN2", target_bir_lowering=False, debug=False)

    def ein(name, shape, dt=F32):
        return nc.dram_tensor(name, shape, dt, kind="ExternalInput").ap()

    x_own = ein("x_own", [TOK, D])
    wqk_in = ein("wqk", [4 * D, 128])
    wv_in = ein("wv", [2 * D, 128])
    wp_in = ein("wpt", [2 * D, D])
    wfr_in = ein("wfr", [D, D])
    wfi_in = ein("wfi", [D, D])
    bias_in = ein("fbias", [1, D])
    mtab_s = ein("mtab_s", [D, D])
    mtab_c = ein("mtab_c", [D, D])
    mtabc_s = ein("mtabc_s", [D, 128])
    mtabc_c = ein("mtabc_c", [D, 128])
    kq_in = ein("kq128", [128, 512])
    sel_in = ein("seltab", [32, 2048])
    ones_in = ein("ones32", [1, T])

    out_own = nc.dram_tensor("out_own", [TOK, D], F32, kind="ExternalOutput").ap()

    xnc = nc.dram_tensor("xnc", [D, TOK], BF16)
    xng = nc.dram_tensor("xng", [8 * D, TOK], BF16)
    mpc = nc.dram_tensor("mpc", [128, D], BF16)
    mg = nc.dram_tensor("mg", [D, D], BF16)
    kbq = nc.dram_tensor("kbq", [24, T], BF16)
    qaug = nc.dram_tensor("qaug", [4 * 70, T], BF16)
    gdr = nc.dram_tensor("gdr", [TOK, D], BF16)
    y_in = nc.dram_tensor("y_in", [2048, TOK], BF16)
    y_out = nc.dram_tensor("y_out", [2048, TOK], BF16)
    d_in = nc.dram_tensor("d_in", [32, TOK], F32)
    d_out = nc.dram_tensor("d_out", [32, TOK], F32)

    dbg = {}
    if DEBUG:
        for nm, src in (("dbg_xnc", [D, TOK]), ("dbg_mg", [D, D]),
                        ("dbg_q", [4 * 70, T]), ("dbg_g", [TOK, D]),
                        ("dbg_y", [2048, TOK])):
            dbg[nm] = nc.dram_tensor(nm, src, BF16, kind="ExternalOutput").ap()
        dbg["dbg_d"] = nc.dram_tensor("dbg_d", [32, TOK], F32, kind="ExternalOutput").ap()
        dbg["dbg_mpc"] = nc.dram_tensor("dbg_mpc", [128, D], BF16, kind="ExternalOutput").ap()
        dbg["dbg_ka"] = nc.dram_tensor("dbg_ka", [70, T], BF16, kind="ExternalOutput").ap()
        dbg["dbg_yin"] = nc.dram_tensor("dbg_yin", [2048, TOK], BF16, kind="ExternalOutput").ap()
        dbg["dbg_bsb"] = nc.dram_tensor("dbg_bsb", [128, 2048], BF16, kind="ExternalOutput").ap()
        dbg["dbg_wr"] = nc.dram_tensor("dbg_wr", [128, 8 * D], BF16, kind="ExternalOutput").ap()
        dbg["dbg_ccol"] = nc.dram_tensor("dbg_ccol", [128, 2 * 1024], BF16, kind="ExternalOutput").ap()
        dbg["dbg_csb"] = nc.dram_tensor("dbg_csb", [128, 8 * D], BF16, kind="ExternalOutput").ap()
        dbg["dbg_din"] = nc.dram_tensor("dbg_din", [32, TOK], F32, kind="ExternalOutput").ap()
        dbg["dbg_wr2"] = nc.dram_tensor("dbg_wr2", [128, 8 * D], BF16, kind="ExternalOutput").ap()
        dbg["dbg_ka3"] = nc.dram_tensor("dbg_ka3", [70, T], BF16, kind="ExternalOutput").ap()
        dbg["dbg_kbq"] = nc.dram_tensor("dbg_kbq", [24, T], BF16, kind="ExternalOutput").ap()
        dbg["dbg_wqk"] = nc.dram_tensor("dbg_wqk", [128, 4 * 1024], BF16, kind="ExternalOutput").ap()
        dbg["dbg_wv"] = nc.dram_tensor("dbg_wv", [128, 2 * 1024], BF16, kind="ExternalOutput").ap()
        dbg["dbg_p3"] = nc.dram_tensor("dbg_p3", [128, 1536], BF16, kind="ExternalOutput").ap()
        dbg["dbg_q0"] = nc.dram_tensor("dbg_q0", [70, 512], BF16, kind="ExternalOutput").ap()

    pl = Plan()
    es = contextlib.ExitStack()

    def mksem(name):
        return Sem(es.enter_context(nc.semaphore(name)), name)

    with es:
        s_ld = mksem("s_ld")
        s_x = mksem("s_x")
        s_xn = mksem("s_xn")
        s_m = mksem("s_m")
        s_xs = mksem("s_xs")
        s_q = mksem("s_q")
        s_q3 = mksem("s_q3")
        s_sp = mksem("s_sp")
        s_out = mksem("s_out")
        c_cc = mksem("c_cc")
        v_g = mksem("v_g")
        g_gp = mksem("g_gp")
        s_spy = mksem("s_spy")
        s_wld = mksem("s_wld")
        v_ev = mksem("v_ev")
        a_m = mksem("a_m")
        t_m = mksem("t_m")
        t_qk = mksem("t_qk")
        a_e = mksem("a_e")
        t_pv = mksem("t_pv")

        ident = es.enter_context(nc.sbuf_tensor("ident", [128, 128], BF16))
        maskt = es.enter_context(nc.sbuf_tensor("maskt", [128, 4 * 512], F32))
        shv = es.enter_context(nc.sbuf_tensor("shv", [128, 1], F32))
        epsv = es.enter_context(nc.sbuf_tensor("epsv", [128, 1], F32))
        ones1 = es.enter_context(nc.sbuf_tensor("ones1", [1, 128], BF16))
        vscr = es.enter_context(nc.sbuf_tensor("vscr", [1, 16], F32))
        sscr = es.enter_context(nc.sbuf_tensor("sscr", [1, 16], F32))
        gscr = es.enter_context(nc.sbuf_tensor("gscr", [1, 16], F32))

        def vfl(sem):
            pl.op("vector", lambda e: e.tensor_copy(vscr[:, 0:8], vscr[:, 8:16]), sem=sem)

        def sfl(sem):
            pl.op("scalar", lambda e: e.copy(sscr[:, 0:8], sscr[:, 8:16]), sem=sem)

        def tfl(ap, sem):
            pl.op("tensor", lambda e, o=ap: e.matmul(
                o, ident[:, 0:128], ident[:, 0:128], start=True, stop=True), sem=sem)

        def g_consts(g):
            g.memset(ident[:, :], 0.0)
            g.affine_select(out=ident[:, :], in_=ident[:, :],
                            compare_op=ALU.not_equal, fill=1.0, base=0,
                            pattern=[[-1, 128]], channel_multiplier=1)
            g.memset(maskt[:, :], 0.0)
            for v in range(4):
                g.affine_select(out=maskt[:, 512 * v:512 * (v + 1)],
                                in_=maskt[:, 512 * v:512 * (v + 1)],
                                compare_op=ALU.is_ge, fill=-100000.0,
                                base=-128 * v,
                                pattern=[[1, 512]], channel_multiplier=-1)
            g.memset(shv[:, :], EXPSHIFT)
            g.memset(epsv[:, :], 1e-5)
            return g.memset(ones1[:, :], 1.0)
        pl.op("gpsimd", g_consts)
        pl.op("gpsimd", lambda e: e.memset(gscr[:, :], 0.0), sem=g_gp)
        GCONST = g_gp.n

        esA = contextlib.ExitStack()
        with esA:
            def sbA(name, shape, dt):
                return esA.enter_context(nc.sbuf_tensor(name, shape, dt))

            xt = [sbA(f"xt{i}", [128, D], F32) for i in range(2)]
            sqt = sbA("sqt", [128, D], F32)
            xnb = [sbA(f"xnb{i}", [128, D], BF16) for i in range(2)]
            st1 = sbA("st1", [128, NT], F32)
            st2 = sbA("st2", [128, NT], F32)
            nmv = sbA("nmv", [128, NT], F32)
            m2v = sbA("m2v", [128, NT], F32)
            vav = sbA("vav", [128, NT], F32)
            sdv = sbA("sdv", [128, NT], F32)
            rsv = sbA("rsv", [128, NT], F32)
            stg = [sbA(f"stg{c}", [128, TOK], BF16) for c in range(8)]
            tp = [esA.enter_context(nc.psum_tensor(f"tp{i}", [128, D], BF16))
                  for i in range(2)]
            psA = esA.enter_context(nc.psum_tensor("psA", [128, 2048], F32))
            psB = esA.enter_context(nc.psum_tensor("psB", [128, 128], F32))
            ptt = esA.enter_context(nc.psum_tensor("ptt", [128, 128], BF16))

            a_sq = mksem("a_sq")
            v_xn = mksem("v_xn")
            t_tr = mksem("t_tr")
            v_st = mksem("v_st")

            kq_sb = sbA("kq_sb", [128, 512], F32)
            pl.op("sync", lambda e: e.dma_start(out=kq_sb[:, :], in_=kq_in[:, :]),
                  sem=s_ld, k=16)
            KQLD = s_ld.n

            for t in range(NT if not NOLN else 0):
                b = t % 2
                if t >= 2:
                    pl.w("sync", v_xn, t - 1)
                    pl.w("sync", a_sq, t - 1)
                pl.op("sync", lambda e, o=xt[b][:, :], i=x_own[128 * t:128 * (t + 1), :]:
                      e.dma_start(out=o, in_=i), sem=s_x, k=16)
                pl.w("scalar", s_x, 16 * (t + 1))
                if t >= 1:
                    pl.w("scalar", v_g)
                pl.op("scalar", lambda e, i=xt[b][:, :]:
                      e.activation(sqt[:, :], i, AF.Square))
                sfl(a_sq)
                pl.w("vector", s_x, 16 * (t + 1))
                pl.op("vector", lambda e, o=st1[:, t:t + 1], i=xt[b][:, :]:
                      e.reduce_sum(o, i, axis=AX.X))
                pl.w("vector", a_sq, t + 1)
                pl.op("vector", lambda e, o=st2[:, t:t + 1]:
                      e.reduce_sum(o, sqt[:, :], axis=AX.X), sem=v_g)
                pl.op("vector", lambda e, o=nmv[:, t:t + 1], i=st1[:, t:t + 1]:
                      e.tensor_scalar(o, i, -1.0 / D, None, ALU.mult), sem=v_g)
                pl.w("vector", v_g)
                pl.op("vector", lambda e, o=m2v[:, t:t + 1], i=nmv[:, t:t + 1]:
                      e.tensor_tensor(o, i, i, ALU.mult))
                pl.op("vector", lambda e, o=vav[:, t:t + 1], i=st2[:, t:t + 1]:
                      e.tensor_scalar(o, i, 1.0 / D, None, ALU.mult), sem=v_g)
                pl.w("vector", v_g)
                pl.op("vector", lambda e, o=vav[:, t:t + 1], i=m2v[:, t:t + 1]:
                      e.tensor_tensor(o, o, i, ALU.subtract), sem=v_g)
                pl.w("scalar", v_g)
                pl.op("scalar", lambda e, o=sdv[:, t:t + 1], i=vav[:, t:t + 1]:
                      e.activation(o, i, AF.Sqrt, bias=epsv[:, 0:1]))
                sfl(a_m)
                pl.w("vector", a_m)
                pl.op("vector", lambda e, o=rsv[:, t:t + 1], i=sdv[:, t:t + 1]:
                      e.reciprocal(o, i), sem=v_g)
                pl.w("vector", v_g)
                if t >= 2:
                    pl.w("vector", t_tr, t - 1)
                pl.op("vector", lambda e, o=xnb[b][:, :], i=xt[b][:, :],
                      s1=nmv[:, t:t + 1], s2=rsv[:, t:t + 1]:
                      e.tensor_scalar(o, i, s1, s2, ALU.add, ALU.mult))
                vfl(v_xn)
                pl.w("tensor", v_xn, t + 1)
                pl.w("tensor", g_gp, GCONST)
                if t >= 2:
                    pl.w("tensor", v_st, t - 1)
                for c in range(8):
                    pl.op("tensor", lambda e, o=tp[b][:, 128 * c:128 * (c + 1)],
                          i=xnb[b][:, 128 * c:128 * (c + 1)]:
                          e.transpose(o, i, ident[:, :]))
                tfl(psA[0:128, 0:128], t_tr)
                pl.w("vector", t_tr, t + 1)
                for c in range(8):
                    pl.op("vector", lambda e, o=stg[c][:, 128 * t:128 * (t + 1)],
                          i=tp[b][:, 128 * c:128 * (c + 1)]:
                          e.tensor_copy(o, i))
                vfl(v_st)

            pl.w("sync", v_st, v_st.n)
            for c in range(8 if not NOLN else 0):
                pl.op("sync", lambda e, o=xnc.ap()[128 * c:128 * (c + 1), :], i=stg[c][:, :]:
                      e.dma_start(out=o, in_=i), sem=s_xn, k=16)
            wr_sb = sbA("wr_sb", [128, 8 * D], BF16)
            wi_sb = sbA("wi_sb", [128, 8 * D], BF16)
            c_sb = sbA("c_sb", [128, 8 * D], BF16)
            ss_sb = sbA("ss_sb", [128, 8 * D], BF16)
            ccol = sbA("ccol", [128, 8 * 128], BF16)
            scol = sbA("scol", [128, 8 * 128], BF16)
            wtmp = [sbA(f"wtmp{i}", [128, D], F32) for i in range(2)]
            btmp = [sbA(f"btmp{i}", [128, 128], F32) for i in range(2)]
            b_sb = sbA("b_sb", [128, 8 * 256], BF16)
            sintmp = sbA("sintmp", [128, D], F32)
            mtp = sbA("mtp", [128, 128], BF16)
            mstage = sbA("mstage", [128, D], BF16)
            s_mw = mksem("s_mw")
            a_mm = mksem("a_mm")
            v_mw = mksem("v_mw")

            SC = 2.0 * math.pi / 1024.0
            wload = []
            for c in range(8):
                wload.append((wfr_in[128 * c:128 * (c + 1), :], wr_sb, D * c, "v"))
                wload.append((mtabc_c[128 * c:128 * (c + 1), :], ccol, 128 * c, "a"))
            for c in range(8):
                wload.append((wfi_in[128 * c:128 * (c + 1), :], wi_sb, D * c, "v"))
                wload.append((mtabc_s[128 * c:128 * (c + 1), :], scol, 128 * c, "a"))
            for c in range(8):
                wload.append((mtab_c[128 * c:128 * (c + 1), :], c_sb, D * c, "a"))
                wload.append((mtab_s[128 * c:128 * (c + 1), :], ss_sb, D * c, "a"))
            pl.w("sync", v_st, v_st.n)
            pl.w("sync", s_xn, s_xn.n)
            slot_guard = [None, None]
            for li, (src, dst, off, engk) in enumerate(wload):
                bb = li % 2
                wide = dst in (wr_sb, wi_sb, c_sb, ss_sb)
                buf = wtmp[bb] if wide else btmp[bb]
                if engk == "v":
                    pl.op("gpsimd", lambda e, o=dst[:, off:off + D], i=src:
                          e.dma_start(out=o, in_=i), sem=s_wld, k=16)
                    continue
                g = slot_guard[bb]
                if g is not None:
                    pl.w("sync", g[0], g[1])
                pl.op("sync", lambda e, o=buf[:, 0:(D if wide else 128)], i=src:
                      e.dma_start(out=o, in_=i), sem=s_mw, k=16)
                if True:
                    pl.w("scalar", s_mw, s_mw.n)
                    pl.op("scalar", lambda e, o=sintmp[:, 0:(D if wide else 128)],
                          i=buf[:, 0:(D if wide else 128)]:
                          e.activation(o, i, AF.Sin, scale=SC))
                    sfl(a_mm)
                    slot_guard[bb] = (a_mm, a_mm.n)
                    pl.w("vector", a_mm, a_mm.n)
                    pl.op("vector", lambda e, o=dst[:, off:off + (D if wide else 128)],
                          i=sintmp[:, 0:(D if wide else 128)]:
                          e.tensor_copy(o, i))
                    vfl(v_mw)
                    pl.w("scalar", v_mw, v_mw.n)
            W_CASTS_V = v_mw.n
            W_CASTS_A = a_mm.n

            pl.w("tensor", v_mw, W_CASTS_V)
            pl.w("tensor", s_wld, s_wld.n)
            pl.w("tensor", a_mm, W_CASTS_A)
            evac_guard = 0
            for oc in range(8):
                if oc >= 1:
                    pl.w("tensor", v_g, evac_guard)
                for wi_, wsb in ((0, wr_sb), (1, wi_sb)):
                    for ci, csb in ((0, ccol), (1, scol)):
                        reg = 512 * (2 * wi_ + ci)
                        for kc in range(8):
                            pl.op("tensor", lambda e,
                                  o=psA[:, reg:reg + 128],
                                  l=wsb[:, D * kc + 128 * oc: D * kc + 128 * (oc + 1)],
                                  r=csb[:, 128 * kc:128 * (kc + 1)],
                                  stt=(kc == 0), sp=(kc == 7):
                                  e.matmul(o, l, r, start=stt, stop=sp))
                tfl(psB[0:128, 0:128], t_m)
                pl.w("scalar", t_m)
                pl.op("scalar", lambda e, o=btmp[0][:, :], i=psA[:, 1536:1664]:
                      e.activation(o, i, AF.Copy, scale=1.0 / 1024.0))
                pl.op("scalar", lambda e, o=btmp[1][:, :], i=psA[:, 1024:1152]:
                      e.activation(o, i, AF.Copy, scale=-1.0 / 1024.0))
                sfl(a_mm)
                pl.w("vector", a_mm)
                pl.op("vector", lambda e, o=b_sb[:, 256 * oc:256 * oc + 128],
                      i=psA[:, 0:128], tt=btmp[0][:, :]:
                      e.scalar_tensor_tensor(o, i, 1.0 / 1024.0, tt, ALU.mult, ALU.add))
                pl.op("vector", lambda e, o=b_sb[:, 256 * oc + 128:256 * oc + 256],
                      i=psA[:, 512:640], tt=btmp[1][:, :]:
                      e.scalar_tensor_tensor(o, i, 1.0 / 1024.0, tt, ALU.mult, ALU.add))
                vfl(v_g)
                evac_guard = v_g.n
            B_DONE = v_g.n

            mw_guard = v_mw.n
            for oc in range(8):
                pl.w("tensor", v_g, B_DONE)
                if oc >= 1:
                    pl.w("tensor", v_mw, mw_guard)
                for cc in range(8):
                    pl.op("tensor", lambda e, o=psB[:, :],
                          l=c_sb[:, D * cc + 128 * oc: D * cc + 128 * (oc + 1)],
                          r=b_sb[:, 256 * cc:256 * cc + 128], stt=(cc == 0):
                          e.matmul(o, l, r, start=stt, stop=False))
                for cc in range(8):
                    pl.op("tensor", lambda e, o=psB[:, :],
                          l=ss_sb[:, D * cc + 128 * oc: D * cc + 128 * (oc + 1)],
                          r=b_sb[:, 256 * cc + 128:256 * cc + 256], sp=(cc == 7):
                          e.matmul(o, l, r, start=False, stop=sp))
                tfl(psA[0:128, 0:128], t_m)
                pl.w("vector", t_m)
                pl.op("vector", lambda e, o=mtp[:, :], i=psB[:, :]:
                      e.tensor_copy(o, i))
                vfl(v_g)
                pl.w("tensor", v_g)
                pl.op("tensor", lambda e, o=ptt[:, :], i=mtp[:, :]:
                      e.transpose(o, i, ident[:, :]))
                tfl(psA[0:128, 0:128], t_m)
                pl.w("vector", t_m)
                pl.op("vector", lambda e, o=mstage[:, 128 * oc:128 * (oc + 1)], i=ptt[:, :]:
                      e.tensor_copy(o, i))
                vfl(v_mw)
                mw_guard = v_mw.n
            pl.w("sync", v_mw, v_mw.n)
            pl.op("sync", lambda e: e.dma_start(out=mpc.ap()[:, :], in_=mstage[:, :]),
                  sem=s_m, k=16)
            if DEBUG and MIDDUMPS:
                pl.op("sync", lambda e: e.dma_start(out=dbg["dbg_bsb"][:, :], in_=b_sb[:, :]),
                      sem=s_m, k=16)
                pl.op("sync", lambda e: e.dma_start(out=dbg["dbg_wr"][:, :], in_=wr_sb[:, :]),
                      sem=s_m, k=16)
                pl.op("sync", lambda e: e.dma_start(out=dbg["dbg_ccol"][:, 0:1024], in_=ccol[:, :]),
                      sem=s_m, k=16)
                pl.op("sync", lambda e: e.dma_start(out=dbg["dbg_ccol"][:, 1024:2048], in_=scol[:, :]),
                      sem=s_m, k=16)
                pl.op("sync", lambda e: e.dma_start(out=dbg["dbg_csb"][:, :], in_=c_sb[:, :]),
                      sem=s_m, k=16)
            kqh = sbA("kqh", [128, 512], BF16)
            kqm = sbA("kqm", [128, 512], BF16)
            kql = sbA("kql", [128, 512], BF16)
            kqr = sbA("kqr", [128, 512], F32)
            kqr2 = sbA("kqr2", [128, 512], F32)
            pl.w("vector", s_ld, KQLD)
            for opf in () if NOKQ else (
                lambda e: e.tensor_copy(kqh[:, :], kq_sb[:, :]),
                lambda e: e.tensor_tensor(kqr[:, :], kq_sb[:, :], kqh[:, :], ALU.subtract),
                lambda e: e.tensor_copy(kqm[:, :], kqr[:, :]),
                lambda e: e.tensor_tensor(kqr2[:, :], kqr[:, :], kqm[:, :], ALU.subtract),
                lambda e: e.tensor_copy(kql[:, :], kqr2[:, :]),
            ):
                pl.op("vector", opf, sem=v_g)
                pl.w("vector", v_g)
            vfl(v_g)
            KQ_DONE = v_g.n
            pl.w("sync", v_g, KQ_DONE)
            for blk in range(8 if not NOKQ else 0):
                for i, src in enumerate((kqh, kqm, kql)):
                    pl.op("sync", lambda e,
                          o=kbq.ap()[3 * blk + i:3 * blk + i + 1, :]
                          .rearrange("o (p j) -> (o p) j", p=128),
                          s=src[:, 64 * blk:64 * (blk + 1)]:
                          e.dma_start(out=o, in_=s), sem=s_spy, k=16)
            KBQ_DMAS = s_spy.n
            pl.w("gpsimd", s_spy, KBQ_DMAS)
            if not (NOQAUG or NOKQ):
                for s in range(4):
                    for i in range(3):
                        pl.op("gpsimd", lambda e, o=qaug.ap()[70 * s + 64 + i:70 * s + 65 + i, :],
                              i2=ones_in[0:1, :]:
                              e.dma_start(out=o, in_=i2), sem=s_sp, k=16)
                    for i in range(3):
                        pl.op("gpsimd", lambda e, o=qaug.ap()[70 * s + 67 + i:70 * s + 68 + i, :],
                              i2=kbq.ap()[3 * (4 + s) + i:3 * (4 + s) + i + 1, :]:
                              e.dma_start(out=o, in_=i2), sem=s_sp, k=16)
            QAUG_ONES = s_sp.n
            for sm_ in (s_x, s_ld, s_xn, s_m, s_mw, s_wld, s_sp, s_spy):
                pl.w("gpsimd", sm_, sm_.n)
            if not NOCC:
                pl.op("gpsimd", lambda e: e.collective_compute(
                    "AllGather", ALU.bypass,
                    replica_groups=[[0, 1, 2, 3], [4, 5, 6, 7]],
                    ins=[xnc.ap().opt()], outs=[xng.ap()[0:4 * D, :].opt()]), sem=c_cc)
            CC_XN = c_cc.n
            pl.w("gpsimd", c_cc, c_cc.n)
            if not NOCC:
                pl.op("gpsimd", lambda e: e.collective_compute(
                    "AllGather", ALU.bypass, replica_groups=[list(range(8))],
                    ins=[mpc.ap().opt()], outs=[mg.ap().opt()]), sem=c_cc)
            CC_M = c_cc.n
            pl.w("gpsimd", c_cc, c_cc.n)
            if DEBUG and PHASES < 2:
                pl.op("gpsimd", lambda e: e.dma_start(out=dbg["dbg_wr2"][:, :], in_=wr_sb[:, :]),
                      sem=s_out, k=16)
                pl.w("gpsimd", s_out, s_out.n)

        PA_SM = s_m.n
        PA_SXN = s_xn.n
        PA_SSP = s_sp.n
        PA_SSPY = s_spy.n
        if PHASES < 2:
            pl.w("sync", s_sp, s_sp.n)
            pl.w("sync", s_m, s_m.n)
            pl.w("sync", c_cc, c_cc.n)
            if DEBUG:
                for src, dst in ((xnc, dbg["dbg_xnc"]), (mg, dbg["dbg_mg"]),
                                 (qaug, dbg["dbg_q"]), (mpc, dbg["dbg_mpc"])):
                    pl.op("gpsimd", lambda e, o=dst[:, :], i=src.ap()[:, :]:
                          e.dma_start(out=o, in_=i), sem=s_out, k=16)
            pl.w("gpsimd", s_out, s_out.n)
            return _emit(nc, pl)
        wp_sb = es.enter_context(nc.sbuf_tensor("wp_sb", [128, 16 * D], BF16))
        wqk_sb = es.enter_context(nc.sbuf_tensor("wqk_sb", [128, 4 * 1024], BF16))
        wv_sb = es.enter_context(nc.sbuf_tensor("wv_sb", [128, 2 * 1024], BF16))
        brow = es.enter_context(nc.sbuf_tensor("brow", [1, D], BF16))

        def wload2(src_rows, dst, dst_off):
            pl.op("gpsimd", lambda e,
                  o=dst[:, dst_off:dst_off + D].rearrange("p (c m) -> p c m", m=128),
                  i=src_rows.rearrange("(c p) m -> p c m", p=128):
                  e.dma_start(out=o, in_=i), sem=s_wld, k=16)

        for s in range(4):
            wload2(wqk_in[D * s:D * (s + 1), :], wqk_sb, 1024 * s)
        for p in range(2):
            wload2(wv_in[D * p:D * (p + 1), :], wv_sb, 1024 * p)
        for c in range(16):
            pl.op("gpsimd", lambda e, o=wp_sb[:, D * c:D * (c + 1)],
                  i=wp_in[128 * c:128 * (c + 1), :]:
                  e.dma_start(out=o, in_=i), sem=s_wld, k=16)
        pl.op("gpsimd", lambda e: e.dma_start(out=brow[:, :], in_=bias_in[:, :]),
              sem=s_wld, k=16)
        WEIGHTS_OK_W = s_wld.n
        esBC = contextlib.ExitStack()
        kaug = [esBC.enter_context(nc.sbuf_tensor(f"kaug{s}", [70, T], BF16))
                for s in range(4)]
        vaug = [esBC.enter_context(nc.sbuf_tensor(f"vaug{s}", [128, 64 * 65], BF16))
                for s in range(4)]

        esB = contextlib.ExitStack()
        with esB:
            def sbB(name, shape, dt):
                return esB.enter_context(nc.sbuf_tensor(name, shape, dt))

            xs = [sbB(f"xs{i}", [128, 8 * 512], BF16) for i in range(2)]
            m_sb = sbB("m_sb", [128, 8 * D], BF16)
            qstage = [sbB(f"qstage{i}", [64, 512], BF16) for i in range(2)]
            gtile = sbB("gtile", [128, D], BF16)
            psQK = [esB.enter_context(nc.psum_tensor(f"psQK{i}", [128, 512], F32))
                    for i in range(2)]
            psV = [esB.enter_context(nc.psum_tensor(f"psV{i}", [128, 128], F32))
                   for i in range(2)]
            psDum = esB.enter_context(nc.psum_tensor("psDum", [128, 128], F32))

            pl.w("gpsimd", c_cc, CC_M)
            pl.w("gpsimd", s_m, PA_SM)
            pl.w("gpsimd", s_xn, PA_SXN)
            pl.w("gpsimd", s_sp, PA_SSP)
            pl.w("gpsimd", s_spy, PA_SSPY)
            for s in range(4):
                pl.op("gpsimd", lambda e, o=kaug[s][64:70, :]:
                      e.memset(o, 1.0), sem=g_gp)
            for s in range(4):
                for i in range(3):
                    pl.op("gpsimd", lambda e, o=kaug[s][64 + i:65 + i, :],
                          i2=kbq.ap()[3 * s + i:3 * s + i + 1, :]:
                          e.dma_start(out=o, in_=i2), sem=s_sp, k=16)
            KB_ROWS = s_sp.n

            pl.w("sync", c_cc, CC_M)
            pl.w("sync", s_m, PA_SM)
            pl.w("sync", s_xn, PA_SXN)
            pl.w("sync", s_sp, PA_SSP)
            pl.w("sync", s_spy, PA_SSPY)

            pl.w("sync", c_cc, CC_M)
            for c in range(8):
                pl.op("sync", lambda e, o=m_sb[:, D * c:D * (c + 1)],
                      i=mg.ap()[128 * c:128 * (c + 1), :]:
                      e.dma_start(out=o, in_=i), sem=s_m, k=16)
            MSB_OK = s_m.n

            for s in range(4):
                pl.op("gpsimd", lambda e, o=vaug[s][:, :]: e.memset(o, 1.0), sem=g_gp)
            VONES = g_gp.n

            pl.w("sync", c_cc, CC_XN)
            if DEBUG and MIDDUMPS:
                pl.w("sync", v_g, WEIGHTS_OK)
                pl.op("sync", lambda e: e.dma_start(out=dbg["dbg_wqk"][:, :], in_=wqk_sb[:, :]),
                      sem=s_ld, k=16)
                pl.op("sync", lambda e: e.dma_start(out=dbg["dbg_wv"][:, :], in_=wv_sb[:, :]),
                      sem=s_ld, k=16)
                pl.w("sync", s_ld, s_ld.n)
            pl.w("tensor", s_wld, WEIGHTS_OK_W)
            pl.w("tensor", g_gp, VONES)
            pl.w("tensor", s_sp, KB_ROWS)

            psqk_guard = [0, 0]
            psv_guard = [0, 0]
            qstage_guard = [0, 0]
            blk_ev = {}
            for blk in range(16):
                bb = blk % 2
                rb, qt = blk // 4, blk % 4
                if blk >= 2:
                    pl.w("sync", v_ev, blk_ev[blk - 2])
                for c in range(8):
                    pl.op("sync", lambda e, o=xs[bb][:, 512 * c:512 * (c + 1)],
                          i=xng.ap()[1024 * rb + 128 * c:1024 * rb + 128 * (c + 1),
                                     512 * qt:512 * (qt + 1)]:
                          e.dma_start(out=o, in_=i), sem=s_xs, k=16)
                pl.w("tensor", s_xs, s_xs.n)
                for s in range(4):
                    pb = s % 2
                    pl.w("tensor", v_ev, psqk_guard[pb])
                    for c in range(8):
                        pl.op("tensor", lambda e, o=psQK[pb][:, :],
                              l=wqk_sb[:, 1024 * s + 128 * c:1024 * s + 128 * (c + 1)],
                              r=xs[bb][:, 512 * c:512 * (c + 1)],
                              stt=(c == 0), sp=(c == 7):
                              e.matmul(o, l, r, start=stt, stop=sp))
                    tfl(psDum[:, :], t_m)
                    qkd = t_m.n
                    pl.w("vector", t_m, qkd)
                    pl.w("vector", s_q, qstage_guard[pb])
                    pl.op("vector", lambda e, o=qstage[pb][:, :], i=psQK[pb][0:64, :]:
                          e.tensor_copy(o, i))
                    pl.op("vector", lambda e,
                          o=kaug[s][0:64, 512 * blk:512 * (blk + 1)],
                          i=psQK[pb][64:128, :]:
                          e.tensor_copy(o, i))
                    vfl(v_ev)
                    psqk_guard[pb] = v_ev.n
                    pl.w("gpsimd", v_ev, v_ev.n)
                    pl.op("gpsimd", lambda e,
                          o=qaug.ap()[70 * s:70 * s + 64, 512 * blk:512 * (blk + 1)],
                          i=qstage[pb][:, :]:
                          e.dma_start(out=o, in_=i), sem=s_q, k=16)
                    qstage_guard[pb] = s_q.n
                for p in range(2):
                    for sub in range(4):
                        vb = (p * 4 + sub) % 2
                        jj = 4 * blk + sub
                        pl.w("tensor", v_ev, psv_guard[vb])
                        for c in range(8):
                            pl.op("tensor", lambda e, o=psV[vb][:, :],
                                  l=xs[bb][:, 512 * c + 128 * sub:512 * c + 128 * (sub + 1)],
                                  r=wv_sb[:, 1024 * p + 128 * c:1024 * p + 128 * (c + 1)],
                                  stt=(c == 0), sp=(c == 7):
                                  e.matmul(o, l, r, start=stt, stop=sp))
                        tfl(psDum[:, :], t_m)
                        pl.w("vector", t_m, t_m.n)
                        pl.op("vector", lambda e,
                              o=vaug[2 * p][:, 65 * jj:65 * jj + 64],
                              i=psV[vb][:, 0:64]:
                              e.tensor_copy(o, i))
                        pl.op("vector", lambda e,
                              o=vaug[2 * p + 1][:, 65 * jj:65 * jj + 64],
                              i=psV[vb][:, 64:128]:
                              e.tensor_copy(o, i))
                        vfl(v_ev)
                        psv_guard[vb] = v_ev.n
                blk_ev[blk] = v_ev.n
            vfl(v_ev)
            P2_T = t_m.n
            P2_EV = v_ev.n
            P2_Q = s_q.n

            pl.w("tensor", s_m, MSB_OK)
            pl.w("sync", v_ev, P2_EV)
            ffn_mm = {}
            ffn_sp_prev = s_sp.n
            for fb in range(4):
                bb = fb % 2
                if fb >= 2:
                    pl.w("sync", t_m, ffn_mm[fb - 2])
                for c in range(8):
                    pl.op("sync", lambda e, o=xs[bb][:, 512 * c:512 * (c + 1)],
                          i=xnc.ap()[128 * c:128 * (c + 1), 512 * fb:512 * (fb + 1)]:
                          e.dma_start(out=o, in_=i), sem=s_xs, k=16)
                pl.w("tensor", s_xs, s_xs.n)
                for sub in range(4):
                    tt_ = 4 * fb + sub
                    pl.w("tensor", a_m, a_m.n)
                    for half in range(2):
                        for c in range(8):
                            pl.op("tensor", lambda e, o=psQK[half][:, :],
                                  l=xs[bb][:, 512 * c + 128 * sub:512 * c + 128 * (sub + 1)],
                                  r=m_sb[:, D * c + 512 * half:D * c + 512 * (half + 1)],
                                  stt=(c == 0):
                                  e.matmul(o, l, r, start=stt, stop=False))
                        pl.op("tensor", lambda e, o=psQK[half][:, :],
                              r=brow[0:1, 512 * half:512 * (half + 1)]:
                              e.matmul(o, ones1[0:1, :], r, start=False, stop=True))
                    tfl(psDum[:, :], t_m)
                    ffn_mm[fb] = t_m.n
                    pl.w("scalar", t_m, t_m.n)
                    pl.w("scalar", s_sp, ffn_sp_prev)
                    for half in range(2):
                        pl.op("scalar", lambda e, o=gtile[:, 512 * half:512 * (half + 1)],
                              i=psQK[half][:, :]:
                              e.activation(o, i, AF.Silu))
                    sfl(a_m)
                    pl.w("gpsimd", a_m, a_m.n)
                    pl.op("gpsimd", lambda e, o=gdr.ap()[128 * tt_:128 * (tt_ + 1), :],
                          i=gtile[:, :]:
                          e.dma_start(out=o, in_=i), sem=s_sp, k=16)
                    ffn_sp_prev = s_sp.n
            FFN_SP = s_sp.n

        if PHASES < 3:
            pl.w("sync", s_sp, s_sp.n)
            pl.w("sync", s_q, s_q.n)
            if DEBUG:
                for src, dst in ((xnc, dbg["dbg_xnc"]), (mg, dbg["dbg_mg"]),
                                 (qaug, dbg["dbg_q"]), (gdr, dbg["dbg_g"]),
                                 (mpc, dbg["dbg_mpc"])):
                    pl.op("gpsimd", lambda e, o=dst[:, :], i=src.ap()[:, :]:
                          e.dma_start(out=o, in_=i), sem=s_out, k=16)
            pl.w("gpsimd", s_out, s_out.n)
            return _emit(nc, pl)
        esC = contextlib.ExitStack()
        with esC:
            def sbC(name, shape, dt):
                return esC.enter_context(nc.sbuf_tensor(name, shape, dt))

            qsb = [sbC(f"qsb{i}", [70, 512], BF16) for i in range(2)]
            p3sb = [sbC(f"p3sb{i}", [128, 1536], BF16) for i in range(2)]
            ystage = sbC("ystage", [64, 512], BF16)
            dstage = sbC("dstage", [1, 512], F32)
            s3 = [esC.enter_context(nc.psum_tensor(f"s3_{i}", [128, 1536], F32))
                  for i in range(2)]
            yps = [esC.enter_context(nc.psum_tensor(f"yps{i}", [128, 512], F32))
                   for i in range(2)]

            pl.w("tensor", s_q, P2_Q)
            pl.w("tensor", v_ev, P2_EV)
            pl.w("tensor", s_sp, QAUG_ONES)
            pl.w("tensor", s_sp, KB_ROWS)
            pl.w("gpsimd", s_q, P2_Q)
            pl.w("gpsimd", s_sp, QAUG_ONES)

            if DEBUG and MIDDUMPS:
                pl.op("gpsimd", lambda e: e.dma_start(out=dbg["dbg_ka"][:, :], in_=kaug[0][:, :]),
                      sem=s_sp, k=16)
                pl.op("gpsimd", lambda e: e.dma_start(out=dbg["dbg_ka3"][:, :], in_=kaug[3][:, :]),
                      sem=s_sp, k=16)
                pl.op("gpsimd", lambda e: e.dma_start(out=dbg["dbg_kbq"][:, :], in_=kbq.ap()[:, :]),
                      sem=s_sp, k=16)
                pl.w("gpsimd", s_sp, s_sp.n)
            pairs = [(s, qb) for s in range(4) for qb in range(NQB)]
            exp_hist = []
            pv_hist = []
            qload = {}
            qsb_guard = [0, 0]
            yev_guard = [0, 0]
            sp_guard = 0
            tri_idx = 0

            def emit_qload(idx):
                s, qb = pairs[idx]
                qbb = qb % 2
                pl.w("gpsimd", t_qk, qsb_guard[qbb])
                pl.op("gpsimd", lambda e, o=qsb[qbb][:, :],
                      i=qaug.ap()[70 * s:70 * (s + 1), 512 * qb:512 * (qb + 1)]:
                      e.dma_start(out=o, in_=i), sem=s_q3, k=16)
                qload[idx] = s_q3.n

            emit_qload(0)
            for idx, (s, qb) in enumerate(pairs):
                chunks = sched[s][qb]
                qbb = qb % 2
                ntri = (len(chunks) + 2) // 3
                first = True
                pend = None

                def emit_pv(p, s=s, qbb=qbb):
                    tri_, tb_, stt_, last_, _ = p
                    tw_ = len(tri_)
                    for i, j in enumerate(tri_):
                        pl.op("tensor", lambda e, o=yps[qbb][0:65, :],
                              l=vaug[s][:, 65 * j:65 * (j + 1)],
                              r=p3sb[tb_][:, 512 * i:512 * (i + 1)],
                              stt=(stt_ and i == 0),
                              sp=(last_ and i == tw_ - 1):
                              e.matmul(o, l, r, start=stt, stop=sp),
                              sem=(t_pv if i == tw_ - 1 else None))
                    pv_hist.append(t_pv.n)

                for ti in range(ntri):
                    tri = chunks[3 * ti:3 * ti + 3]
                    tb = tri_idx % 2
                    tw = len(tri)
                    if tri_idx >= 2:
                        pl.w("tensor", a_e, exp_hist[tri_idx - 2])
                    pl.w("tensor", s_q3, qload[idx])
                    if first:
                        pl.w("tensor", v_ev, yev_guard[qbb])
                    for i, j in enumerate(tri):
                        pl.op("tensor", lambda e, o=s3[tb][:, 512 * i:512 * (i + 1)],
                              l=kaug[s][:, 128 * j:128 * (j + 1)],
                              r=qsb[qbb][:, :]:
                              e.matmul(o, l, r, start=True, stop=True),
                              sem=(t_qk if i == tw - 1 else None))
                    QKD = t_qk.n
                    has_diag = any(j >= 4 * qb for j in tri)
                    if has_diag:
                        pl.w("vector", t_qk, QKD)
                        for i, j in enumerate(tri):
                            if j >= 4 * qb:
                                v = j - 4 * qb
                                pl.op("vector", lambda e,
                                      o=s3[tb][:, 512 * i:512 * (i + 1)],
                                      m=maskt[:, 512 * v:512 * (v + 1)]:
                                      e.tensor_tensor(o, o, m, ALU.add))
                        vfl(v_g)
                        pl.w("scalar", v_g, v_g.n)
                    pl.w("scalar", t_qk, QKD)
                    if tri_idx >= 2:
                        pl.w("scalar", t_pv, pv_hist[tri_idx - 2])
                    pl.op("scalar", lambda e, o=p3sb[tb][:, 0:512 * tw],
                          i=s3[tb][:, 0:512 * tw]:
                          e.activation(o, i, AF.Exp, bias=shv[:, 0:1], scale=0.125))
                    sfl(a_e)
                    exp_hist.append(a_e.n)
                    if DEBUG and idx == 49 and ti == 0:
                        pl.w("gpsimd", a_e, a_e.n)
                        pl.op("gpsimd", lambda e, src=p3sb[tb]:
                              e.dma_start(out=dbg["dbg_p3"][:, :], in_=src[:, :]),
                              sem=s_sp, k=16)
                        pl.op("gpsimd", lambda e, src=qsb[qbb]:
                              e.dma_start(out=dbg["dbg_q0"][:, :], in_=src[:, :]),
                              sem=s_sp, k=16)
                        DBGP3 = s_sp.n
                        pl.w("scalar", s_sp, DBGP3)
                    if ti == 0 and idx + 1 < len(pairs):
                        emit_qload(idx + 1)
                    if pend is not None:
                        pl.w("tensor", a_e, pend[4])
                        emit_pv(pend)
                    pend = (tri, tb, first, ti == ntri - 1, exp_hist[-1])
                    first = False
                    tri_idx += 1
                pl.w("tensor", a_e, pend[4])
                emit_pv(pend)
                tb_last = (tri_idx - 1) % 2
                tfl(s3[tb_last][0:128, 0:128], t_pv)
                qsb_guard[qbb] = t_qk.n
                PVD = t_pv.n
                pl.w("vector", t_pv, PVD)
                pl.w("vector", s_sp, sp_guard)
                pl.op("vector", lambda e, o=ystage[:, :], i=yps[qbb][0:64, :]:
                      e.tensor_copy(o, i))
                pl.op("vector", lambda e, o=dstage[:, :], i=yps[qbb][64:65, :]:
                      e.tensor_copy(o, i))
                vfl(v_ev)
                yev_guard[qbb] = v_ev.n
                pl.w("gpsimd", v_ev, v_ev.n)
                co = 512 * (qb % 4)
                for dst in (qb // 4, qb // 4 + 4):
                    ro = 256 * dst + 64 * s
                    pl.op("gpsimd", lambda e, o=y_in.ap()[ro:ro + 64, co:co + 512],
                          i=ystage[:, :]:
                          e.dma_start(out=o, in_=i), sem=s_sp, k=16)
                    dr = 4 * dst + s
                    pl.op("gpsimd", lambda e, o=d_in.ap()[dr:dr + 1, co:co + 512],
                          i=dstage[:, :]:
                          e.dma_start(out=o, in_=i), sem=s_sp, k=16)
                sp_guard = s_sp.n
            P3_SP = s_sp.n
            for sm_ in (s_x, s_ld, s_xn, s_m, s_sp, s_spy, s_xs, s_q, s_q3, s_out):
                pl.w("gpsimd", sm_, sm_.n)
            with allow_ici_all_to_all():
                pl.op("gpsimd", lambda e: e.collective_compute(
                    "AllToAll", ALU.bypass,
                    replica_groups=[list(range(8))],
                    ins=[y_in.ap().opt()], outs=[y_out.ap().opt()]), sem=c_cc)
                pl.w("gpsimd", c_cc, c_cc.n)
                pl.op("gpsimd", lambda e: e.collective_compute(
                    "AllToAll", ALU.bypass,
                    replica_groups=[list(range(8))],
                    ins=[d_in.ap().opt()], outs=[d_out.ap().opt()]), sem=c_cc)
            CC_Y = c_cc.n
            pl.w("gpsimd", c_cc, CC_Y)
        esBC.close()
        if PHASES < 4:
            pl.w("sync", s_sp, s_sp.n)
            pl.w("sync", c_cc, c_cc.n)
            if DEBUG:
                for src, dst in ((xnc, dbg["dbg_xnc"]), (mg, dbg["dbg_mg"]),
                                 (qaug, dbg["dbg_q"]), (gdr, dbg["dbg_g"]),
                                 (y_out, dbg["dbg_y"]), (d_out, dbg["dbg_d"]),
                                 (mpc, dbg["dbg_mpc"]), (y_in, dbg["dbg_yin"]),
                                 (d_in, dbg["dbg_din"])):
                    pl.op("gpsimd", lambda e, o=dst[:, :], i=src.ap()[:, :]:
                          e.dma_start(out=o, in_=i), sem=s_out, k=16)
            pl.w("gpsimd", s_out, s_out.n)
            return _emit(nc, pl)

        esD = contextlib.ExitStack()
        with esD:
            def sbD(name, shape, dt):
                return esD.enter_context(nc.sbuf_tensor(name, shape, dt))

            den_sb = sbD("den_sb", [32, TOK], F32)
            rden = sbD("rden", [32, TOK], F32)
            sel_sb = sbD("sel_sb", [32, 2048], F32)
            ysb = sbD("ysb", [128, TOK], BF16)
            ynorm = sbD("ynorm", [128, 16 * TOK], BF16)
            gx = [sbD(f"gx{i}", [128, D], BF16) for i in range(2)]
            xres = [sbD(f"xres{i}", [128, D], F32) for i in range(2)]
            tsg = sbD("tsg", [128, D], F32)
            osb = [sbD(f"osb{i}", [128, D], F32) for i in range(2)]
            bca = esD.enter_context(nc.psum_tensor("bca", [128, 512], F32))
            psDum4 = esD.enter_context(nc.psum_tensor("psDum4", [128, 128], F32))
            psS = [esD.enter_context(nc.psum_tensor(f"psS{i}", [128, 512], F32))
                   for i in range(4)]

            pl.w("sync", c_cc, CC_Y)
            pl.op("sync", lambda e: e.dma_start(out=sel_sb[:, :], in_=sel_in[:, :]),
                  sem=s_ld, k=16)
            pl.op("sync", lambda e: e.dma_start(out=den_sb[:, :], in_=d_out.ap()[:, :]),
                  sem=s_ld, k=16)
            pl.w("vector", s_ld, s_ld.n)
            pl.op("vector", lambda e: e.reciprocal(rden[:, :], den_sb[:, :]))
            vfl(v_g)
            RDEN = v_g.n

            for c in range(16):
                pl.op("sync", lambda e, o=ysb[:, :],
                      i=y_out.ap()[128 * c:128 * (c + 1), :]:
                      e.dma_start(out=o, in_=i), sem=s_ld, k=16)
                YL = s_ld.n
                for qtr in range(4):
                    pl.w("tensor", s_ld, YL)
                    pl.w("tensor", v_g, RDEN)
                    pl.w("tensor", v_g, v_g.n)
                    pl.op("tensor", lambda e, o=bca[:, :],
                          l=sel_sb[:, 128 * c:128 * (c + 1)],
                          r=rden[:, 512 * qtr:512 * (qtr + 1)]:
                          e.matmul(o, l, r, start=True, stop=True))
                    tfl(psDum4[:, :], t_m)
                    pl.w("vector", t_m, t_m.n)
                    pl.op("vector", lambda e,
                          o=ynorm[:, TOK * c + 512 * qtr:TOK * c + 512 * (qtr + 1)],
                          a=ysb[:, 512 * qtr:512 * (qtr + 1)],
                          b2=bca[:, :]:
                          e.tensor_tensor(o, a, b2, ALU.mult), sem=v_g)
                vfl(v_g)
                pl.w("sync", v_g, v_g.n)
            YN = v_g.n

            pl.w("tensor", v_g, YN)
            for t in range(NT):
                tb = t % 2
                if t >= 2:
                    pl.w("tensor", v_g, max(RDEN, v_g.n))
                for half in range(2):
                    pk = 2 * tb + half
                    for c in range(16):
                        pl.op("tensor", lambda e, o=psS[pk][:, :],
                              l=ynorm[:, TOK * c + 128 * t:TOK * c + 128 * (t + 1)],
                              r=wp_sb[:, D * c + 512 * half:D * c + 512 * (half + 1)],
                              stt=(c == 0), sp=(c == 15):
                              e.matmul(o, l, r, start=stt, stop=sp))
                tfl(psDum4[:, :], t_m)
                SD = t_m.n
                pl.op("sync", lambda e, o=gx[tb][:, :],
                      i=gdr.ap()[128 * t:128 * (t + 1), :]:
                      e.dma_start(out=o, in_=i), sem=s_ld, k=16)
                pl.op("sync", lambda e, o=xres[tb][:, :],
                      i=x_own[128 * t:128 * (t + 1), :]:
                      e.dma_start(out=o, in_=i), sem=s_ld, k=16)
                GXL = s_ld.n
                pl.w("vector", t_m, SD)
                pl.w("vector", s_ld, GXL)
                pl.w("vector", s_sp, FFN_SP)
                if t >= 2:
                    pl.w("vector", s_out, 16 * (t - 1))
                for half in range(2):
                    pk = 2 * tb + half
                    pl.op("vector", lambda e, o=tsg[:, 512 * half:512 * (half + 1)],
                          a=psS[pk][:, :], b2=gx[tb][:, 512 * half:512 * (half + 1)]:
                          e.tensor_tensor(o, a, b2, ALU.mult), sem=v_g)
                pl.w("vector", v_g, v_g.n)
                pl.op("vector", lambda e, o=osb[tb][:, :], a=tsg[:, :], b2=xres[tb][:, :]:
                      e.tensor_tensor(o, a, b2, ALU.add))
                vfl(v_g)
                pl.w("sync", v_g, v_g.n)
                pl.op("sync", lambda e, o=out_own[128 * t:128 * (t + 1), :], i=osb[tb][:, :]:
                      e.dma_start(out=o, in_=i), sem=s_out, k=16)

            if DEBUG:
                pl.w("gpsimd", s_out, s_out.n)
                pl.w("gpsimd", c_cc, CC_Y)
                for src, dst in ((xnc, dbg["dbg_xnc"]), (mg, dbg["dbg_mg"]),
                                 (qaug, dbg["dbg_q"]), (gdr, dbg["dbg_g"]),
                                 (y_out, dbg["dbg_y"]), (d_out, dbg["dbg_d"]),
                                 (mpc, dbg["dbg_mpc"]), (y_in, dbg["dbg_yin"]),
                                 (d_in, dbg["dbg_din"])):
                    pl.op("gpsimd", lambda e, o=dst[:, :], i=src.ap()[:, :]:
                          e.dma_start(out=o, in_=i), sem=s_out, k=16)

            pl.w("sync", s_out, s_out.n)

        return _emit(nc, pl)


def _emit(nc, pl):
    with allow_ici_all_to_all(), nc.Block() as block:
        @block.sync
        def _(e):
            for f in pl.q["sync"]:
                f(e)

        @block.vector
        def _(e):
            for f in pl.q["vector"]:
                f(e)

        @block.scalar
        def _(e):
            for f in pl.q["scalar"]:
                f(e)

        @block.tensor
        def _(e):
            for f in pl.q["tensor"]:
                f(e)

        @block.gpsimd
        def _(e):
            for f in pl.q["gpsimd"]:
                f(e)

    return nc



def host_inputs(L0_attn_x, L0_attn_w_attn, L0_attn_w_proj,
                L0_ffn_w_freq_real, L0_ffn_w_freq_imag, L0_ffn_bias):
    x = np.asarray(L0_attn_x, dtype=np.float32)
    wa = np.ascontiguousarray(np.asarray(L0_attn_w_attn, dtype=np.float32))
    wp = np.asarray(L0_attn_w_proj, dtype=np.float32)
    wfr = np.ascontiguousarray(np.asarray(L0_ffn_w_freq_real, dtype=np.float32))
    wfi = np.ascontiguousarray(np.asarray(L0_ffn_w_freq_imag, dtype=np.float32))
    fb = np.ascontiguousarray(np.asarray(L0_ffn_bias, dtype=np.float32).reshape(1, D))

    slots, Ls, sl = head_slots()
    perm = np.zeros(D, dtype=np.int64)
    for g in range(4):
        for s in range(4):
            h = slots[s][g]
            perm[256 * g + 64 * s: 256 * g + 64 * (s + 1)] = np.arange(64 * h, 64 * (h + 1))
    wpt1 = wp.T[perm, :]
    wpt = np.ascontiguousarray(np.concatenate([wpt1, wpt1], axis=0))

    jk = (np.arange(D, dtype=np.int64)[:, None] * np.arange(D, dtype=np.int64)[None, :])
    msin = ((((jk) % D) + 512) % D - 512).astype(np.float32)
    mcos = ((((jk + 256) % D) + 512) % D - 512).astype(np.float32)

    karr = (np.arange(128, dtype=np.float64)[:, None] * 64 +
            np.arange(64, dtype=np.float64)[None, :])


    ones32 = np.ones((1, T), dtype=np.float32)

    in_maps = []
    for core in range(NC):
        b, g = core // 4, core % 4
        heads = [slots[s][g] for s in range(4)]
        wqk = np.concatenate([
            np.ascontiguousarray(
                np.concatenate([wa[64 * h:64 * (h + 1), :],
                                wa[D + 64 * h:D + 64 * (h + 1), :]], axis=0).T)
            for h in heads], axis=0)
        wv = np.concatenate([
            np.ascontiguousarray(
                np.concatenate(
                    [wa[2 * D + 64 * heads[2 * p]:2 * D + 64 * (heads[2 * p] + 1), :],
                     wa[2 * D + 64 * heads[2 * p + 1]:2 * D + 64 * (heads[2 * p + 1] + 1), :]],
                    axis=0).T)
            for p in range(2)], axis=0)
        selt = np.zeros((32, 2048), dtype=np.float32)
        gl = np.arange(2048)
        hb = gl // 64
        ownrow = (hb < 16) if b == 0 else (hb >= 16)
        selt[hb[ownrow], gl[ownrow]] = 1.0
        kq = np.zeros((128, 512), dtype=np.float32)
        for s in range(4):
            ss = float(sl[heads[s]])
            kq[:, 64 * s:64 * (s + 1)] = (8.0 * ss * karr).astype(np.float32)
            kq[:, 256 + 64 * s:256 + 64 * (s + 1)] = (-8.0 * ss * karr).astype(np.float32)
        in_maps.append({
            "x_own": np.ascontiguousarray(x[b, 2048 * g:2048 * (g + 1), :]),
            "wqk": wqk, "wv": wv, "wpt": wpt,
            "wfr": wfr, "wfi": wfi, "fbias": fb,
            "mtab_s": msin, "mtab_c": mcos,
            "mtabc_s": np.ascontiguousarray(msin[:, 128 * core:128 * (core + 1)]),
            "mtabc_c": np.ascontiguousarray(mcos[:, 128 * core:128 * (core + 1)]),
            "kq128": kq, "seltab": selt, "ones32": ones32,
        })
    return in_maps


_NC_CACHE = [None]


def kernel(L0_attn_x, L0_attn_w_attn, L0_attn_w_proj,
           L0_ffn_w_freq_real, L0_ffn_w_freq_imag, L0_ffn_bias):
    in_maps = host_inputs(L0_attn_x, L0_attn_w_attn, L0_attn_w_proj,
                          L0_ffn_w_freq_real, L0_ffn_w_freq_imag, L0_ffn_bias)
    if _NC_CACHE[0] is None:
        _NC_CACHE[0] = build_nc()
    res = run_bass_kernel_spmd(_NC_CACHE[0], in_maps, core_ids=list(range(NC)))
    out = np.zeros((B, T, D), dtype=np.float32)
    for core in range(NC):
        b, g = core // 4, core % 4
        out[b, 2048 * g:2048 * (g + 1), :] = res.results[core]["out_own"]
    if DEBUG:
        kernel.last = res
    return out


# version 08
# speedup vs baseline: 1.0980x; 1.0980x over previous
import contextlib
import math
import os

import numpy as np
import ml_dtypes

import concourse.bass as bass
import concourse.mybir as mybir
from concourse.bass_utils import run_bass_kernel_spmd
from concourse.aot_env import allow_ici_all_to_all

F32 = mybir.dt.float32
BF16 = mybir.dt.bfloat16
AF = mybir.ActivationFunctionType
ALU = mybir.AluOpType
AX = mybir.AxisListType
BF = ml_dtypes.bfloat16

ENGINES = ("sync", "vector", "scalar", "tensor", "gpsimd")

B, T, D, H, HD = 2, 8192, 1024, 16, 64
NC = 8
TOK = 2048
NT = TOK // 128
NBLK = 16
NQB = 16
QBW = 512
CKEEP = 30.0
EXPSHIFT = -30.0
DEBUG = bool(int(os.environ.get("MK_DEBUG", "0")))
MIDDUMPS = bool(int(os.environ.get("MK_MIDDUMPS", "0")))
PHASES = int(os.environ.get("MK_PHASES", "4"))
NOCC = bool(int(os.environ.get("MK_NOCC", "0")))
NOQAUG = bool(int(os.environ.get("MK_NOQAUG", "0")))
NOLN = bool(int(os.environ.get("MK_NOLN", "0")))
NOKQ = bool(int(os.environ.get("MK_NOKQ", "0")))


def alibi_slopes(n_heads=H):
    start = 2.0 ** (-2.0 ** (-(math.log2(n_heads) - 3)))
    return np.array([start * start ** i for i in range(n_heads)], dtype=np.float32)


def head_slots():
    sl = alibi_slopes()
    ranks = list(np.argsort(sl))
    slots = [ranks[4 * s: 4 * s + 4] for s in range(4)]
    Ls = [int(math.ceil(CKEEP / float(sl[slots[s][0]]))) for s in range(4)]
    return slots, Ls, sl


def chunk_sched(L):
    out = []
    for qb in range(NQB):
        qb0 = QBW * qb
        jmax = (qb0 + QBW - 1) // 128
        jmin = max(0, int(math.ceil((qb0 - L - 127) / 128.0)))
        out.append(list(range(jmin, jmax + 1)))
    return out


class Sem:
    __slots__ = ("h", "n", "name")

    def __init__(self, h, name):
        self.h = h
        self.n = 0
        self.name = name


class Plan:
    def __init__(self):
        self.q = {e: [] for e in ENGINES}

    def em(self, eng, fn):
        self.q[eng].append(fn)

    def w(self, eng, sem, val=None):
        v = sem.n if val is None else val
        if v > 0:
            self.em(eng, lambda e, s=sem.h, vv=v: e.wait_ge(s, vv))

    def op(self, eng, fn, sem=None, k=1):
        if sem is not None:
            sem.n += k
            self.em(eng, lambda e, f=fn, s=sem.h, kk=k: f(e).then_inc(s, kk))
        else:
            self.em(eng, fn)


def build_nc():
    slots, Ls, sl = head_slots()
    sched = [chunk_sched(Ls[s]) for s in range(4)]

    nc = bass.Bass("TRN2", target_bir_lowering=False, debug=False)

    def ein(name, shape, dt=F32):
        return nc.dram_tensor(name, shape, dt, kind="ExternalInput").ap()

    x_own = ein("x_own", [TOK, D])
    wqk_in = ein("wqk", [4 * D, 128])
    wv_in = ein("wv", [2 * D, 128])
    wp_in = ein("wpt", [2 * D, D])
    wfr_in = ein("wfr", [D, D])
    wfi_in = ein("wfi", [D, D])
    bias_in = ein("fbias", [1, D])
    mtab_s = ein("mtab_s", [D, D])
    mtab_c = ein("mtab_c", [D, D])
    mtabc_s = ein("mtabc_s", [D, 128])
    mtabc_c = ein("mtabc_c", [D, 128])
    kq_in = ein("kq128", [128, 512])
    sel_in = ein("seltab", [32, 2048])
    ones_in = ein("ones32", [1, T])

    out_own = nc.dram_tensor("out_own", [TOK, D], F32, kind="ExternalOutput").ap()

    xnc = nc.dram_tensor("xnc", [D, TOK], BF16)
    xng = nc.dram_tensor("xng", [8 * D, TOK], BF16)
    mpc = nc.dram_tensor("mpc", [128, D], BF16)
    mg = nc.dram_tensor("mg", [D, D], BF16)
    kbq = nc.dram_tensor("kbq", [24, T], BF16)
    qaug = nc.dram_tensor("qaug", [4 * 70, T], BF16)
    gdr = nc.dram_tensor("gdr", [TOK, D], BF16)
    y_in = nc.dram_tensor("y_in", [2048, TOK], BF16)
    y_out = nc.dram_tensor("y_out", [2048, TOK], BF16)
    d_in = nc.dram_tensor("d_in", [32, TOK], F32)
    d_out = nc.dram_tensor("d_out", [32, TOK], F32)

    dbg = {}
    if DEBUG:
        for nm, src in (("dbg_xnc", [D, TOK]), ("dbg_mg", [D, D]),
                        ("dbg_q", [4 * 70, T]), ("dbg_g", [TOK, D]),
                        ("dbg_y", [2048, TOK])):
            dbg[nm] = nc.dram_tensor(nm, src, BF16, kind="ExternalOutput").ap()
        dbg["dbg_d"] = nc.dram_tensor("dbg_d", [32, TOK], F32, kind="ExternalOutput").ap()
        dbg["dbg_mpc"] = nc.dram_tensor("dbg_mpc", [128, D], BF16, kind="ExternalOutput").ap()
        dbg["dbg_ka"] = nc.dram_tensor("dbg_ka", [70, T], BF16, kind="ExternalOutput").ap()
        dbg["dbg_yin"] = nc.dram_tensor("dbg_yin", [2048, TOK], BF16, kind="ExternalOutput").ap()
        dbg["dbg_bsb"] = nc.dram_tensor("dbg_bsb", [128, 2048], BF16, kind="ExternalOutput").ap()
        dbg["dbg_wr"] = nc.dram_tensor("dbg_wr", [128, 8 * D], BF16, kind="ExternalOutput").ap()
        dbg["dbg_ccol"] = nc.dram_tensor("dbg_ccol", [128, 2 * 1024], BF16, kind="ExternalOutput").ap()
        dbg["dbg_csb"] = nc.dram_tensor("dbg_csb", [128, 8 * D], BF16, kind="ExternalOutput").ap()
        dbg["dbg_din"] = nc.dram_tensor("dbg_din", [32, TOK], F32, kind="ExternalOutput").ap()
        dbg["dbg_wr2"] = nc.dram_tensor("dbg_wr2", [128, 8 * D], BF16, kind="ExternalOutput").ap()
        dbg["dbg_ka3"] = nc.dram_tensor("dbg_ka3", [70, T], BF16, kind="ExternalOutput").ap()
        dbg["dbg_kbq"] = nc.dram_tensor("dbg_kbq", [24, T], BF16, kind="ExternalOutput").ap()
        dbg["dbg_wqk"] = nc.dram_tensor("dbg_wqk", [128, 4 * 1024], BF16, kind="ExternalOutput").ap()
        dbg["dbg_wv"] = nc.dram_tensor("dbg_wv", [128, 2 * 1024], BF16, kind="ExternalOutput").ap()
        dbg["dbg_p3"] = nc.dram_tensor("dbg_p3", [128, 1536], BF16, kind="ExternalOutput").ap()
        dbg["dbg_q0"] = nc.dram_tensor("dbg_q0", [70, 512], BF16, kind="ExternalOutput").ap()

    pl = Plan()
    es = contextlib.ExitStack()

    def mksem(name):
        return Sem(es.enter_context(nc.semaphore(name)), name)

    with es:
        s_ld = mksem("s_ld")
        s_x = mksem("s_x")
        s_xn = mksem("s_xn")
        s_m = mksem("s_m")
        s_xs = mksem("s_xs")
        s_q = mksem("s_q")
        s_q3 = mksem("s_q3")
        s_sp = mksem("s_sp")
        s_out = mksem("s_out")
        c_cc = mksem("c_cc")
        v_g = mksem("v_g")
        g_gp = mksem("g_gp")
        s_spy = mksem("s_spy")
        s_wld = mksem("s_wld")
        v_ev = mksem("v_ev")
        a_m = mksem("a_m")
        t_m = mksem("t_m")
        t_qk = mksem("t_qk")
        a_e = mksem("a_e")
        t_pv = mksem("t_pv")

        ident = es.enter_context(nc.sbuf_tensor("ident", [128, 128], BF16))
        maskt = es.enter_context(nc.sbuf_tensor("maskt", [128, 4 * 512], F32))
        shv = es.enter_context(nc.sbuf_tensor("shv", [128, 1], F32))
        epsv = es.enter_context(nc.sbuf_tensor("epsv", [128, 1], F32))
        ones1 = es.enter_context(nc.sbuf_tensor("ones1", [1, 128], BF16))
        vscr = es.enter_context(nc.sbuf_tensor("vscr", [1, 16], F32))
        sscr = es.enter_context(nc.sbuf_tensor("sscr", [1, 16], F32))
        gscr = es.enter_context(nc.sbuf_tensor("gscr", [1, 16], F32))

        def vfl(sem):
            pl.op("vector", lambda e: e.tensor_copy(vscr[:, 0:8], vscr[:, 8:16]), sem=sem)

        def sfl(sem):
            pl.op("scalar", lambda e: e.copy(sscr[:, 0:8], sscr[:, 8:16]), sem=sem)

        def tfl(ap, sem):
            pl.op("tensor", lambda e, o=ap: e.matmul(
                o, ident[:, 0:128], ident[:, 0:128], start=True, stop=True), sem=sem)

        def g_consts(g):
            g.memset(ident[:, :], 0.0)
            g.affine_select(out=ident[:, :], in_=ident[:, :],
                            compare_op=ALU.not_equal, fill=1.0, base=0,
                            pattern=[[-1, 128]], channel_multiplier=1)
            g.memset(maskt[:, :], 0.0)
            for v in range(4):
                g.affine_select(out=maskt[:, 512 * v:512 * (v + 1)],
                                in_=maskt[:, 512 * v:512 * (v + 1)],
                                compare_op=ALU.is_ge, fill=-100000.0,
                                base=-128 * v,
                                pattern=[[1, 512]], channel_multiplier=-1)
            g.memset(shv[:, :], EXPSHIFT)
            g.memset(epsv[:, :], 1e-5)
            return g.memset(ones1[:, :], 1.0)
        pl.op("gpsimd", g_consts)
        pl.op("gpsimd", lambda e: e.memset(gscr[:, :], 0.0), sem=g_gp)
        GCONST = g_gp.n

        esA = contextlib.ExitStack()
        with esA:
            def sbA(name, shape, dt):
                return esA.enter_context(nc.sbuf_tensor(name, shape, dt))

            xt = [sbA(f"xt{i}", [128, D], F32) for i in range(2)]
            sqt = sbA("sqt", [128, D], F32)
            xnb = [sbA(f"xnb{i}", [128, D], BF16) for i in range(2)]
            st1 = sbA("st1", [128, NT], F32)
            st2 = sbA("st2", [128, NT], F32)
            nmv = sbA("nmv", [128, NT], F32)
            m2v = sbA("m2v", [128, NT], F32)
            vav = sbA("vav", [128, NT], F32)
            sdv = sbA("sdv", [128, NT], F32)
            rsv = sbA("rsv", [128, NT], F32)
            stg = [sbA(f"stg{c}", [128, TOK], BF16) for c in range(8)]
            tp = [esA.enter_context(nc.psum_tensor(f"tp{i}", [128, D], BF16))
                  for i in range(2)]
            psA = esA.enter_context(nc.psum_tensor("psA", [128, 2048], F32))
            psB = esA.enter_context(nc.psum_tensor("psB", [128, 128], F32))
            ptt = esA.enter_context(nc.psum_tensor("ptt", [128, 128], BF16))

            a_sq = mksem("a_sq")
            v_xn = mksem("v_xn")
            t_tr = mksem("t_tr")
            v_st = mksem("v_st")

            kq_sb = sbA("kq_sb", [128, 512], F32)
            pl.op("sync", lambda e: e.dma_start(out=kq_sb[:, :], in_=kq_in[:, :]),
                  sem=s_ld, k=16)
            KQLD = s_ld.n

            for t in range(NT if not NOLN else 0):
                b = t % 2
                if t >= 2:
                    pl.w("gpsimd", v_xn, t - 1)
                    pl.w("gpsimd", a_sq, t - 1)
                pl.op("gpsimd", lambda e, o=xt[b][:, :], i=x_own[128 * t:128 * (t + 1), :]:
                      e.dma_start(out=o, in_=i), sem=s_x, k=16)
                pl.w("scalar", s_x, 16 * (t + 1))
                if t >= 1:
                    pl.w("scalar", v_g)
                pl.op("scalar", lambda e, i=xt[b][:, :]:
                      e.activation(sqt[:, :], i, AF.Square))
                sfl(a_sq)
                pl.w("vector", s_x, 16 * (t + 1))
                pl.op("vector", lambda e, o=st1[:, t:t + 1], i=xt[b][:, :]:
                      e.reduce_sum(o, i, axis=AX.X))
                pl.w("vector", a_sq, t + 1)
                pl.op("vector", lambda e, o=st2[:, t:t + 1]:
                      e.reduce_sum(o, sqt[:, :], axis=AX.X), sem=v_g)
                pl.op("vector", lambda e, o=nmv[:, t:t + 1], i=st1[:, t:t + 1]:
                      e.tensor_scalar(o, i, -1.0 / D, None, ALU.mult), sem=v_g)
                pl.w("vector", v_g)
                pl.op("vector", lambda e, o=m2v[:, t:t + 1], i=nmv[:, t:t + 1]:
                      e.tensor_tensor(o, i, i, ALU.mult))
                pl.op("vector", lambda e, o=vav[:, t:t + 1], i=st2[:, t:t + 1]:
                      e.tensor_scalar(o, i, 1.0 / D, None, ALU.mult), sem=v_g)
                pl.w("vector", v_g)
                pl.op("vector", lambda e, o=vav[:, t:t + 1], i=m2v[:, t:t + 1]:
                      e.tensor_tensor(o, o, i, ALU.subtract), sem=v_g)
                pl.w("scalar", v_g)
                pl.op("scalar", lambda e, o=sdv[:, t:t + 1], i=vav[:, t:t + 1]:
                      e.activation(o, i, AF.Sqrt, bias=epsv[:, 0:1]))
                sfl(a_m)
                pl.w("vector", a_m)
                pl.op("vector", lambda e, o=rsv[:, t:t + 1], i=sdv[:, t:t + 1]:
                      e.reciprocal(o, i), sem=v_g)
                pl.w("vector", v_g)
                if t >= 2:
                    pl.w("vector", t_tr, t - 1)
                pl.op("vector", lambda e, o=xnb[b][:, :], i=xt[b][:, :],
                      s1=nmv[:, t:t + 1], s2=rsv[:, t:t + 1]:
                      e.tensor_scalar(o, i, s1, s2, ALU.add, ALU.mult))
                vfl(v_xn)
                pl.w("tensor", v_xn, t + 1)
                pl.w("tensor", g_gp, GCONST)
                if t >= 2:
                    pl.w("tensor", v_st, t - 1)
                for c in range(8):
                    pl.op("tensor", lambda e, o=tp[b][:, 128 * c:128 * (c + 1)],
                          i=xnb[b][:, 128 * c:128 * (c + 1)]:
                          e.transpose(o, i, ident[:, :]))
                tfl(psA[0:128, 0:128], t_tr)
                pl.w("vector", t_tr, t + 1)
                for c in range(8):
                    pl.op("vector", lambda e, o=stg[c][:, 128 * t:128 * (t + 1)],
                          i=tp[b][:, 128 * c:128 * (c + 1)]:
                          e.tensor_copy(o, i))
                vfl(v_st)

            pl.w("sync", v_st, v_st.n)
            for c in range(8 if not NOLN else 0):
                pl.op("sync", lambda e, o=xnc.ap()[128 * c:128 * (c + 1), :], i=stg[c][:, :]:
                      e.dma_start(out=o, in_=i), sem=s_xn, k=16)
            wr_sb = sbA("wr_sb", [128, 8 * D], BF16)
            wi_sb = sbA("wi_sb", [128, 8 * D], BF16)
            c_sb = sbA("c_sb", [128, 8 * D], BF16)
            ss_sb = sbA("ss_sb", [128, 8 * D], BF16)
            ccol = sbA("ccol", [128, 8 * 128], BF16)
            scol = sbA("scol", [128, 8 * 128], BF16)
            wtmp = [sbA(f"wtmp{i}", [128, D], F32) for i in range(2)]
            btmp = [sbA(f"btmp{i}", [128, 128], F32) for i in range(2)]
            b_sb = sbA("b_sb", [128, 8 * 256], BF16)
            sintmp = sbA("sintmp", [128, D], F32)
            mtp = sbA("mtp", [128, 128], BF16)
            mstage = sbA("mstage", [128, D], BF16)
            s_mw = mksem("s_mw")
            a_mm = mksem("a_mm")
            v_mw = mksem("v_mw")

            SC = 2.0 * math.pi / 1024.0
            wload = []
            for c in range(8):
                wload.append((wfr_in[128 * c:128 * (c + 1), :], wr_sb, D * c, "v"))
                wload.append((mtabc_c[128 * c:128 * (c + 1), :], ccol, 128 * c, "a"))
            for c in range(8):
                wload.append((wfi_in[128 * c:128 * (c + 1), :], wi_sb, D * c, "v"))
                wload.append((mtabc_s[128 * c:128 * (c + 1), :], scol, 128 * c, "a"))
            for c in range(8):
                wload.append((mtab_c[128 * c:128 * (c + 1), :], c_sb, D * c, "a"))
                wload.append((mtab_s[128 * c:128 * (c + 1), :], ss_sb, D * c, "a"))
            pl.w("sync", v_st, v_st.n)
            pl.w("sync", s_xn, s_xn.n)
            slot_guard = [None, None]
            for li, (src, dst, off, engk) in enumerate(wload):
                bb = li % 2
                wide = dst in (wr_sb, wi_sb, c_sb, ss_sb)
                buf = wtmp[bb] if wide else btmp[bb]
                if engk == "v":
                    pl.op("gpsimd", lambda e, o=dst[:, off:off + D], i=src:
                          e.dma_start(out=o, in_=i), sem=s_wld, k=16)
                    continue
                g = slot_guard[bb]
                if g is not None:
                    pl.w("sync", g[0], g[1])
                pl.op("sync", lambda e, o=buf[:, 0:(D if wide else 128)], i=src:
                      e.dma_start(out=o, in_=i), sem=s_mw, k=16)
                if True:
                    pl.w("scalar", s_mw, s_mw.n)
                    pl.op("scalar", lambda e, o=sintmp[:, 0:(D if wide else 128)],
                          i=buf[:, 0:(D if wide else 128)]:
                          e.activation(o, i, AF.Sin, scale=SC))
                    sfl(a_mm)
                    slot_guard[bb] = (a_mm, a_mm.n)
                    pl.w("vector", a_mm, a_mm.n)
                    pl.op("vector", lambda e, o=dst[:, off:off + (D if wide else 128)],
                          i=sintmp[:, 0:(D if wide else 128)]:
                          e.tensor_copy(o, i))
                    vfl(v_mw)
                    pl.w("scalar", v_mw, v_mw.n)
            W_CASTS_V = v_mw.n
            W_CASTS_A = a_mm.n

            pl.w("tensor", v_mw, W_CASTS_V)
            pl.w("tensor", s_wld, s_wld.n)
            pl.w("tensor", a_mm, W_CASTS_A)
            evac_guard = 0
            for oc in range(8):
                if oc >= 1:
                    pl.w("tensor", v_g, evac_guard)
                for wi_, wsb in ((0, wr_sb), (1, wi_sb)):
                    for ci, csb in ((0, ccol), (1, scol)):
                        reg = 512 * (2 * wi_ + ci)
                        for kc in range(8):
                            pl.op("tensor", lambda e,
                                  o=psA[:, reg:reg + 128],
                                  l=wsb[:, D * kc + 128 * oc: D * kc + 128 * (oc + 1)],
                                  r=csb[:, 128 * kc:128 * (kc + 1)],
                                  stt=(kc == 0), sp=(kc == 7):
                                  e.matmul(o, l, r, start=stt, stop=sp))
                tfl(psB[0:128, 0:128], t_m)
                pl.w("scalar", t_m)
                pl.op("scalar", lambda e, o=btmp[0][:, :], i=psA[:, 1536:1664]:
                      e.activation(o, i, AF.Copy, scale=1.0 / 1024.0))
                pl.op("scalar", lambda e, o=btmp[1][:, :], i=psA[:, 1024:1152]:
                      e.activation(o, i, AF.Copy, scale=-1.0 / 1024.0))
                sfl(a_mm)
                pl.w("vector", a_mm)
                pl.op("vector", lambda e, o=b_sb[:, 256 * oc:256 * oc + 128],
                      i=psA[:, 0:128], tt=btmp[0][:, :]:
                      e.scalar_tensor_tensor(o, i, 1.0 / 1024.0, tt, ALU.mult, ALU.add))
                pl.op("vector", lambda e, o=b_sb[:, 256 * oc + 128:256 * oc + 256],
                      i=psA[:, 512:640], tt=btmp[1][:, :]:
                      e.scalar_tensor_tensor(o, i, 1.0 / 1024.0, tt, ALU.mult, ALU.add))
                vfl(v_g)
                evac_guard = v_g.n
            B_DONE = v_g.n

            mw_guard = v_mw.n
            for oc in range(8):
                pl.w("tensor", v_g, B_DONE)
                if oc >= 1:
                    pl.w("tensor", v_mw, mw_guard)
                for cc in range(8):
                    pl.op("tensor", lambda e, o=psB[:, :],
                          l=c_sb[:, D * cc + 128 * oc: D * cc + 128 * (oc + 1)],
                          r=b_sb[:, 256 * cc:256 * cc + 128], stt=(cc == 0):
                          e.matmul(o, l, r, start=stt, stop=False))
                for cc in range(8):
                    pl.op("tensor", lambda e, o=psB[:, :],
                          l=ss_sb[:, D * cc + 128 * oc: D * cc + 128 * (oc + 1)],
                          r=b_sb[:, 256 * cc + 128:256 * cc + 256], sp=(cc == 7):
                          e.matmul(o, l, r, start=False, stop=sp))
                tfl(psA[0:128, 0:128], t_m)
                pl.w("vector", t_m)
                pl.op("vector", lambda e, o=mtp[:, :], i=psB[:, :]:
                      e.tensor_copy(o, i))
                vfl(v_g)
                pl.w("tensor", v_g)
                pl.op("tensor", lambda e, o=ptt[:, :], i=mtp[:, :]:
                      e.transpose(o, i, ident[:, :]))
                tfl(psA[0:128, 0:128], t_m)
                pl.w("vector", t_m)
                pl.op("vector", lambda e, o=mstage[:, 128 * oc:128 * (oc + 1)], i=ptt[:, :]:
                      e.tensor_copy(o, i))
                vfl(v_mw)
                mw_guard = v_mw.n
            pl.w("sync", v_mw, v_mw.n)
            pl.op("sync", lambda e: e.dma_start(out=mpc.ap()[:, :], in_=mstage[:, :]),
                  sem=s_m, k=16)
            if DEBUG and MIDDUMPS:
                pl.op("sync", lambda e: e.dma_start(out=dbg["dbg_bsb"][:, :], in_=b_sb[:, :]),
                      sem=s_m, k=16)
                pl.op("sync", lambda e: e.dma_start(out=dbg["dbg_wr"][:, :], in_=wr_sb[:, :]),
                      sem=s_m, k=16)
                pl.op("sync", lambda e: e.dma_start(out=dbg["dbg_ccol"][:, 0:1024], in_=ccol[:, :]),
                      sem=s_m, k=16)
                pl.op("sync", lambda e: e.dma_start(out=dbg["dbg_ccol"][:, 1024:2048], in_=scol[:, :]),
                      sem=s_m, k=16)
                pl.op("sync", lambda e: e.dma_start(out=dbg["dbg_csb"][:, :], in_=c_sb[:, :]),
                      sem=s_m, k=16)
            kqh = sbA("kqh", [128, 512], BF16)
            kqm = sbA("kqm", [128, 512], BF16)
            kql = sbA("kql", [128, 512], BF16)
            kqr = sbA("kqr", [128, 512], F32)
            kqr2 = sbA("kqr2", [128, 512], F32)
            pl.w("vector", s_ld, KQLD)
            for opf in () if NOKQ else (
                lambda e: e.tensor_copy(kqh[:, :], kq_sb[:, :]),
                lambda e: e.tensor_tensor(kqr[:, :], kq_sb[:, :], kqh[:, :], ALU.subtract),
                lambda e: e.tensor_copy(kqm[:, :], kqr[:, :]),
                lambda e: e.tensor_tensor(kqr2[:, :], kqr[:, :], kqm[:, :], ALU.subtract),
                lambda e: e.tensor_copy(kql[:, :], kqr2[:, :]),
            ):
                pl.op("vector", opf, sem=v_g)
                pl.w("vector", v_g)
            vfl(v_g)
            KQ_DONE = v_g.n
            pl.w("sync", v_g, KQ_DONE)
            for blk in range(8 if not NOKQ else 0):
                for i, src in enumerate((kqh, kqm, kql)):
                    pl.op("sync", lambda e,
                          o=kbq.ap()[3 * blk + i:3 * blk + i + 1, :]
                          .rearrange("o (p j) -> (o p) j", p=128),
                          s=src[:, 64 * blk:64 * (blk + 1)]:
                          e.dma_start(out=o, in_=s), sem=s_spy, k=16)
            KBQ_DMAS = s_spy.n
            pl.w("gpsimd", s_spy, KBQ_DMAS)
            if not (NOQAUG or NOKQ):
                for s in range(4):
                    for i in range(3):
                        pl.op("gpsimd", lambda e, o=qaug.ap()[70 * s + 64 + i:70 * s + 65 + i, :],
                              i2=ones_in[0:1, :]:
                              e.dma_start(out=o, in_=i2), sem=s_sp, k=16)
                    for i in range(3):
                        pl.op("gpsimd", lambda e, o=qaug.ap()[70 * s + 67 + i:70 * s + 68 + i, :],
                              i2=kbq.ap()[3 * (4 + s) + i:3 * (4 + s) + i + 1, :]:
                              e.dma_start(out=o, in_=i2), sem=s_sp, k=16)
            QAUG_ONES = s_sp.n
            for sm_ in (s_x, s_ld, s_xn, s_m, s_mw, s_wld, s_sp, s_spy):
                pl.w("gpsimd", sm_, sm_.n)
            if not NOCC:
                pl.op("gpsimd", lambda e: e.collective_compute(
                    "AllGather", ALU.bypass,
                    replica_groups=[[0, 1, 2, 3], [4, 5, 6, 7]],
                    ins=[xnc.ap().opt()], outs=[xng.ap()[0:4 * D, :].opt()]), sem=c_cc)
            CC_XN = c_cc.n
            pl.w("gpsimd", c_cc, c_cc.n)
            if not NOCC:
                pl.op("gpsimd", lambda e: e.collective_compute(
                    "AllGather", ALU.bypass, replica_groups=[list(range(8))],
                    ins=[mpc.ap().opt()], outs=[mg.ap().opt()]), sem=c_cc)
            CC_M = c_cc.n
            pl.w("gpsimd", c_cc, c_cc.n)
            if DEBUG and PHASES < 2:
                pl.op("gpsimd", lambda e: e.dma_start(out=dbg["dbg_wr2"][:, :], in_=wr_sb[:, :]),
                      sem=s_out, k=16)
                pl.w("gpsimd", s_out, s_out.n)

        PA_SM = s_m.n
        PA_SXN = s_xn.n
        PA_SSP = s_sp.n
        PA_SSPY = s_spy.n
        if PHASES < 2:
            pl.w("sync", s_sp, s_sp.n)
            pl.w("sync", s_m, s_m.n)
            pl.w("sync", c_cc, c_cc.n)
            if DEBUG:
                for src, dst in ((xnc, dbg["dbg_xnc"]), (mg, dbg["dbg_mg"]),
                                 (qaug, dbg["dbg_q"]), (mpc, dbg["dbg_mpc"])):
                    pl.op("gpsimd", lambda e, o=dst[:, :], i=src.ap()[:, :]:
                          e.dma_start(out=o, in_=i), sem=s_out, k=16)
            pl.w("gpsimd", s_out, s_out.n)
            return _emit(nc, pl)
        wp_sb = es.enter_context(nc.sbuf_tensor("wp_sb", [128, 16 * D], BF16))
        wqk_sb = es.enter_context(nc.sbuf_tensor("wqk_sb", [128, 4 * 1024], BF16))
        wv_sb = es.enter_context(nc.sbuf_tensor("wv_sb", [128, 2 * 1024], BF16))
        brow = es.enter_context(nc.sbuf_tensor("brow", [1, D], BF16))

        def wload2(src_rows, dst, dst_off):
            pl.op("gpsimd", lambda e,
                  o=dst[:, dst_off:dst_off + D].rearrange("p (c m) -> p c m", m=128),
                  i=src_rows.rearrange("(c p) m -> p c m", p=128):
                  e.dma_start(out=o, in_=i), sem=s_wld, k=16)

        for s in range(4):
            wload2(wqk_in[D * s:D * (s + 1), :], wqk_sb, 1024 * s)
        for p in range(2):
            wload2(wv_in[D * p:D * (p + 1), :], wv_sb, 1024 * p)
        for c in range(16):
            pl.op("gpsimd", lambda e, o=wp_sb[:, D * c:D * (c + 1)],
                  i=wp_in[128 * c:128 * (c + 1), :]:
                  e.dma_start(out=o, in_=i), sem=s_wld, k=16)
        pl.op("gpsimd", lambda e: e.dma_start(out=brow[:, :], in_=bias_in[:, :]),
              sem=s_wld, k=16)
        WEIGHTS_OK_W = s_wld.n
        esBC = contextlib.ExitStack()
        kaug = [esBC.enter_context(nc.sbuf_tensor(f"kaug{s}", [70, T], BF16))
                for s in range(4)]
        vaug = [esBC.enter_context(nc.sbuf_tensor(f"vaug{s}", [128, 64 * 65], BF16))
                for s in range(4)]

        esB = contextlib.ExitStack()
        with esB:
            def sbB(name, shape, dt):
                return esB.enter_context(nc.sbuf_tensor(name, shape, dt))

            xs = [sbB(f"xs{i}", [128, 8 * 512], BF16) for i in range(2)]
            m_sb = sbB("m_sb", [128, 8 * D], BF16)
            qstage = [sbB(f"qstage{i}", [64, 512], BF16) for i in range(2)]
            gtile = [sbB(f"gtile{i}", [128, D], BF16) for i in range(2)]
            psQK = [esB.enter_context(nc.psum_tensor(f"psQK{i}", [128, 512], F32))
                    for i in range(2)]
            psV = [esB.enter_context(nc.psum_tensor(f"psV{i}", [128, 128], F32))
                   for i in range(2)]
            psDum = esB.enter_context(nc.psum_tensor("psDum", [128, 128], F32))

            pl.w("gpsimd", c_cc, CC_M)
            pl.w("gpsimd", s_m, PA_SM)
            pl.w("gpsimd", s_xn, PA_SXN)
            pl.w("gpsimd", s_sp, PA_SSP)
            pl.w("gpsimd", s_spy, PA_SSPY)
            for s in range(4):
                pl.op("gpsimd", lambda e, o=kaug[s][64:70, :]:
                      e.memset(o, 1.0), sem=g_gp)
            for s in range(4):
                for i in range(3):
                    pl.op("gpsimd", lambda e, o=kaug[s][64 + i:65 + i, :],
                          i2=kbq.ap()[3 * s + i:3 * s + i + 1, :]:
                          e.dma_start(out=o, in_=i2), sem=s_sp, k=16)
            KB_ROWS = s_sp.n

            pl.w("sync", c_cc, CC_M)
            pl.w("sync", s_m, PA_SM)
            pl.w("sync", s_xn, PA_SXN)
            pl.w("sync", s_sp, PA_SSP)
            pl.w("sync", s_spy, PA_SSPY)

            pl.w("sync", c_cc, CC_M)
            for c in range(8):
                pl.op("sync", lambda e, o=m_sb[:, D * c:D * (c + 1)],
                      i=mg.ap()[128 * c:128 * (c + 1), :]:
                      e.dma_start(out=o, in_=i), sem=s_m, k=16)
            MSB_OK = s_m.n

            for s in range(4):
                pl.op("gpsimd", lambda e, o=vaug[s][:, :]: e.memset(o, 1.0), sem=g_gp)
            VONES = g_gp.n

            pl.w("sync", c_cc, CC_XN)
            if DEBUG and MIDDUMPS:
                pl.w("sync", v_g, WEIGHTS_OK)
                pl.op("sync", lambda e: e.dma_start(out=dbg["dbg_wqk"][:, :], in_=wqk_sb[:, :]),
                      sem=s_ld, k=16)
                pl.op("sync", lambda e: e.dma_start(out=dbg["dbg_wv"][:, :], in_=wv_sb[:, :]),
                      sem=s_ld, k=16)
                pl.w("sync", s_ld, s_ld.n)
            pl.w("tensor", s_wld, WEIGHTS_OK_W)
            pl.w("tensor", g_gp, VONES)
            pl.w("tensor", s_sp, KB_ROWS)

            psqk_guard = [0, 0]
            psv_guard = [0, 0]
            qstage_guard = [0, 0]
            blk_ev = {}
            for blk in range(16):
                bb = blk % 2
                rb, qt = blk // 4, blk % 4
                if blk >= 2:
                    pl.w("sync", v_ev, blk_ev[blk - 2])
                for c in range(8):
                    pl.op("sync", lambda e, o=xs[bb][:, 512 * c:512 * (c + 1)],
                          i=xng.ap()[1024 * rb + 128 * c:1024 * rb + 128 * (c + 1),
                                     512 * qt:512 * (qt + 1)]:
                          e.dma_start(out=o, in_=i), sem=s_xs, k=16)
                pl.w("tensor", s_xs, s_xs.n)
                vgroups = [(p, sub) for p in range(2) for sub in range(4)]
                for s in range(4):
                    pb = s % 2
                    pl.w("tensor", v_ev, psqk_guard[pb])
                    for c in range(8):
                        pl.op("tensor", lambda e, o=psQK[pb][:, :],
                              l=wqk_sb[:, 1024 * s + 128 * c:1024 * s + 128 * (c + 1)],
                              r=xs[bb][:, 512 * c:512 * (c + 1)],
                              stt=(c == 0), sp=(c == 7):
                              e.matmul(o, l, r, start=stt, stop=sp))
                    tfl(psDum[:, :], t_m)
                    qkd = t_m.n
                    for p, sub in vgroups[2 * s:2 * s + 2]:
                        vb = (p * 4 + sub) % 2
                        jj = 4 * blk + sub
                        pl.w("tensor", v_ev, psv_guard[vb])
                        for c in range(8):
                            pl.op("tensor", lambda e, o=psV[vb][:, :],
                                  l=xs[bb][:, 512 * c + 128 * sub:512 * c + 128 * (sub + 1)],
                                  r=wv_sb[:, 1024 * p + 128 * c:1024 * p + 128 * (c + 1)],
                                  stt=(c == 0), sp=(c == 7):
                                  e.matmul(o, l, r, start=stt, stop=sp))
                        tfl(psDum[:, :], t_m)
                        vd = t_m.n
                        pl.w("vector", t_m, vd)
                        pl.op("vector", lambda e,
                              o=vaug[2 * p][:, 65 * jj:65 * jj + 64],
                              i=psV[vb][:, 0:64]:
                              e.tensor_copy(o, i))
                        pl.op("vector", lambda e,
                              o=vaug[2 * p + 1][:, 65 * jj:65 * jj + 64],
                              i=psV[vb][:, 64:128]:
                              e.tensor_copy(o, i))
                        vfl(v_ev)
                        psv_guard[vb] = v_ev.n
                    pl.w("vector", t_m, qkd)
                    pl.w("vector", s_q, qstage_guard[pb])
                    pl.op("vector", lambda e, o=qstage[pb][:, :], i=psQK[pb][0:64, :]:
                          e.tensor_copy(o, i))
                    pl.op("vector", lambda e,
                          o=kaug[s][0:64, 512 * blk:512 * (blk + 1)],
                          i=psQK[pb][64:128, :]:
                          e.tensor_copy(o, i))
                    vfl(v_ev)
                    psqk_guard[pb] = v_ev.n
                    pl.w("gpsimd", v_ev, v_ev.n)
                    pl.op("gpsimd", lambda e,
                          o=qaug.ap()[70 * s:70 * s + 64, 512 * blk:512 * (blk + 1)],
                          i=qstage[pb][:, :]:
                          e.dma_start(out=o, in_=i), sem=s_q, k=16)
                    qstage_guard[pb] = s_q.n
                blk_ev[blk] = v_ev.n
            vfl(v_ev)
            P2_T = t_m.n
            P2_EV = v_ev.n
            P2_Q = s_q.n

            pl.w("tensor", s_m, MSB_OK)
            pl.w("sync", v_ev, P2_EV)
            ffn_mm = {}
            ffn_silu = {}
            ffn_spill = {}
            for fb in range(4):
                bb = fb % 2
                if fb >= 2:
                    pl.w("sync", t_m, ffn_mm[fb - 2])
                for c in range(8):
                    pl.op("sync", lambda e, o=xs[bb][:, 512 * c:512 * (c + 1)],
                          i=xnc.ap()[128 * c:128 * (c + 1), 512 * fb:512 * (fb + 1)]:
                          e.dma_start(out=o, in_=i), sem=s_xs, k=16)
                pl.w("tensor", s_xs, s_xs.n)
                for sub in range(4):
                    tt_ = 4 * fb + sub
                    gb = tt_ % 2
                    pl.w("tensor", a_m, ffn_silu.get(tt_ - 1, 0))
                    for half in range(2):
                        for c in range(8):
                            pl.op("tensor", lambda e, o=psQK[half][:, :],
                                  l=xs[bb][:, 512 * c + 128 * sub:512 * c + 128 * (sub + 1)],
                                  r=m_sb[:, D * c + 512 * half:D * c + 512 * (half + 1)],
                                  stt=(c == 0):
                                  e.matmul(o, l, r, start=stt, stop=False))
                        pl.op("tensor", lambda e, o=psQK[half][:, :],
                              r=brow[0:1, 512 * half:512 * (half + 1)]:
                              e.matmul(o, ones1[0:1, :], r, start=False, stop=True))
                    tfl(psDum[:, :], t_m)
                    ffn_mm[fb] = t_m.n
                    pl.w("scalar", t_m, t_m.n)
                    pl.w("scalar", s_sp, ffn_spill.get(tt_ - 2, 0))
                    for half in range(2):
                        pl.op("scalar", lambda e, o=gtile[gb][:, 512 * half:512 * (half + 1)],
                              i=psQK[half][:, :]:
                              e.activation(o, i, AF.Silu))
                    sfl(a_m)
                    ffn_silu[tt_] = a_m.n
                    pl.w("gpsimd", a_m, ffn_silu[tt_])
                    pl.op("gpsimd", lambda e, o=gdr.ap()[128 * tt_:128 * (tt_ + 1), :],
                          i=gtile[gb][:, :]:
                          e.dma_start(out=o, in_=i), sem=s_sp, k=16)
                    ffn_spill[tt_] = s_sp.n
            FFN_SP = s_sp.n

        if PHASES < 3:
            pl.w("sync", s_sp, s_sp.n)
            pl.w("sync", s_q, s_q.n)
            if DEBUG:
                for src, dst in ((xnc, dbg["dbg_xnc"]), (mg, dbg["dbg_mg"]),
                                 (qaug, dbg["dbg_q"]), (gdr, dbg["dbg_g"]),
                                 (mpc, dbg["dbg_mpc"])):
                    pl.op("gpsimd", lambda e, o=dst[:, :], i=src.ap()[:, :]:
                          e.dma_start(out=o, in_=i), sem=s_out, k=16)
            pl.w("gpsimd", s_out, s_out.n)
            return _emit(nc, pl)
        esC = contextlib.ExitStack()
        with esC:
            def sbC(name, shape, dt):
                return esC.enter_context(nc.sbuf_tensor(name, shape, dt))

            qsb = [sbC(f"qsb{i}", [70, 512], BF16) for i in range(2)]
            p3sb = [sbC(f"p3sb{i}", [128, 1536], BF16) for i in range(2)]
            ystage = sbC("ystage", [64, 512], BF16)
            dstage = sbC("dstage", [1, 512], F32)
            s3 = [esC.enter_context(nc.psum_tensor(f"s3_{i}", [128, 1536], F32))
                  for i in range(2)]
            yps = [esC.enter_context(nc.psum_tensor(f"yps{i}", [128, 512], F32))
                   for i in range(2)]

            pl.w("tensor", s_q, P2_Q)
            pl.w("tensor", v_ev, P2_EV)
            pl.w("tensor", s_sp, QAUG_ONES)
            pl.w("tensor", s_sp, KB_ROWS)
            pl.w("gpsimd", s_q, P2_Q)
            pl.w("gpsimd", s_sp, QAUG_ONES)

            if DEBUG and MIDDUMPS:
                pl.op("gpsimd", lambda e: e.dma_start(out=dbg["dbg_ka"][:, :], in_=kaug[0][:, :]),
                      sem=s_sp, k=16)
                pl.op("gpsimd", lambda e: e.dma_start(out=dbg["dbg_ka3"][:, :], in_=kaug[3][:, :]),
                      sem=s_sp, k=16)
                pl.op("gpsimd", lambda e: e.dma_start(out=dbg["dbg_kbq"][:, :], in_=kbq.ap()[:, :]),
                      sem=s_sp, k=16)
                pl.w("gpsimd", s_sp, s_sp.n)
            pairs = [(s, qb) for s in range(4) for qb in range(NQB)]
            exp_hist = []
            pv_hist = []
            qload = {}
            qsb_guard = [0, 0]
            yev_guard = [0, 0]
            sp_guard = 0
            tri_idx = 0

            def emit_qload(idx):
                s, qb = pairs[idx]
                qbb = qb % 2
                pl.w("gpsimd", t_qk, qsb_guard[qbb])
                pl.op("gpsimd", lambda e, o=qsb[qbb][:, :],
                      i=qaug.ap()[70 * s:70 * (s + 1), 512 * qb:512 * (qb + 1)]:
                      e.dma_start(out=o, in_=i), sem=s_q3, k=16)
                qload[idx] = s_q3.n

            emit_qload(0)
            for idx, (s, qb) in enumerate(pairs):
                chunks = sched[s][qb]
                qbb = qb % 2
                ntri = (len(chunks) + 2) // 3
                first = True
                pend = None

                def emit_pv(p, s=s, qbb=qbb):
                    tri_, tb_, stt_, last_, _ = p
                    tw_ = len(tri_)
                    for i, j in enumerate(tri_):
                        pl.op("tensor", lambda e, o=yps[qbb][0:65, :],
                              l=vaug[s][:, 65 * j:65 * (j + 1)],
                              r=p3sb[tb_][:, 512 * i:512 * (i + 1)],
                              stt=(stt_ and i == 0),
                              sp=(last_ and i == tw_ - 1):
                              e.matmul(o, l, r, start=stt, stop=sp),
                              sem=(t_pv if i == tw_ - 1 else None))
                    pv_hist.append(t_pv.n)

                for ti in range(ntri):
                    tri = chunks[3 * ti:3 * ti + 3]
                    tb = tri_idx % 2
                    tw = len(tri)
                    if tri_idx >= 2:
                        pl.w("tensor", a_e, exp_hist[tri_idx - 2])
                    pl.w("tensor", s_q3, qload[idx])
                    if first:
                        pl.w("tensor", v_ev, yev_guard[qbb])
                    for i, j in enumerate(tri):
                        pl.op("tensor", lambda e, o=s3[tb][:, 512 * i:512 * (i + 1)],
                              l=kaug[s][:, 128 * j:128 * (j + 1)],
                              r=qsb[qbb][:, :]:
                              e.matmul(o, l, r, start=True, stop=True),
                              sem=(t_qk if i == tw - 1 else None))
                    QKD = t_qk.n
                    has_diag = any(j >= 4 * qb for j in tri)
                    if has_diag:
                        pl.w("vector", t_qk, QKD)
                        for i, j in enumerate(tri):
                            if j >= 4 * qb:
                                v = j - 4 * qb
                                pl.op("vector", lambda e,
                                      o=s3[tb][:, 512 * i:512 * (i + 1)],
                                      m=maskt[:, 512 * v:512 * (v + 1)]:
                                      e.tensor_tensor(o, o, m, ALU.add))
                        vfl(v_g)
                        pl.w("scalar", v_g, v_g.n)
                    pl.w("scalar", t_qk, QKD)
                    if tri_idx >= 2:
                        pl.w("scalar", t_pv, pv_hist[tri_idx - 2])
                    pl.op("scalar", lambda e, o=p3sb[tb][:, 0:512 * tw],
                          i=s3[tb][:, 0:512 * tw]:
                          e.activation(o, i, AF.Exp, bias=shv[:, 0:1], scale=0.125))
                    sfl(a_e)
                    exp_hist.append(a_e.n)
                    if DEBUG and idx == 49 and ti == 0:
                        pl.w("gpsimd", a_e, a_e.n)
                        pl.op("gpsimd", lambda e, src=p3sb[tb]:
                              e.dma_start(out=dbg["dbg_p3"][:, :], in_=src[:, :]),
                              sem=s_sp, k=16)
                        pl.op("gpsimd", lambda e, src=qsb[qbb]:
                              e.dma_start(out=dbg["dbg_q0"][:, :], in_=src[:, :]),
                              sem=s_sp, k=16)
                        DBGP3 = s_sp.n
                        pl.w("scalar", s_sp, DBGP3)
                    if ti == 0 and idx + 1 < len(pairs):
                        emit_qload(idx + 1)
                    if pend is not None:
                        pl.w("tensor", a_e, pend[4])
                        emit_pv(pend)
                    pend = (tri, tb, first, ti == ntri - 1, exp_hist[-1])
                    first = False
                    tri_idx += 1
                pl.w("tensor", a_e, pend[4])
                emit_pv(pend)
                tb_last = (tri_idx - 1) % 2
                tfl(s3[tb_last][0:128, 0:128], t_pv)
                qsb_guard[qbb] = t_qk.n
                PVD = t_pv.n
                pl.w("vector", t_pv, PVD)
                pl.w("vector", s_sp, sp_guard)
                pl.op("vector", lambda e, o=ystage[:, :], i=yps[qbb][0:64, :]:
                      e.tensor_copy(o, i))
                pl.op("vector", lambda e, o=dstage[:, :], i=yps[qbb][64:65, :]:
                      e.tensor_copy(o, i))
                vfl(v_ev)
                yev_guard[qbb] = v_ev.n
                pl.w("gpsimd", v_ev, v_ev.n)
                co = 512 * (qb % 4)
                for dst in (qb // 4, qb // 4 + 4):
                    ro = 256 * dst + 64 * s
                    pl.op("gpsimd", lambda e, o=y_in.ap()[ro:ro + 64, co:co + 512],
                          i=ystage[:, :]:
                          e.dma_start(out=o, in_=i), sem=s_sp, k=16)
                    dr = 4 * dst + s
                    pl.op("gpsimd", lambda e, o=d_in.ap()[dr:dr + 1, co:co + 512],
                          i=dstage[:, :]:
                          e.dma_start(out=o, in_=i), sem=s_sp, k=16)
                sp_guard = s_sp.n
            P3_SP = s_sp.n
            for sm_ in (s_x, s_ld, s_xn, s_m, s_sp, s_spy, s_xs, s_q, s_q3, s_out):
                pl.w("gpsimd", sm_, sm_.n)
            with allow_ici_all_to_all():
                pl.op("gpsimd", lambda e: e.collective_compute(
                    "AllToAll", ALU.bypass,
                    replica_groups=[list(range(8))],
                    ins=[y_in.ap().opt()], outs=[y_out.ap().opt()]), sem=c_cc)
                pl.w("gpsimd", c_cc, c_cc.n)
                pl.op("gpsimd", lambda e: e.collective_compute(
                    "AllToAll", ALU.bypass,
                    replica_groups=[list(range(8))],
                    ins=[d_in.ap().opt()], outs=[d_out.ap().opt()]), sem=c_cc)
            CC_Y = c_cc.n
            pl.w("gpsimd", c_cc, CC_Y)
        esBC.close()
        if PHASES < 4:
            pl.w("sync", s_sp, s_sp.n)
            pl.w("sync", c_cc, c_cc.n)
            if DEBUG:
                for src, dst in ((xnc, dbg["dbg_xnc"]), (mg, dbg["dbg_mg"]),
                                 (qaug, dbg["dbg_q"]), (gdr, dbg["dbg_g"]),
                                 (y_out, dbg["dbg_y"]), (d_out, dbg["dbg_d"]),
                                 (mpc, dbg["dbg_mpc"]), (y_in, dbg["dbg_yin"]),
                                 (d_in, dbg["dbg_din"])):
                    pl.op("gpsimd", lambda e, o=dst[:, :], i=src.ap()[:, :]:
                          e.dma_start(out=o, in_=i), sem=s_out, k=16)
            pl.w("gpsimd", s_out, s_out.n)
            return _emit(nc, pl)

        esD = contextlib.ExitStack()
        with esD:
            def sbD(name, shape, dt):
                return esD.enter_context(nc.sbuf_tensor(name, shape, dt))

            den_sb = sbD("den_sb", [32, TOK], F32)
            rden = sbD("rden", [32, TOK], F32)
            sel_sb = sbD("sel_sb", [32, 2048], F32)
            ysb = [sbD(f"ysb{i}", [128, TOK], BF16) for i in range(2)]
            ynorm = sbD("ynorm", [128, 16 * TOK], BF16)
            gx = [sbD(f"gx{i}", [128, D], BF16) for i in range(2)]
            xres = [sbD(f"xres{i}", [128, D], F32) for i in range(2)]
            tsg = sbD("tsg", [128, D], F32)
            osb = [sbD(f"osb{i}", [128, D], F32) for i in range(2)]
            psDum4 = esD.enter_context(nc.psum_tensor("psDum4", [128, 128], F32))

            pl.w("sync", c_cc, CC_Y)
            pl.op("sync", lambda e: e.dma_start(out=sel_sb[:, :], in_=sel_in[:, :]),
                  sem=s_ld, k=16)
            pl.op("sync", lambda e: e.dma_start(out=den_sb[:, :], in_=d_out.ap()[:, :]),
                  sem=s_ld, k=16)
            pl.w("vector", s_ld, s_ld.n)
            pl.op("vector", lambda e: e.reciprocal(rden[:, :], den_sb[:, :]))
            vfl(v_g)
            RDEN = v_g.n

            esD1 = contextlib.ExitStack()
            bcah = [esD1.enter_context(nc.psum_tensor(f"bcah{i}", [128, 1024], F32))
                    for i in range(2)]
            ysb_ld = [0, 0]
            ymul_done = [0, 0]
            hmul_done = [0, 0]
            for c in range(16):
                cb = c % 2
                if c < 2:
                    pl.op("sync", lambda e, o=ysb[cb][:, :],
                          i=y_out.ap()[128 * c:128 * (c + 1), :]:
                          e.dma_start(out=o, in_=i), sem=s_ld, k=16)
                    ysb_ld[cb] = s_ld.n
                for h in range(2):
                    hb = (2 * c + h) % 2
                    pl.w("tensor", s_ld, ysb_ld[cb])
                    pl.w("tensor", v_g, RDEN)
                    pl.w("tensor", v_g, hmul_done[hb])
                    for q in range(2):
                        qtr = 2 * h + q
                        pl.op("tensor", lambda e, o=bcah[hb][:, 512 * q:512 * (q + 1)],
                              l=sel_sb[:, 128 * c:128 * (c + 1)],
                              r=rden[:, 512 * qtr:512 * (qtr + 1)]:
                              e.matmul(o, l, r, start=True, stop=True))
                    tfl(psDum4[:, :], t_m)
                    pl.w("vector", t_m, t_m.n)
                    pl.op("vector", lambda e,
                          o=ynorm[:, TOK * c + 1024 * h:TOK * c + 1024 * (h + 1)],
                          a=ysb[cb][:, 1024 * h:1024 * (h + 1)], b2=bcah[hb][:, :]:
                          e.tensor_tensor(o, a, b2, ALU.mult))
                    vfl(v_g)
                    hmul_done[hb] = v_g.n
                ymul_done[cb] = v_g.n
                if c + 2 < 16:
                    pl.w("sync", v_g, ymul_done[cb])
                    pl.op("sync", lambda e, o=ysb[cb][:, :],
                          i=y_out.ap()[128 * (c + 2):128 * (c + 3), :]:
                          e.dma_start(out=o, in_=i), sem=s_ld, k=16)
                    ysb_ld[cb] = s_ld.n
            YN = v_g.n
            esD1.close()
            psS = [esD.enter_context(nc.psum_tensor(f"psS{i}", [128, 512], F32))
                   for i in range(4)]

            pl.w("tensor", v_g, YN)
            gx_ld = [0, 0]
            fin_done = [0, 0]
            for t in range(2):
                tb = t % 2
                pl.op("sync", lambda e, o=gx[tb][:, :],
                      i=gdr.ap()[128 * t:128 * (t + 1), :]:
                      e.dma_start(out=o, in_=i), sem=s_ld, k=16)
                pl.op("sync", lambda e, o=xres[tb][:, :],
                      i=x_own[128 * t:128 * (t + 1), :]:
                      e.dma_start(out=o, in_=i), sem=s_ld, k=16)
                gx_ld[tb] = s_ld.n
            for t in range(NT):
                tb = t % 2
                for half in range(2):
                    pk = 2 * tb + half
                    if t >= 2:
                        pl.w("tensor", v_g, fin_done[tb])
                    for c in range(16):
                        pl.op("tensor", lambda e, o=psS[pk][:, :],
                              l=ynorm[:, TOK * c + 128 * t:TOK * c + 128 * (t + 1)],
                              r=wp_sb[:, D * c + 512 * half:D * c + 512 * (half + 1)],
                              stt=(c == 0), sp=(c == 15):
                              e.matmul(o, l, r, start=stt, stop=sp))
                tfl(psDum4[:, :], t_m)
                SD = t_m.n
                pl.w("vector", t_m, SD)
                pl.w("vector", s_ld, gx_ld[tb])
                pl.w("vector", s_sp, FFN_SP)
                if t >= 2:
                    pl.w("vector", s_out, 16 * (t - 1))
                for half in range(2):
                    pk = 2 * tb + half
                    pl.op("vector", lambda e, o=tsg[:, 512 * half:512 * (half + 1)],
                          a=psS[pk][:, :], b2=gx[tb][:, 512 * half:512 * (half + 1)]:
                          e.tensor_tensor(o, a, b2, ALU.mult), sem=v_g)
                pl.w("vector", v_g, v_g.n)
                pl.op("vector", lambda e, o=osb[tb][:, :], a=tsg[:, :], b2=xres[tb][:, :]:
                      e.tensor_tensor(o, a, b2, ALU.add))
                vfl(v_g)
                fin_done[tb] = v_g.n
                pl.w("sync", v_g, fin_done[tb])
                pl.op("sync", lambda e, o=out_own[128 * t:128 * (t + 1), :], i=osb[tb][:, :]:
                      e.dma_start(out=o, in_=i), sem=s_out, k=16)
                if t + 2 < NT:
                    pl.op("sync", lambda e, o=gx[tb][:, :],
                          i=gdr.ap()[128 * (t + 2):128 * (t + 3), :]:
                          e.dma_start(out=o, in_=i), sem=s_ld, k=16)
                    pl.op("sync", lambda e, o=xres[tb][:, :],
                          i=x_own[128 * (t + 2):128 * (t + 3), :]:
                          e.dma_start(out=o, in_=i), sem=s_ld, k=16)
                    gx_ld[tb] = s_ld.n

            if DEBUG:
                pl.w("gpsimd", s_out, s_out.n)
                pl.w("gpsimd", c_cc, CC_Y)
                for src, dst in ((xnc, dbg["dbg_xnc"]), (mg, dbg["dbg_mg"]),
                                 (qaug, dbg["dbg_q"]), (gdr, dbg["dbg_g"]),
                                 (y_out, dbg["dbg_y"]), (d_out, dbg["dbg_d"]),
                                 (mpc, dbg["dbg_mpc"]), (y_in, dbg["dbg_yin"]),
                                 (d_in, dbg["dbg_din"])):
                    pl.op("gpsimd", lambda e, o=dst[:, :], i=src.ap()[:, :]:
                          e.dma_start(out=o, in_=i), sem=s_out, k=16)

            pl.w("sync", s_out, s_out.n)

        return _emit(nc, pl)


def _emit(nc, pl):
    with allow_ici_all_to_all(), nc.Block() as block:
        @block.sync
        def _(e):
            for f in pl.q["sync"]:
                f(e)

        @block.vector
        def _(e):
            for f in pl.q["vector"]:
                f(e)

        @block.scalar
        def _(e):
            for f in pl.q["scalar"]:
                f(e)

        @block.tensor
        def _(e):
            for f in pl.q["tensor"]:
                f(e)

        @block.gpsimd
        def _(e):
            for f in pl.q["gpsimd"]:
                f(e)

    return nc



def host_inputs(L0_attn_x, L0_attn_w_attn, L0_attn_w_proj,
                L0_ffn_w_freq_real, L0_ffn_w_freq_imag, L0_ffn_bias):
    x = np.asarray(L0_attn_x, dtype=np.float32)
    wa = np.ascontiguousarray(np.asarray(L0_attn_w_attn, dtype=np.float32))
    wp = np.asarray(L0_attn_w_proj, dtype=np.float32)
    wfr = np.ascontiguousarray(np.asarray(L0_ffn_w_freq_real, dtype=np.float32))
    wfi = np.ascontiguousarray(np.asarray(L0_ffn_w_freq_imag, dtype=np.float32))
    fb = np.ascontiguousarray(np.asarray(L0_ffn_bias, dtype=np.float32).reshape(1, D))

    slots, Ls, sl = head_slots()
    perm = np.zeros(D, dtype=np.int64)
    for g in range(4):
        for s in range(4):
            h = slots[s][g]
            perm[256 * g + 64 * s: 256 * g + 64 * (s + 1)] = np.arange(64 * h, 64 * (h + 1))
    wpt1 = wp.T[perm, :]
    wpt = np.ascontiguousarray(np.concatenate([wpt1, wpt1], axis=0))

    jk = (np.arange(D, dtype=np.int64)[:, None] * np.arange(D, dtype=np.int64)[None, :])
    msin = ((((jk) % D) + 512) % D - 512).astype(np.float32)
    mcos = ((((jk + 256) % D) + 512) % D - 512).astype(np.float32)

    karr = (np.arange(128, dtype=np.float64)[:, None] * 64 +
            np.arange(64, dtype=np.float64)[None, :])


    ones32 = np.ones((1, T), dtype=np.float32)

    in_maps = []
    for core in range(NC):
        b, g = core // 4, core % 4
        heads = [slots[s][g] for s in range(4)]
        wqk = np.concatenate([
            np.ascontiguousarray(
                np.concatenate([wa[64 * h:64 * (h + 1), :],
                                wa[D + 64 * h:D + 64 * (h + 1), :]], axis=0).T)
            for h in heads], axis=0)
        wv = np.concatenate([
            np.ascontiguousarray(
                np.concatenate(
                    [wa[2 * D + 64 * heads[2 * p]:2 * D + 64 * (heads[2 * p] + 1), :],
                     wa[2 * D + 64 * heads[2 * p + 1]:2 * D + 64 * (heads[2 * p + 1] + 1), :]],
                    axis=0).T)
            for p in range(2)], axis=0)
        selt = np.zeros((32, 2048), dtype=np.float32)
        gl = np.arange(2048)
        hb = gl // 64
        ownrow = (hb < 16) if b == 0 else (hb >= 16)
        selt[hb[ownrow], gl[ownrow]] = 1.0
        kq = np.zeros((128, 512), dtype=np.float32)
        for s in range(4):
            ss = float(sl[heads[s]])
            kq[:, 64 * s:64 * (s + 1)] = (8.0 * ss * karr).astype(np.float32)
            kq[:, 256 + 64 * s:256 + 64 * (s + 1)] = (-8.0 * ss * karr).astype(np.float32)
        in_maps.append({
            "x_own": np.ascontiguousarray(x[b, 2048 * g:2048 * (g + 1), :]),
            "wqk": wqk, "wv": wv, "wpt": wpt,
            "wfr": wfr, "wfi": wfi, "fbias": fb,
            "mtab_s": msin, "mtab_c": mcos,
            "mtabc_s": np.ascontiguousarray(msin[:, 128 * core:128 * (core + 1)]),
            "mtabc_c": np.ascontiguousarray(mcos[:, 128 * core:128 * (core + 1)]),
            "kq128": kq, "seltab": selt, "ones32": ones32,
        })
    return in_maps


_NC_CACHE = [None]


def kernel(L0_attn_x, L0_attn_w_attn, L0_attn_w_proj,
           L0_ffn_w_freq_real, L0_ffn_w_freq_imag, L0_ffn_bias):
    in_maps = host_inputs(L0_attn_x, L0_attn_w_attn, L0_attn_w_proj,
                          L0_ffn_w_freq_real, L0_ffn_w_freq_imag, L0_ffn_bias)
    if _NC_CACHE[0] is None:
        _NC_CACHE[0] = build_nc()
    res = run_bass_kernel_spmd(_NC_CACHE[0], in_maps, core_ids=list(range(NC)))
    out = np.zeros((B, T, D), dtype=np.float32)
    for core in range(NC):
        b, g = core // 4, core % 4
        out[b, 2048 * g:2048 * (g + 1), :] = res.results[core]["out_own"]
    if DEBUG:
        kernel.last = res
    return out


# version 11
# speedup vs baseline: 1.1560x; 1.0529x over previous
import contextlib
import math
import os

import numpy as np
import ml_dtypes

import concourse.bass as bass
import concourse.mybir as mybir
from concourse.bass_utils import run_bass_kernel_spmd
from concourse.aot_env import allow_ici_all_to_all

F32 = mybir.dt.float32
BF16 = mybir.dt.bfloat16
AF = mybir.ActivationFunctionType
ALU = mybir.AluOpType
AX = mybir.AxisListType
BF = ml_dtypes.bfloat16

ENGINES = ("sync", "vector", "scalar", "tensor", "gpsimd")

B, T, D, H, HD = 2, 8192, 1024, 16, 64
NC = 8
TOK = 2048
NT = TOK // 128
NBLK = 16
NQB = 16
QBW = 512
CKEEP = 30.0
EXPSHIFT = -30.0
DEBUG = bool(int(os.environ.get("MK_DEBUG", "0")))
MIDDUMPS = bool(int(os.environ.get("MK_MIDDUMPS", "0")))
PHASES = int(os.environ.get("MK_PHASES", "4"))
NOCC = bool(int(os.environ.get("MK_NOCC", "0")))
NOQAUG = bool(int(os.environ.get("MK_NOQAUG", "0")))
NOLN = bool(int(os.environ.get("MK_NOLN", "0")))
NOKQ = bool(int(os.environ.get("MK_NOKQ", "0")))


def alibi_slopes(n_heads=H):
    start = 2.0 ** (-2.0 ** (-(math.log2(n_heads) - 3)))
    return np.array([start * start ** i for i in range(n_heads)], dtype=np.float32)


def head_slots():
    sl = alibi_slopes()
    ranks = list(np.argsort(sl))
    slots = [ranks[4 * s: 4 * s + 4] for s in range(4)]
    Ls = [int(math.ceil(CKEEP / float(sl[slots[s][0]]))) for s in range(4)]
    return slots, Ls, sl


def chunk_sched(L):
    out = []
    for qb in range(NQB):
        qb0 = QBW * qb
        jmax = (qb0 + QBW - 1) // 128
        jmin = max(0, int(math.ceil((qb0 - L - 127) / 128.0)))
        out.append(list(range(jmin, jmax + 1)))
    return out


class Sem:
    __slots__ = ("h", "n", "name")

    def __init__(self, h, name):
        self.h = h
        self.n = 0
        self.name = name


class Plan:
    def __init__(self):
        self.q = {e: [] for e in ENGINES}

    def em(self, eng, fn):
        self.q[eng].append(fn)

    def w(self, eng, sem, val=None):
        v = sem.n if val is None else val
        if v > 0:
            self.em(eng, lambda e, s=sem.h, vv=v: e.wait_ge(s, vv))

    def op(self, eng, fn, sem=None, k=1):
        if sem is not None:
            sem.n += k
            self.em(eng, lambda e, f=fn, s=sem.h, kk=k: f(e).then_inc(s, kk))
        else:
            self.em(eng, fn)


def build_nc():
    slots, Ls, sl = head_slots()
    sched = [chunk_sched(Ls[s]) for s in range(4)]

    nc = bass.Bass("TRN2", target_bir_lowering=False, debug=False)

    def ein(name, shape, dt=F32):
        return nc.dram_tensor(name, shape, dt, kind="ExternalInput").ap()

    x_own = ein("x_own", [TOK, D])
    wqk_in = ein("wqk", [4 * D, 128])
    wv_in = ein("wv", [2 * D, 128])
    wp_in = ein("wpt", [2 * D, D])
    wfr_in = ein("wfr", [D, D])
    wfi_in = ein("wfi", [D, D])
    bias_in = ein("fbias", [1, D])
    mtab_s = ein("mtab_s", [D, D])
    mtab_c = ein("mtab_c", [D, D])
    mtabc_s = ein("mtabc_s", [D, 128])
    mtabc_c = ein("mtabc_c", [D, 128])
    kq_in = ein("kq128", [128, 512])
    sel_in = ein("seltab", [32, 2048])
    ones_in = ein("ones32", [1, T])

    out_own = nc.dram_tensor("out_own", [TOK, D], F32, kind="ExternalOutput").ap()

    xnc = nc.dram_tensor("xnc", [D, TOK], BF16)
    xng = nc.dram_tensor("xng", [8 * D, TOK], BF16)
    mpc = nc.dram_tensor("mpc", [128, D], BF16)
    mg = nc.dram_tensor("mg", [D, D], BF16)
    kbq = nc.dram_tensor("kbq", [24, T], BF16)
    qaug = nc.dram_tensor("qaug", [4 * 70, T], BF16)
    gdr = nc.dram_tensor("gdr", [TOK, D], BF16)
    y_in = nc.dram_tensor("y_in", [2048, TOK], BF16)
    y_out = nc.dram_tensor("y_out", [2048, TOK], BF16)
    d_in = nc.dram_tensor("d_in", [32, TOK], F32)
    d_out = nc.dram_tensor("d_out", [32, TOK], F32)

    dbg = {}
    if DEBUG:
        for nm, src in (("dbg_xnc", [D, TOK]), ("dbg_mg", [D, D]),
                        ("dbg_q", [4 * 70, T]), ("dbg_g", [TOK, D]),
                        ("dbg_y", [2048, TOK])):
            dbg[nm] = nc.dram_tensor(nm, src, BF16, kind="ExternalOutput").ap()
        dbg["dbg_d"] = nc.dram_tensor("dbg_d", [32, TOK], F32, kind="ExternalOutput").ap()
        dbg["dbg_mpc"] = nc.dram_tensor("dbg_mpc", [128, D], BF16, kind="ExternalOutput").ap()
        dbg["dbg_ka"] = nc.dram_tensor("dbg_ka", [70, T], BF16, kind="ExternalOutput").ap()
        dbg["dbg_yin"] = nc.dram_tensor("dbg_yin", [2048, TOK], BF16, kind="ExternalOutput").ap()
        dbg["dbg_bsb"] = nc.dram_tensor("dbg_bsb", [128, 2048], BF16, kind="ExternalOutput").ap()
        dbg["dbg_wr"] = nc.dram_tensor("dbg_wr", [128, 8 * D], BF16, kind="ExternalOutput").ap()
        dbg["dbg_ccol"] = nc.dram_tensor("dbg_ccol", [128, 2 * 1024], BF16, kind="ExternalOutput").ap()
        dbg["dbg_csb"] = nc.dram_tensor("dbg_csb", [128, 8 * D], BF16, kind="ExternalOutput").ap()
        dbg["dbg_din"] = nc.dram_tensor("dbg_din", [32, TOK], F32, kind="ExternalOutput").ap()
        dbg["dbg_wr2"] = nc.dram_tensor("dbg_wr2", [128, 8 * D], BF16, kind="ExternalOutput").ap()
        dbg["dbg_ka3"] = nc.dram_tensor("dbg_ka3", [70, T], BF16, kind="ExternalOutput").ap()
        dbg["dbg_kbq"] = nc.dram_tensor("dbg_kbq", [24, T], BF16, kind="ExternalOutput").ap()
        dbg["dbg_wqk"] = nc.dram_tensor("dbg_wqk", [128, 4 * 1024], BF16, kind="ExternalOutput").ap()
        dbg["dbg_wv"] = nc.dram_tensor("dbg_wv", [128, 2 * 1024], BF16, kind="ExternalOutput").ap()
        dbg["dbg_p3"] = nc.dram_tensor("dbg_p3", [128, 1536], BF16, kind="ExternalOutput").ap()
        dbg["dbg_q0"] = nc.dram_tensor("dbg_q0", [70, 512], BF16, kind="ExternalOutput").ap()

    pl = Plan()
    es = contextlib.ExitStack()

    def mksem(name):
        return Sem(es.enter_context(nc.semaphore(name)), name)

    with es:
        s_ld = mksem("s_ld")
        s_x = mksem("s_x")
        s_xn = mksem("s_xn")
        s_m = mksem("s_m")
        s_xs = mksem("s_xs")
        s_q = mksem("s_q")
        s_q3 = mksem("s_q3")
        s_sp = mksem("s_sp")
        s_out = mksem("s_out")
        c_cc = mksem("c_cc")
        v_g = mksem("v_g")
        g_gp = mksem("g_gp")
        s_spy = mksem("s_spy")
        s_wld = mksem("s_wld")
        v_ev = mksem("v_ev")
        a_m = mksem("a_m")
        t_m = mksem("t_m")
        t_qk = mksem("t_qk")
        a_e = mksem("a_e")
        t_pv = mksem("t_pv")

        ident = es.enter_context(nc.sbuf_tensor("ident", [128, 128], BF16))
        maskt = es.enter_context(nc.sbuf_tensor("maskt", [128, 4 * 512], F32))
        mask01 = es.enter_context(nc.sbuf_tensor("mask01", [128, 4 * 512], BF16))
        shv = es.enter_context(nc.sbuf_tensor("shv", [128, 1], F32))
        epsv = es.enter_context(nc.sbuf_tensor("epsv", [128, 1], F32))
        ones1 = es.enter_context(nc.sbuf_tensor("ones1", [1, 128], BF16))
        vscr = es.enter_context(nc.sbuf_tensor("vscr", [1, 16], F32))
        sscr = es.enter_context(nc.sbuf_tensor("sscr", [1, 16], F32))
        gscr = es.enter_context(nc.sbuf_tensor("gscr", [1, 16], F32))

        def vfl(sem):
            pl.op("vector", lambda e: e.tensor_copy(vscr[:, 0:8], vscr[:, 8:16]), sem=sem)

        def sfl(sem):
            pl.op("scalar", lambda e: e.copy(sscr[:, 0:8], sscr[:, 8:16]), sem=sem)

        def tfl(ap, sem):
            pl.op("tensor", lambda e, o=ap: e.matmul(
                o, ident[:, 0:128], ident[:, 0:128], start=True, stop=True), sem=sem)

        def g_consts(g):
            g.memset(ident[:, :], 0.0)
            g.affine_select(out=ident[:, :], in_=ident[:, :],
                            compare_op=ALU.not_equal, fill=1.0, base=0,
                            pattern=[[-1, 128]], channel_multiplier=1)
            g.memset(maskt[:, :], 0.0)
            for v in range(4):
                g.affine_select(out=maskt[:, 512 * v:512 * (v + 1)],
                                in_=maskt[:, 512 * v:512 * (v + 1)],
                                compare_op=ALU.is_ge, fill=-100000.0,
                                base=-128 * v,
                                pattern=[[1, 512]], channel_multiplier=-1)
            g.memset(mask01[:, :], 3.3895e38)
            for v in range(4):
                g.affine_select(out=mask01[:, 512 * v:512 * (v + 1)],
                                in_=mask01[:, 512 * v:512 * (v + 1)],
                                compare_op=ALU.is_ge, fill=0.0,
                                base=-128 * v,
                                pattern=[[1, 512]], channel_multiplier=-1)
            g.memset(shv[:, :], EXPSHIFT)
            g.memset(epsv[:, :], 1e-5)
            return g.memset(ones1[:, :], 1.0)
        pl.op("gpsimd", g_consts)
        pl.op("gpsimd", lambda e: e.memset(gscr[:, :], 0.0), sem=g_gp)
        GCONST = g_gp.n

        esA = contextlib.ExitStack()
        with esA:
            def sbA(name, shape, dt):
                return esA.enter_context(nc.sbuf_tensor(name, shape, dt))

            xt = [sbA(f"xt{i}", [128, D], F32) for i in range(2)]
            sqt = sbA("sqt", [128, D], F32)
            xnb = [sbA(f"xnb{i}", [128, D], BF16) for i in range(2)]
            st1 = sbA("st1", [128, NT], F32)
            st2 = sbA("st2", [128, NT], F32)
            nmv = sbA("nmv", [128, NT], F32)
            m2v = sbA("m2v", [128, NT], F32)
            vav = sbA("vav", [128, NT], F32)
            sdv = sbA("sdv", [128, NT], F32)
            rsv = sbA("rsv", [128, NT], F32)
            stg = [sbA(f"stg{c}", [128, TOK], BF16) for c in range(8)]
            tp = [esA.enter_context(nc.psum_tensor(f"tp{i}", [128, D], BF16))
                  for i in range(2)]
            psA = esA.enter_context(nc.psum_tensor("psA", [128, 2048], F32))
            psB = esA.enter_context(nc.psum_tensor("psB", [128, 128], F32))
            ptt = esA.enter_context(nc.psum_tensor("ptt", [128, 128], BF16))

            a_sq = mksem("a_sq")
            v_xn = mksem("v_xn")
            t_tr = mksem("t_tr")
            v_st = mksem("v_st")

            kq_sb = sbA("kq_sb", [128, 512], F32)
            pl.op("sync", lambda e: e.dma_start(out=kq_sb[:, :], in_=kq_in[:, :]),
                  sem=s_ld, k=16)
            KQLD = s_ld.n

            for t in range(NT if not NOLN else 0):
                b = t % 2
                if t >= 2:
                    pl.w("gpsimd", v_xn, t - 1)
                    pl.w("gpsimd", a_sq, t - 1)
                pl.op("gpsimd", lambda e, o=xt[b][:, :], i=x_own[128 * t:128 * (t + 1), :]:
                      e.dma_start(out=o, in_=i), sem=s_x, k=16)
                pl.w("scalar", s_x, 16 * (t + 1))
                if t >= 1:
                    pl.w("scalar", v_g)
                pl.op("scalar", lambda e, i=xt[b][:, :]:
                      e.activation(sqt[:, :], i, AF.Square))
                sfl(a_sq)
                pl.w("vector", s_x, 16 * (t + 1))
                pl.op("vector", lambda e, o=st1[:, t:t + 1], i=xt[b][:, :]:
                      e.reduce_sum(o, i, axis=AX.X))
                pl.w("vector", a_sq, t + 1)
                pl.op("vector", lambda e, o=st2[:, t:t + 1]:
                      e.reduce_sum(o, sqt[:, :], axis=AX.X), sem=v_g)
                pl.op("vector", lambda e, o=nmv[:, t:t + 1], i=st1[:, t:t + 1]:
                      e.tensor_scalar(o, i, -1.0 / D, None, ALU.mult), sem=v_g)
                pl.w("vector", v_g)
                pl.op("vector", lambda e, o=m2v[:, t:t + 1], i=nmv[:, t:t + 1]:
                      e.tensor_tensor(o, i, i, ALU.mult))
                pl.op("vector", lambda e, o=vav[:, t:t + 1], i=st2[:, t:t + 1]:
                      e.tensor_scalar(o, i, 1.0 / D, None, ALU.mult), sem=v_g)
                pl.w("vector", v_g)
                pl.op("vector", lambda e, o=vav[:, t:t + 1], i=m2v[:, t:t + 1]:
                      e.tensor_tensor(o, o, i, ALU.subtract), sem=v_g)
                pl.w("scalar", v_g)
                pl.op("scalar", lambda e, o=sdv[:, t:t + 1], i=vav[:, t:t + 1]:
                      e.activation(o, i, AF.Sqrt, bias=epsv[:, 0:1]))
                sfl(a_m)
                pl.w("vector", a_m)
                pl.op("vector", lambda e, o=rsv[:, t:t + 1], i=sdv[:, t:t + 1]:
                      e.reciprocal(o, i), sem=v_g)
                pl.w("vector", v_g)
                if t >= 2:
                    pl.w("vector", t_tr, t - 1)
                pl.op("vector", lambda e, o=xnb[b][:, :], i=xt[b][:, :],
                      s1=nmv[:, t:t + 1], s2=rsv[:, t:t + 1]:
                      e.tensor_scalar(o, i, s1, s2, ALU.add, ALU.mult))
                vfl(v_xn)
                pl.w("tensor", v_xn, t + 1)
                pl.w("tensor", g_gp, GCONST)
                if t >= 2:
                    pl.w("tensor", v_st, t - 1)
                for c in range(8):
                    pl.op("tensor", lambda e, o=tp[b][:, 128 * c:128 * (c + 1)],
                          i=xnb[b][:, 128 * c:128 * (c + 1)]:
                          e.transpose(o, i, ident[:, :]))
                tfl(psA[0:128, 0:128], t_tr)
                pl.w("vector", t_tr, t + 1)
                for c in range(8):
                    pl.op("vector", lambda e, o=stg[c][:, 128 * t:128 * (t + 1)],
                          i=tp[b][:, 128 * c:128 * (c + 1)]:
                          e.tensor_copy(o, i))
                vfl(v_st)

            pl.w("sync", v_st, v_st.n)
            for c in range(8 if not NOLN else 0):
                pl.op("sync", lambda e, o=xnc.ap()[128 * c:128 * (c + 1), :], i=stg[c][:, :]:
                      e.dma_start(out=o, in_=i), sem=s_xn, k=16)
            wr_sb = sbA("wr_sb", [128, 8 * D], BF16)
            wi_sb = sbA("wi_sb", [128, 8 * D], BF16)
            c_sb = sbA("c_sb", [128, 8 * D], BF16)
            ss_sb = sbA("ss_sb", [128, 8 * D], BF16)
            ccol = sbA("ccol", [128, 8 * 128], BF16)
            scol = sbA("scol", [128, 8 * 128], BF16)
            wtmp = [sbA(f"wtmp{i}", [128, D], F32) for i in range(2)]
            btmp = [sbA(f"btmp{i}", [128, 128], F32) for i in range(2)]
            b_sb = sbA("b_sb", [128, 8 * 256], BF16)
            sintmp = sbA("sintmp", [128, D], F32)
            mtp = sbA("mtp", [128, 128], BF16)
            mstage = sbA("mstage", [128, D], BF16)
            s_mw = mksem("s_mw")
            a_mm = mksem("a_mm")
            v_mw = mksem("v_mw")

            SC = 2.0 * math.pi / 1024.0
            wload = []
            for c in range(8):
                wload.append((wfr_in[128 * c:128 * (c + 1), :], wr_sb, D * c, "v"))
                wload.append((mtabc_c[128 * c:128 * (c + 1), :], ccol, 128 * c, "a"))
            for c in range(8):
                wload.append((wfi_in[128 * c:128 * (c + 1), :], wi_sb, D * c, "v"))
                wload.append((mtabc_s[128 * c:128 * (c + 1), :], scol, 128 * c, "a"))
            for c in range(8):
                wload.append((mtab_c[128 * c:128 * (c + 1), :], c_sb, D * c, "a"))
                wload.append((mtab_s[128 * c:128 * (c + 1), :], ss_sb, D * c, "a"))
            pl.w("sync", v_st, v_st.n)
            pl.w("sync", s_xn, s_xn.n)
            kqh = sbA("kqh", [128, 512], BF16)
            kqm = sbA("kqm", [128, 512], BF16)
            kql = sbA("kql", [128, 512], BF16)
            kqr = sbA("kqr", [128, 512], F32)
            kqr2 = sbA("kqr2", [128, 512], F32)
            pl.w("vector", s_ld, KQLD)
            for opf in () if NOKQ else (
                lambda e: e.tensor_copy(kqh[:, :], kq_sb[:, :]),
                lambda e: e.tensor_tensor(kqr[:, :], kq_sb[:, :], kqh[:, :], ALU.subtract),
                lambda e: e.tensor_copy(kqm[:, :], kqr[:, :]),
                lambda e: e.tensor_tensor(kqr2[:, :], kqr[:, :], kqm[:, :], ALU.subtract),
                lambda e: e.tensor_copy(kql[:, :], kqr2[:, :]),
            ):
                pl.op("vector", opf, sem=v_g)
                pl.w("vector", v_g)
            vfl(v_g)
            KQ_DONE = v_g.n
            pl.w("sync", v_g, KQ_DONE)
            for blk in range(8 if not NOKQ else 0):
                for i, src in enumerate((kqh, kqm, kql)):
                    pl.op("sync", lambda e,
                          o=kbq.ap()[3 * blk + i:3 * blk + i + 1, :]
                          .rearrange("o (p j) -> (o p) j", p=128),
                          s=src[:, 64 * blk:64 * (blk + 1)]:
                          e.dma_start(out=o, in_=s), sem=s_spy, k=16)
            KBQ_DMAS = s_spy.n
            pl.w("gpsimd", s_spy, KBQ_DMAS)
            if not (NOQAUG or NOKQ):
                for s in range(4):
                    for i in range(3):
                        pl.op("gpsimd", lambda e, o=qaug.ap()[70 * s + 64 + i:70 * s + 65 + i, :],
                              i2=ones_in[0:1, :]:
                              e.dma_start(out=o, in_=i2), sem=s_sp, k=16)
                    for i in range(3):
                        pl.op("gpsimd", lambda e, o=qaug.ap()[70 * s + 67 + i:70 * s + 68 + i, :],
                              i2=kbq.ap()[3 * (4 + s) + i:3 * (4 + s) + i + 1, :]:
                              e.dma_start(out=o, in_=i2), sem=s_sp, k=16)
            QAUG_ONES = s_sp.n
            slot_guard = [None, None]
            for li, (src, dst, off, engk) in enumerate(wload):
                bb = li % 2
                wide = dst in (wr_sb, wi_sb, c_sb, ss_sb)
                buf = wtmp[bb] if wide else btmp[bb]
                if engk == "v":
                    pl.op("gpsimd", lambda e, o=dst[:, off:off + D], i=src:
                          e.dma_start(out=o, in_=i), sem=s_wld, k=16)
                    continue
                g = slot_guard[bb]
                if g is not None:
                    pl.w("sync", g[0], g[1])
                pl.op("sync", lambda e, o=buf[:, 0:(D if wide else 128)], i=src:
                      e.dma_start(out=o, in_=i), sem=s_mw, k=16)
                if True:
                    pl.w("scalar", s_mw, s_mw.n)
                    pl.op("scalar", lambda e, o=sintmp[:, 0:(D if wide else 128)],
                          i=buf[:, 0:(D if wide else 128)]:
                          e.activation(o, i, AF.Sin, scale=SC))
                    sfl(a_mm)
                    slot_guard[bb] = (a_mm, a_mm.n)
                    pl.w("vector", a_mm, a_mm.n)
                    pl.op("vector", lambda e, o=dst[:, off:off + (D if wide else 128)],
                          i=sintmp[:, 0:(D if wide else 128)]:
                          e.tensor_copy(o, i))
                    vfl(v_mw)
                    pl.w("scalar", v_mw, v_mw.n)
            W_CASTS_V = v_mw.n
            W_CASTS_A = a_mm.n

            for sm_ in (s_x, s_ld, s_xn, s_m, s_mw, s_wld, s_sp, s_spy):
                pl.w("gpsimd", sm_, sm_.n)
            if not NOCC:
                pl.op("gpsimd", lambda e: e.collective_compute(
                    "AllGather", ALU.bypass,
                    replica_groups=[[0, 1, 2, 3], [4, 5, 6, 7]],
                    ins=[xnc.ap().opt()], outs=[xng.ap()[0:4 * D, :].opt()]), sem=c_cc)
            CC_XN = c_cc.n

            pl.w("tensor", v_mw, W_CASTS_V)
            pl.w("tensor", s_wld, s_wld.n)
            pl.w("tensor", a_mm, W_CASTS_A)
            evac_guard = 0
            for oc in range(8):
                if oc >= 1:
                    pl.w("tensor", v_g, evac_guard)
                for wi_, wsb in ((0, wr_sb), (1, wi_sb)):
                    for ci, csb in ((0, ccol), (1, scol)):
                        reg = 512 * (2 * wi_ + ci)
                        for kc in range(8):
                            pl.op("tensor", lambda e,
                                  o=psA[:, reg:reg + 128],
                                  l=wsb[:, D * kc + 128 * oc: D * kc + 128 * (oc + 1)],
                                  r=csb[:, 128 * kc:128 * (kc + 1)],
                                  stt=(kc == 0), sp=(kc == 7):
                                  e.matmul(o, l, r, start=stt, stop=sp))
                tfl(psB[0:128, 0:128], t_m)
                pl.w("scalar", t_m)
                pl.op("scalar", lambda e, o=btmp[0][:, :], i=psA[:, 1536:1664]:
                      e.activation(o, i, AF.Copy, scale=1.0 / 1024.0))
                pl.op("scalar", lambda e, o=btmp[1][:, :], i=psA[:, 1024:1152]:
                      e.activation(o, i, AF.Copy, scale=-1.0 / 1024.0))
                sfl(a_mm)
                pl.w("vector", a_mm)
                pl.op("vector", lambda e, o=b_sb[:, 256 * oc:256 * oc + 128],
                      i=psA[:, 0:128], tt=btmp[0][:, :]:
                      e.scalar_tensor_tensor(o, i, 1.0 / 1024.0, tt, ALU.mult, ALU.add))
                pl.op("vector", lambda e, o=b_sb[:, 256 * oc + 128:256 * oc + 256],
                      i=psA[:, 512:640], tt=btmp[1][:, :]:
                      e.scalar_tensor_tensor(o, i, 1.0 / 1024.0, tt, ALU.mult, ALU.add))
                vfl(v_g)
                evac_guard = v_g.n
            B_DONE = v_g.n

            mw_guard = v_mw.n
            for oc in range(8):
                pl.w("tensor", v_g, B_DONE)
                if oc >= 1:
                    pl.w("tensor", v_mw, mw_guard)
                for cc in range(8):
                    pl.op("tensor", lambda e, o=psB[:, :],
                          l=c_sb[:, D * cc + 128 * oc: D * cc + 128 * (oc + 1)],
                          r=b_sb[:, 256 * cc:256 * cc + 128], stt=(cc == 0):
                          e.matmul(o, l, r, start=stt, stop=False))
                for cc in range(8):
                    pl.op("tensor", lambda e, o=psB[:, :],
                          l=ss_sb[:, D * cc + 128 * oc: D * cc + 128 * (oc + 1)],
                          r=b_sb[:, 256 * cc + 128:256 * cc + 256], sp=(cc == 7):
                          e.matmul(o, l, r, start=False, stop=sp))
                tfl(psA[0:128, 0:128], t_m)
                pl.w("vector", t_m)
                pl.op("vector", lambda e, o=mtp[:, :], i=psB[:, :]:
                      e.tensor_copy(o, i))
                vfl(v_g)
                pl.w("tensor", v_g)
                pl.op("tensor", lambda e, o=ptt[:, :], i=mtp[:, :]:
                      e.transpose(o, i, ident[:, :]))
                tfl(psA[0:128, 0:128], t_m)
                pl.w("vector", t_m)
                pl.op("vector", lambda e, o=mstage[:, 128 * oc:128 * (oc + 1)], i=ptt[:, :]:
                      e.tensor_copy(o, i))
                vfl(v_mw)
                mw_guard = v_mw.n
            pl.w("sync", v_mw, v_mw.n)
            pl.w("sync", c_cc, CC_XN)
            pl.op("sync", lambda e: e.dma_start(out=mpc.ap()[:, :], in_=mstage[:, :]),
                  sem=s_m, k=16)
            if DEBUG and MIDDUMPS:
                pl.op("sync", lambda e: e.dma_start(out=dbg["dbg_bsb"][:, :], in_=b_sb[:, :]),
                      sem=s_m, k=16)
                pl.op("sync", lambda e: e.dma_start(out=dbg["dbg_wr"][:, :], in_=wr_sb[:, :]),
                      sem=s_m, k=16)
                pl.op("sync", lambda e: e.dma_start(out=dbg["dbg_ccol"][:, 0:1024], in_=ccol[:, :]),
                      sem=s_m, k=16)
                pl.op("sync", lambda e: e.dma_start(out=dbg["dbg_ccol"][:, 1024:2048], in_=scol[:, :]),
                      sem=s_m, k=16)
                pl.op("sync", lambda e: e.dma_start(out=dbg["dbg_csb"][:, :], in_=c_sb[:, :]),
                      sem=s_m, k=16)
            pl.w("gpsimd", c_cc, CC_XN)
            for sm_ in (s_m, s_spy, s_sp, s_ld):
                pl.w("gpsimd", sm_, sm_.n)
            if not NOCC:
                pl.op("gpsimd", lambda e: e.collective_compute(
                    "AllGather", ALU.bypass, replica_groups=[list(range(8))],
                    ins=[mpc.ap().opt()], outs=[mg.ap().opt()]), sem=c_cc)
            CC_M = c_cc.n
            pl.w("gpsimd", c_cc, c_cc.n)
            if DEBUG and PHASES < 2:
                pl.op("gpsimd", lambda e: e.dma_start(out=dbg["dbg_wr2"][:, :], in_=wr_sb[:, :]),
                      sem=s_out, k=16)
                pl.w("gpsimd", s_out, s_out.n)

        PA_SM = s_m.n
        PA_SXN = s_xn.n
        PA_SSP = s_sp.n
        PA_SSPY = s_spy.n
        if PHASES < 2:
            pl.w("sync", s_sp, s_sp.n)
            pl.w("sync", s_m, s_m.n)
            pl.w("sync", c_cc, c_cc.n)
            if DEBUG:
                for src, dst in ((xnc, dbg["dbg_xnc"]), (mg, dbg["dbg_mg"]),
                                 (qaug, dbg["dbg_q"]), (mpc, dbg["dbg_mpc"])):
                    pl.op("gpsimd", lambda e, o=dst[:, :], i=src.ap()[:, :]:
                          e.dma_start(out=o, in_=i), sem=s_out, k=16)
            pl.w("gpsimd", s_out, s_out.n)
            return _emit(nc, pl)
        wp_sb = es.enter_context(nc.sbuf_tensor("wp_sb", [128, 16 * D], BF16))
        wqk_sb = es.enter_context(nc.sbuf_tensor("wqk_sb", [128, 4 * 1024], BF16))
        wv_sb = es.enter_context(nc.sbuf_tensor("wv_sb", [128, 2 * 1024], BF16))
        brow = es.enter_context(nc.sbuf_tensor("brow", [1, D], BF16))

        def wload2(src_rows, dst, dst_off):
            pl.op("gpsimd", lambda e,
                  o=dst[:, dst_off:dst_off + D].rearrange("p (c m) -> p c m", m=128),
                  i=src_rows.rearrange("(c p) m -> p c m", p=128):
                  e.dma_start(out=o, in_=i), sem=s_wld, k=16)

        for s in range(4):
            wload2(wqk_in[D * s:D * (s + 1), :], wqk_sb, 1024 * s)
        for p in range(2):
            wload2(wv_in[D * p:D * (p + 1), :], wv_sb, 1024 * p)
        for c in range(16):
            pl.op("gpsimd", lambda e, o=wp_sb[:, D * c:D * (c + 1)],
                  i=wp_in[128 * c:128 * (c + 1), :]:
                  e.dma_start(out=o, in_=i), sem=s_wld, k=16)
        pl.op("gpsimd", lambda e: e.dma_start(out=brow[:, :], in_=bias_in[:, :]),
              sem=s_wld, k=16)
        WEIGHTS_OK_W = s_wld.n
        esBC = contextlib.ExitStack()
        kaug = [esBC.enter_context(nc.sbuf_tensor(f"kaug{s}", [70, T], BF16))
                for s in range(4)]
        vaug = [esBC.enter_context(nc.sbuf_tensor(f"vaug{s}", [128, 64 * 65], BF16))
                for s in range(4)]

        esB = contextlib.ExitStack()
        with esB:
            def sbB(name, shape, dt):
                return esB.enter_context(nc.sbuf_tensor(name, shape, dt))

            xs = [sbB(f"xs{i}", [128, 8 * 512], BF16) for i in range(2)]
            m_sb = sbB("m_sb", [128, 8 * D], BF16)
            qstage = [sbB(f"qstage{i}", [64, 512], BF16) for i in range(2)]
            gtile = [sbB(f"gtile{i}", [128, D], BF16) for i in range(2)]
            psQK = [esB.enter_context(nc.psum_tensor(f"psQK{i}", [128, 512], F32))
                    for i in range(2)]
            psV = [esB.enter_context(nc.psum_tensor(f"psV{i}", [128, 128], F32))
                   for i in range(2)]
            psDum = esB.enter_context(nc.psum_tensor("psDum", [128, 128], F32))

            pl.w("gpsimd", c_cc, CC_M)
            pl.w("gpsimd", s_m, PA_SM)
            pl.w("gpsimd", s_xn, PA_SXN)
            pl.w("gpsimd", s_sp, PA_SSP)
            pl.w("gpsimd", s_spy, PA_SSPY)
            for s in range(4):
                pl.op("gpsimd", lambda e, o=kaug[s][64:70, :]:
                      e.memset(o, 1.0), sem=g_gp)
            for s in range(4):
                for i in range(3):
                    pl.op("gpsimd", lambda e, o=kaug[s][64 + i:65 + i, :],
                          i2=kbq.ap()[3 * s + i:3 * s + i + 1, :]:
                          e.dma_start(out=o, in_=i2), sem=s_sp, k=16)
            KB_ROWS = s_sp.n

            pl.w("sync", c_cc, CC_M)
            pl.w("sync", s_m, PA_SM)
            pl.w("sync", s_xn, PA_SXN)
            pl.w("sync", s_sp, PA_SSP)
            pl.w("sync", s_spy, PA_SSPY)

            pl.w("sync", c_cc, CC_M)
            for c in range(8):
                pl.op("sync", lambda e, o=m_sb[:, D * c:D * (c + 1)],
                      i=mg.ap()[128 * c:128 * (c + 1), :]:
                      e.dma_start(out=o, in_=i), sem=s_m, k=16)
            MSB_OK = s_m.n

            for s in range(4):
                pl.op("gpsimd", lambda e, o=vaug[s][:, :]: e.memset(o, 1.0), sem=g_gp)
            VONES = g_gp.n

            pl.w("sync", c_cc, CC_XN)
            if DEBUG and MIDDUMPS:
                pl.w("sync", v_g, WEIGHTS_OK)
                pl.op("sync", lambda e: e.dma_start(out=dbg["dbg_wqk"][:, :], in_=wqk_sb[:, :]),
                      sem=s_ld, k=16)
                pl.op("sync", lambda e: e.dma_start(out=dbg["dbg_wv"][:, :], in_=wv_sb[:, :]),
                      sem=s_ld, k=16)
                pl.w("sync", s_ld, s_ld.n)
            pl.w("tensor", s_wld, WEIGHTS_OK_W)
            pl.w("tensor", g_gp, VONES)
            pl.w("tensor", s_sp, KB_ROWS)

            psqk_guard = [0, 0]
            psv_guard = [0, 0]
            qstage_guard = [0, 0]
            blk_ev = {}
            for blk in range(16):
                bb = blk % 2
                rb, qt = blk // 4, blk % 4
                if blk >= 2:
                    pl.w("sync", v_ev, blk_ev[blk - 2])
                for c in range(8):
                    pl.op("sync", lambda e, o=xs[bb][:, 512 * c:512 * (c + 1)],
                          i=xng.ap()[1024 * rb + 128 * c:1024 * rb + 128 * (c + 1),
                                     512 * qt:512 * (qt + 1)]:
                          e.dma_start(out=o, in_=i), sem=s_xs, k=16)
                pl.w("tensor", s_xs, s_xs.n)
                vgroups = [(p, sub) for p in range(2) for sub in range(4)]
                for s in range(4):
                    pb = s % 2
                    pl.w("tensor", v_ev, psqk_guard[pb])
                    for c in range(8):
                        pl.op("tensor", lambda e, o=psQK[pb][:, :],
                              l=wqk_sb[:, 1024 * s + 128 * c:1024 * s + 128 * (c + 1)],
                              r=xs[bb][:, 512 * c:512 * (c + 1)],
                              stt=(c == 0), sp=(c == 7):
                              e.matmul(o, l, r, start=stt, stop=sp))
                    tfl(psDum[:, :], t_m)
                    qkd = t_m.n
                    for p, sub in vgroups[2 * s:2 * s + 2]:
                        vb = (p * 4 + sub) % 2
                        jj = 4 * blk + sub
                        pl.w("tensor", v_ev, psv_guard[vb])
                        for c in range(8):
                            pl.op("tensor", lambda e, o=psV[vb][:, :],
                                  l=xs[bb][:, 512 * c + 128 * sub:512 * c + 128 * (sub + 1)],
                                  r=wv_sb[:, 1024 * p + 128 * c:1024 * p + 128 * (c + 1)],
                                  stt=(c == 0), sp=(c == 7):
                                  e.matmul(o, l, r, start=stt, stop=sp))
                        tfl(psDum[:, :], t_m)
                        vd = t_m.n
                        pl.w("vector", t_m, vd)
                        pl.op("vector", lambda e,
                              o=vaug[2 * p][:, 65 * jj:65 * jj + 64],
                              i=psV[vb][:, 0:64]:
                              e.tensor_copy(o, i))
                        pl.op("vector", lambda e,
                              o=vaug[2 * p + 1][:, 65 * jj:65 * jj + 64],
                              i=psV[vb][:, 64:128]:
                              e.tensor_copy(o, i))
                        vfl(v_ev)
                        psv_guard[vb] = v_ev.n
                    pl.w("vector", t_m, qkd)
                    pl.w("vector", s_q, qstage_guard[pb])
                    pl.op("vector", lambda e, o=qstage[pb][:, :], i=psQK[pb][0:64, :]:
                          e.tensor_copy(o, i))
                    pl.op("vector", lambda e,
                          o=kaug[s][0:64, 512 * blk:512 * (blk + 1)],
                          i=psQK[pb][64:128, :]:
                          e.tensor_copy(o, i))
                    vfl(v_ev)
                    psqk_guard[pb] = v_ev.n
                    pl.w("gpsimd", v_ev, v_ev.n)
                    pl.op("gpsimd", lambda e,
                          o=qaug.ap()[70 * s:70 * s + 64, 512 * blk:512 * (blk + 1)],
                          i=qstage[pb][:, :]:
                          e.dma_start(out=o, in_=i), sem=s_q, k=16)
                    qstage_guard[pb] = s_q.n
                blk_ev[blk] = v_ev.n
            vfl(v_ev)
            P2_T = t_m.n
            P2_EV = v_ev.n
            P2_Q = s_q.n

            pl.w("tensor", s_m, MSB_OK)
            pl.w("sync", v_ev, P2_EV)
            ffn_mm = {}
            ffn_silu = {}
            ffn_spill = {}
            for fb in range(4):
                bb = fb % 2
                if fb >= 2:
                    pl.w("sync", t_m, ffn_mm[fb - 2])
                for c in range(8):
                    pl.op("sync", lambda e, o=xs[bb][:, 512 * c:512 * (c + 1)],
                          i=xnc.ap()[128 * c:128 * (c + 1), 512 * fb:512 * (fb + 1)]:
                          e.dma_start(out=o, in_=i), sem=s_xs, k=16)
                pl.w("tensor", s_xs, s_xs.n)
                for sub in range(4):
                    tt_ = 4 * fb + sub
                    gb = tt_ % 2
                    pl.w("tensor", a_m, ffn_silu.get(tt_ - 1, 0))
                    for half in range(2):
                        for c in range(8):
                            pl.op("tensor", lambda e, o=psQK[half][:, :],
                                  l=xs[bb][:, 512 * c + 128 * sub:512 * c + 128 * (sub + 1)],
                                  r=m_sb[:, D * c + 512 * half:D * c + 512 * (half + 1)],
                                  stt=(c == 0):
                                  e.matmul(o, l, r, start=stt, stop=False))
                        pl.op("tensor", lambda e, o=psQK[half][:, :],
                              r=brow[0:1, 512 * half:512 * (half + 1)]:
                              e.matmul(o, ones1[0:1, :], r, start=False, stop=True))
                    tfl(psDum[:, :], t_m)
                    ffn_mm[fb] = t_m.n
                    pl.w("scalar", t_m, t_m.n)
                    pl.w("scalar", s_sp, ffn_spill.get(tt_ - 2, 0))
                    for half in range(2):
                        pl.op("scalar", lambda e, o=gtile[gb][:, 512 * half:512 * (half + 1)],
                              i=psQK[half][:, :]:
                              e.activation(o, i, AF.Silu))
                    sfl(a_m)
                    ffn_silu[tt_] = a_m.n
                    pl.w("gpsimd", a_m, ffn_silu[tt_])
                    pl.op("gpsimd", lambda e, o=gdr.ap()[128 * tt_:128 * (tt_ + 1), :],
                          i=gtile[gb][:, :]:
                          e.dma_start(out=o, in_=i), sem=s_sp, k=16)
                    ffn_spill[tt_] = s_sp.n
            FFN_SP = s_sp.n

        if PHASES < 3:
            pl.w("sync", s_sp, s_sp.n)
            pl.w("sync", s_q, s_q.n)
            if DEBUG:
                for src, dst in ((xnc, dbg["dbg_xnc"]), (mg, dbg["dbg_mg"]),
                                 (qaug, dbg["dbg_q"]), (gdr, dbg["dbg_g"]),
                                 (mpc, dbg["dbg_mpc"])):
                    pl.op("gpsimd", lambda e, o=dst[:, :], i=src.ap()[:, :]:
                          e.dma_start(out=o, in_=i), sem=s_out, k=16)
            pl.w("gpsimd", s_out, s_out.n)
            return _emit(nc, pl)
        esC = contextlib.ExitStack()
        with esC:
            def sbC(name, shape, dt):
                return esC.enter_context(nc.sbuf_tensor(name, shape, dt))

            qsb = [sbC(f"qsb{i}", [70, 512], BF16) for i in range(2)]
            p3sb = [sbC(f"p3sb{i}", [128, 1536], BF16) for i in range(2)]
            ystage = sbC("ystage", [64, 512], BF16)
            dstage = sbC("dstage", [1, 512], F32)
            s3 = [esC.enter_context(nc.psum_tensor(f"s3_{i}", [128, 1536], F32))
                  for i in range(2)]
            yps = [esC.enter_context(nc.psum_tensor(f"yps{i}", [128, 512], F32))
                   for i in range(2)]

            pl.w("tensor", s_q, P2_Q)
            pl.w("tensor", v_ev, P2_EV)
            pl.w("tensor", s_sp, QAUG_ONES)
            pl.w("tensor", s_sp, KB_ROWS)
            pl.w("gpsimd", s_q, P2_Q)
            pl.w("gpsimd", s_sp, QAUG_ONES)

            if DEBUG and MIDDUMPS:
                pl.op("gpsimd", lambda e: e.dma_start(out=dbg["dbg_ka"][:, :], in_=kaug[0][:, :]),
                      sem=s_sp, k=16)
                pl.op("gpsimd", lambda e: e.dma_start(out=dbg["dbg_ka3"][:, :], in_=kaug[3][:, :]),
                      sem=s_sp, k=16)
                pl.op("gpsimd", lambda e: e.dma_start(out=dbg["dbg_kbq"][:, :], in_=kbq.ap()[:, :]),
                      sem=s_sp, k=16)
                pl.w("gpsimd", s_sp, s_sp.n)
            pairs = [(s, qb) for s in range(4) for qb in range(NQB)]
            exp_hist = []
            pv_hist = []
            qload = {}
            qsb_guard = [0, 0]
            yev_guard = [0, 0]
            sp_guard = 0
            tri_idx = 0

            def emit_qload(idx):
                s, qb = pairs[idx]
                qbb = qb % 2
                pl.w("gpsimd", t_qk, qsb_guard[qbb])
                pl.op("gpsimd", lambda e, o=qsb[qbb][:, :],
                      i=qaug.ap()[70 * s:70 * (s + 1), 512 * qb:512 * (qb + 1)]:
                      e.dma_start(out=o, in_=i), sem=s_q3, k=16)
                qload[idx] = s_q3.n

            emit_qload(0)
            for idx, (s, qb) in enumerate(pairs):
                chunks = sched[s][qb]
                qbb = qb % 2
                ntri = (len(chunks) + 2) // 3
                first = True
                pend = None

                def emit_pv(p, s=s, qbb=qbb):
                    tri_, tb_, stt_, last_, _, sel_ = p
                    if sel_:
                        pl.w("tensor", v_g, sel_)
                    tw_ = len(tri_)
                    for i, j in enumerate(tri_):
                        pl.op("tensor", lambda e, o=yps[qbb][0:65, :],
                              l=vaug[s][:, 65 * j:65 * (j + 1)],
                              r=p3sb[tb_][:, 512 * i:512 * (i + 1)],
                              stt=(stt_ and i == 0),
                              sp=(last_ and i == tw_ - 1):
                              e.matmul(o, l, r, start=stt, stop=sp),
                              sem=(t_pv if i == tw_ - 1 else None))
                    pv_hist.append(t_pv.n)

                for ti in range(ntri):
                    tri = chunks[3 * ti:3 * ti + 3]
                    tb = tri_idx % 2
                    tw = len(tri)
                    if tri_idx >= 2:
                        pl.w("tensor", a_e, exp_hist[tri_idx - 2])
                    pl.w("tensor", s_q3, qload[idx])
                    if first:
                        pl.w("tensor", v_ev, yev_guard[qbb])
                    for i, j in enumerate(tri):
                        pl.op("tensor", lambda e, o=s3[tb][:, 512 * i:512 * (i + 1)],
                              l=kaug[s][:, 128 * j:128 * (j + 1)],
                              r=qsb[qbb][:, :]:
                              e.matmul(o, l, r, start=True, stop=True),
                              sem=(t_qk if i == tw - 1 else None))
                    QKD = t_qk.n
                    has_diag = any(j >= 4 * qb for j in tri)
                    pl.w("scalar", t_qk, QKD)
                    if tri_idx >= 2:
                        pl.w("scalar", t_pv, pv_hist[tri_idx - 2])
                    pl.op("scalar", lambda e, o=p3sb[tb][:, 0:512 * tw],
                          i=s3[tb][:, 0:512 * tw]:
                          e.activation(o, i, AF.Exp, bias=shv[:, 0:1], scale=0.125))
                    sfl(a_e)
                    exp_hist.append(a_e.n)
                    sel_done = 0
                    if has_diag:
                        pl.w("vector", a_e, exp_hist[-1])
                        for i, j in enumerate(tri):
                            if j >= 4 * qb:
                                v = j - 4 * qb
                                pl.op("vector", lambda e,
                                      o=p3sb[tb][:, 512 * i:512 * (i + 1)],
                                      m=mask01[:, 512 * v:512 * (v + 1)]:
                                      e.tensor_tensor(o, o, m, ALU.min))
                        vfl(v_g)
                        sel_done = v_g.n
                    if DEBUG and idx == 49 and ti == 0:
                        pl.w("gpsimd", a_e, a_e.n)
                        pl.op("gpsimd", lambda e, src=p3sb[tb]:
                              e.dma_start(out=dbg["dbg_p3"][:, :], in_=src[:, :]),
                              sem=s_sp, k=16)
                        pl.op("gpsimd", lambda e, src=qsb[qbb]:
                              e.dma_start(out=dbg["dbg_q0"][:, :], in_=src[:, :]),
                              sem=s_sp, k=16)
                        DBGP3 = s_sp.n
                        pl.w("scalar", s_sp, DBGP3)
                    if ti == 0 and idx + 1 < len(pairs):
                        emit_qload(idx + 1)
                    if pend is not None:
                        pl.w("tensor", a_e, pend[4])
                        emit_pv(pend)
                    pend = (tri, tb, first, ti == ntri - 1, exp_hist[-1], sel_done)
                    first = False
                    tri_idx += 1
                pl.w("tensor", a_e, pend[4])
                emit_pv(pend)
                tb_last = (tri_idx - 1) % 2
                tfl(s3[tb_last][0:128, 0:128], t_pv)
                qsb_guard[qbb] = t_qk.n
                PVD = t_pv.n
                pl.w("vector", t_pv, PVD)
                pl.w("vector", s_sp, sp_guard)
                pl.op("vector", lambda e, o=ystage[:, :], i=yps[qbb][0:64, :]:
                      e.tensor_copy(o, i))
                pl.op("vector", lambda e, o=dstage[:, :], i=yps[qbb][64:65, :]:
                      e.tensor_copy(o, i))
                vfl(v_ev)
                yev_guard[qbb] = v_ev.n
                pl.w("gpsimd", v_ev, v_ev.n)
                co = 512 * (qb % 4)
                for dst in (qb // 4, qb // 4 + 4):
                    ro = 256 * dst + 64 * s
                    pl.op("gpsimd", lambda e, o=y_in.ap()[ro:ro + 64, co:co + 512],
                          i=ystage[:, :]:
                          e.dma_start(out=o, in_=i), sem=s_sp, k=16)
                    dr = 4 * dst + s
                    pl.op("gpsimd", lambda e, o=d_in.ap()[dr:dr + 1, co:co + 512],
                          i=dstage[:, :]:
                          e.dma_start(out=o, in_=i), sem=s_sp, k=16)
                sp_guard = s_sp.n
            P3_SP = s_sp.n
            for sm_ in (s_x, s_ld, s_xn, s_m, s_sp, s_spy, s_xs, s_q, s_q3, s_out):
                pl.w("gpsimd", sm_, sm_.n)
            with allow_ici_all_to_all():
                pl.op("gpsimd", lambda e: e.collective_compute(
                    "AllToAll", ALU.bypass,
                    replica_groups=[list(range(8))],
                    ins=[y_in.ap().opt()], outs=[y_out.ap().opt()]), sem=c_cc)
                pl.w("gpsimd", c_cc, c_cc.n)
                pl.op("gpsimd", lambda e: e.collective_compute(
                    "AllToAll", ALU.bypass,
                    replica_groups=[list(range(8))],
                    ins=[d_in.ap().opt()], outs=[d_out.ap().opt()]), sem=c_cc)
            CC_Y = c_cc.n
            pl.w("gpsimd", c_cc, CC_Y)
        esBC.close()
        if PHASES < 4:
            pl.w("sync", s_sp, s_sp.n)
            pl.w("sync", c_cc, c_cc.n)
            if DEBUG:
                for src, dst in ((xnc, dbg["dbg_xnc"]), (mg, dbg["dbg_mg"]),
                                 (qaug, dbg["dbg_q"]), (gdr, dbg["dbg_g"]),
                                 (y_out, dbg["dbg_y"]), (d_out, dbg["dbg_d"]),
                                 (mpc, dbg["dbg_mpc"]), (y_in, dbg["dbg_yin"]),
                                 (d_in, dbg["dbg_din"])):
                    pl.op("gpsimd", lambda e, o=dst[:, :], i=src.ap()[:, :]:
                          e.dma_start(out=o, in_=i), sem=s_out, k=16)
            pl.w("gpsimd", s_out, s_out.n)
            return _emit(nc, pl)

        esD = contextlib.ExitStack()
        with esD:
            def sbD(name, shape, dt):
                return esD.enter_context(nc.sbuf_tensor(name, shape, dt))

            den_sb = sbD("den_sb", [32, TOK], F32)
            rden = sbD("rden", [32, TOK], F32)
            sel_sb = sbD("sel_sb", [32, 2048], F32)
            ysb = [sbD(f"ysb{i}", [128, TOK], BF16) for i in range(2)]
            ynorm = sbD("ynorm", [128, 16 * TOK], BF16)
            gx = [sbD(f"gx{i}", [128, D], BF16) for i in range(2)]
            xres = [sbD(f"xres{i}", [128, D], F32) for i in range(2)]
            tsg = sbD("tsg", [128, D], F32)
            osb = [sbD(f"osb{i}", [128, D], F32) for i in range(2)]
            psDum4 = esD.enter_context(nc.psum_tensor("psDum4", [128, 128], F32))

            pl.w("sync", c_cc, CC_Y)
            pl.op("sync", lambda e: e.dma_start(out=sel_sb[:, :], in_=sel_in[:, :]),
                  sem=s_ld, k=16)
            pl.op("sync", lambda e: e.dma_start(out=den_sb[:, :], in_=d_out.ap()[:, :]),
                  sem=s_ld, k=16)
            pl.w("vector", s_ld, s_ld.n)
            pl.op("vector", lambda e: e.reciprocal(rden[:, :], den_sb[:, :]))
            vfl(v_g)
            RDEN = v_g.n

            esD1 = contextlib.ExitStack()
            bcah = [esD1.enter_context(nc.psum_tensor(f"bcah{i}", [128, 1024], F32))
                    for i in range(2)]
            ysb_ld = [0, 0]
            ymul_done = [0, 0]
            hmul_done = [0, 0]
            for c in range(16):
                cb = c % 2
                if c < 2:
                    pl.op("sync", lambda e, o=ysb[cb][:, :],
                          i=y_out.ap()[128 * c:128 * (c + 1), :]:
                          e.dma_start(out=o, in_=i), sem=s_ld, k=16)
                    ysb_ld[cb] = s_ld.n
                for h in range(2):
                    hb = (2 * c + h) % 2
                    pl.w("tensor", s_ld, ysb_ld[cb])
                    pl.w("tensor", v_g, RDEN)
                    pl.w("tensor", v_g, hmul_done[hb])
                    for q in range(2):
                        qtr = 2 * h + q
                        pl.op("tensor", lambda e, o=bcah[hb][:, 512 * q:512 * (q + 1)],
                              l=sel_sb[:, 128 * c:128 * (c + 1)],
                              r=rden[:, 512 * qtr:512 * (qtr + 1)]:
                              e.matmul(o, l, r, start=True, stop=True))
                    tfl(psDum4[:, :], t_m)
                    pl.w("vector", t_m, t_m.n)
                    pl.op("vector", lambda e,
                          o=ynorm[:, TOK * c + 1024 * h:TOK * c + 1024 * (h + 1)],
                          a=ysb[cb][:, 1024 * h:1024 * (h + 1)], b2=bcah[hb][:, :]:
                          e.tensor_tensor(o, a, b2, ALU.mult))
                    vfl(v_g)
                    hmul_done[hb] = v_g.n
                ymul_done[cb] = v_g.n
                if c + 2 < 16:
                    pl.w("sync", v_g, ymul_done[cb])
                    pl.op("sync", lambda e, o=ysb[cb][:, :],
                          i=y_out.ap()[128 * (c + 2):128 * (c + 3), :]:
                          e.dma_start(out=o, in_=i), sem=s_ld, k=16)
                    ysb_ld[cb] = s_ld.n
            YN = v_g.n
            esD1.close()
            psS = [esD.enter_context(nc.psum_tensor(f"psS{i}", [128, 512], F32))
                   for i in range(4)]

            pl.w("tensor", v_g, YN)
            gx_ld = [0, 0]
            fin_done = [0, 0]
            for t in range(2):
                tb = t % 2
                pl.op("sync", lambda e, o=gx[tb][:, :],
                      i=gdr.ap()[128 * t:128 * (t + 1), :]:
                      e.dma_start(out=o, in_=i), sem=s_ld, k=16)
                pl.op("sync", lambda e, o=xres[tb][:, :],
                      i=x_own[128 * t:128 * (t + 1), :]:
                      e.dma_start(out=o, in_=i), sem=s_ld, k=16)
                gx_ld[tb] = s_ld.n
            for t in range(NT):
                tb = t % 2
                for half in range(2):
                    pk = 2 * tb + half
                    if t >= 2:
                        pl.w("tensor", v_g, fin_done[tb])
                    for c in range(16):
                        pl.op("tensor", lambda e, o=psS[pk][:, :],
                              l=ynorm[:, TOK * c + 128 * t:TOK * c + 128 * (t + 1)],
                              r=wp_sb[:, D * c + 512 * half:D * c + 512 * (half + 1)],
                              stt=(c == 0), sp=(c == 15):
                              e.matmul(o, l, r, start=stt, stop=sp))
                tfl(psDum4[:, :], t_m)
                SD = t_m.n
                pl.w("vector", t_m, SD)
                pl.w("vector", s_ld, gx_ld[tb])
                pl.w("vector", s_sp, FFN_SP)
                if t >= 2:
                    pl.w("vector", s_out, 16 * (t - 1))
                for half in range(2):
                    pk = 2 * tb + half
                    pl.op("vector", lambda e, o=tsg[:, 512 * half:512 * (half + 1)],
                          a=psS[pk][:, :], b2=gx[tb][:, 512 * half:512 * (half + 1)]:
                          e.tensor_tensor(o, a, b2, ALU.mult), sem=v_g)
                pl.w("vector", v_g, v_g.n)
                pl.op("vector", lambda e, o=osb[tb][:, :], a=tsg[:, :], b2=xres[tb][:, :]:
                      e.tensor_tensor(o, a, b2, ALU.add))
                vfl(v_g)
                fin_done[tb] = v_g.n
                pl.w("sync", v_g, fin_done[tb])
                pl.op("sync", lambda e, o=out_own[128 * t:128 * (t + 1), :], i=osb[tb][:, :]:
                      e.dma_start(out=o, in_=i), sem=s_out, k=16)
                if t + 2 < NT:
                    pl.op("sync", lambda e, o=gx[tb][:, :],
                          i=gdr.ap()[128 * (t + 2):128 * (t + 3), :]:
                          e.dma_start(out=o, in_=i), sem=s_ld, k=16)
                    pl.op("sync", lambda e, o=xres[tb][:, :],
                          i=x_own[128 * (t + 2):128 * (t + 3), :]:
                          e.dma_start(out=o, in_=i), sem=s_ld, k=16)
                    gx_ld[tb] = s_ld.n

            if DEBUG:
                pl.w("gpsimd", s_out, s_out.n)
                pl.w("gpsimd", c_cc, CC_Y)
                for src, dst in ((xnc, dbg["dbg_xnc"]), (mg, dbg["dbg_mg"]),
                                 (qaug, dbg["dbg_q"]), (gdr, dbg["dbg_g"]),
                                 (y_out, dbg["dbg_y"]), (d_out, dbg["dbg_d"]),
                                 (mpc, dbg["dbg_mpc"]), (y_in, dbg["dbg_yin"]),
                                 (d_in, dbg["dbg_din"])):
                    pl.op("gpsimd", lambda e, o=dst[:, :], i=src.ap()[:, :]:
                          e.dma_start(out=o, in_=i), sem=s_out, k=16)

            pl.w("sync", s_out, s_out.n)

        return _emit(nc, pl)


def _emit(nc, pl):
    with allow_ici_all_to_all(), nc.Block() as block:
        @block.sync
        def _(e):
            for f in pl.q["sync"]:
                f(e)

        @block.vector
        def _(e):
            for f in pl.q["vector"]:
                f(e)

        @block.scalar
        def _(e):
            for f in pl.q["scalar"]:
                f(e)

        @block.tensor
        def _(e):
            for f in pl.q["tensor"]:
                f(e)

        @block.gpsimd
        def _(e):
            for f in pl.q["gpsimd"]:
                f(e)

    return nc



def host_inputs(L0_attn_x, L0_attn_w_attn, L0_attn_w_proj,
                L0_ffn_w_freq_real, L0_ffn_w_freq_imag, L0_ffn_bias):
    x = np.asarray(L0_attn_x, dtype=np.float32)
    wa = np.ascontiguousarray(np.asarray(L0_attn_w_attn, dtype=np.float32))
    wp = np.asarray(L0_attn_w_proj, dtype=np.float32)
    wfr = np.ascontiguousarray(np.asarray(L0_ffn_w_freq_real, dtype=np.float32))
    wfi = np.ascontiguousarray(np.asarray(L0_ffn_w_freq_imag, dtype=np.float32))
    fb = np.ascontiguousarray(np.asarray(L0_ffn_bias, dtype=np.float32).reshape(1, D))

    slots, Ls, sl = head_slots()
    perm = np.zeros(D, dtype=np.int64)
    for g in range(4):
        for s in range(4):
            h = slots[s][g]
            perm[256 * g + 64 * s: 256 * g + 64 * (s + 1)] = np.arange(64 * h, 64 * (h + 1))
    wpt1 = wp.T[perm, :]
    wpt = np.ascontiguousarray(np.concatenate([wpt1, wpt1], axis=0))

    jk = (np.arange(D, dtype=np.int64)[:, None] * np.arange(D, dtype=np.int64)[None, :])
    msin = ((((jk) % D) + 512) % D - 512).astype(np.float32)
    mcos = ((((jk + 256) % D) + 512) % D - 512).astype(np.float32)

    karr = (np.arange(128, dtype=np.float64)[:, None] * 64 +
            np.arange(64, dtype=np.float64)[None, :])


    ones32 = np.ones((1, T), dtype=np.float32)

    in_maps = []
    for core in range(NC):
        b, g = core // 4, core % 4
        heads = [slots[s][g] for s in range(4)]
        wqk = np.concatenate([
            np.ascontiguousarray(
                np.concatenate([wa[64 * h:64 * (h + 1), :],
                                wa[D + 64 * h:D + 64 * (h + 1), :]], axis=0).T)
            for h in heads], axis=0)
        wv = np.concatenate([
            np.ascontiguousarray(
                np.concatenate(
                    [wa[2 * D + 64 * heads[2 * p]:2 * D + 64 * (heads[2 * p] + 1), :],
                     wa[2 * D + 64 * heads[2 * p + 1]:2 * D + 64 * (heads[2 * p + 1] + 1), :]],
                    axis=0).T)
            for p in range(2)], axis=0)
        selt = np.zeros((32, 2048), dtype=np.float32)
        gl = np.arange(2048)
        hb = gl // 64
        ownrow = (hb < 16) if b == 0 else (hb >= 16)
        selt[hb[ownrow], gl[ownrow]] = 1.0
        kq = np.zeros((128, 512), dtype=np.float32)
        for s in range(4):
            ss = float(sl[heads[s]])
            kq[:, 64 * s:64 * (s + 1)] = (8.0 * ss * karr).astype(np.float32)
            kq[:, 256 + 64 * s:256 + 64 * (s + 1)] = (-8.0 * ss * karr).astype(np.float32)
        in_maps.append({
            "x_own": np.ascontiguousarray(x[b, 2048 * g:2048 * (g + 1), :]),
            "wqk": wqk, "wv": wv, "wpt": wpt,
            "wfr": wfr, "wfi": wfi, "fbias": fb,
            "mtab_s": msin, "mtab_c": mcos,
            "mtabc_s": np.ascontiguousarray(msin[:, 128 * core:128 * (core + 1)]),
            "mtabc_c": np.ascontiguousarray(mcos[:, 128 * core:128 * (core + 1)]),
            "kq128": kq, "seltab": selt, "ones32": ones32,
        })
    return in_maps


_NC_CACHE = [None]


def kernel(L0_attn_x, L0_attn_w_attn, L0_attn_w_proj,
           L0_ffn_w_freq_real, L0_ffn_w_freq_imag, L0_ffn_bias):
    in_maps = host_inputs(L0_attn_x, L0_attn_w_attn, L0_attn_w_proj,
                          L0_ffn_w_freq_real, L0_ffn_w_freq_imag, L0_ffn_bias)
    if _NC_CACHE[0] is None:
        _NC_CACHE[0] = build_nc()
    res = run_bass_kernel_spmd(_NC_CACHE[0], in_maps, core_ids=list(range(NC)))
    out = np.zeros((B, T, D), dtype=np.float32)
    for core in range(NC):
        b, g = core // 4, core % 4
        out[b, 2048 * g:2048 * (g + 1), :] = res.results[core]["out_own"]
    if DEBUG:
        kernel.last = res
    return out


# version 13
# speedup vs baseline: 1.2191x; 1.0545x over previous
import contextlib
import math
import os

import numpy as np
import ml_dtypes

import concourse.bass as bass
import concourse.mybir as mybir
from concourse.bass_utils import run_bass_kernel_spmd
from concourse.aot_env import allow_ici_all_to_all

F32 = mybir.dt.float32
BF16 = mybir.dt.bfloat16
AF = mybir.ActivationFunctionType
ALU = mybir.AluOpType
AX = mybir.AxisListType
BF = ml_dtypes.bfloat16

ENGINES = ("sync", "vector", "scalar", "tensor", "gpsimd")

B, T, D, H, HD = 2, 8192, 1024, 16, 64
NC = 8
TOK = 2048
NT = TOK // 128
NBLK = 16
NQB = 16
QBW = 512
CKEEP = 30.0
EXPSHIFT = -30.0
DEBUG = bool(int(os.environ.get("MK_DEBUG", "0")))
MIDDUMPS = bool(int(os.environ.get("MK_MIDDUMPS", "0")))
PHASES = int(os.environ.get("MK_PHASES", "4"))
NOCC = bool(int(os.environ.get("MK_NOCC", "0")))
NOQAUG = bool(int(os.environ.get("MK_NOQAUG", "0")))
NOLN = bool(int(os.environ.get("MK_NOLN", "0")))
NOKQ = bool(int(os.environ.get("MK_NOKQ", "0")))


def alibi_slopes(n_heads=H):
    start = 2.0 ** (-2.0 ** (-(math.log2(n_heads) - 3)))
    return np.array([start * start ** i for i in range(n_heads)], dtype=np.float32)


def head_slots():
    sl = alibi_slopes()
    ranks = list(np.argsort(sl))
    slots = [ranks[4 * s: 4 * s + 4] for s in range(4)]
    Ls = [int(math.ceil(CKEEP / float(sl[slots[s][0]]))) for s in range(4)]
    return slots, Ls, sl


def chunk_sched(L):
    out = []
    for qb in range(NQB):
        qb0 = QBW * qb
        jmax = (qb0 + QBW - 1) // 128
        jmin = max(0, int(math.ceil((qb0 - L - 127) / 128.0)))
        out.append(list(range(jmin, jmax + 1)))
    return out


class Sem:
    __slots__ = ("h", "n", "name")

    def __init__(self, h, name):
        self.h = h
        self.n = 0
        self.name = name


class Plan:
    def __init__(self):
        self.q = {e: [] for e in ENGINES}

    def em(self, eng, fn):
        self.q[eng].append(fn)

    def w(self, eng, sem, val=None):
        v = sem.n if val is None else val
        if v > 0:
            self.em(eng, lambda e, s=sem.h, vv=v: e.wait_ge(s, vv))

    def op(self, eng, fn, sem=None, k=1):
        if sem is not None:
            sem.n += k
            self.em(eng, lambda e, f=fn, s=sem.h, kk=k: f(e).then_inc(s, kk))
        else:
            self.em(eng, fn)


def build_nc():
    slots, Ls, sl = head_slots()
    sched = [chunk_sched(Ls[s]) for s in range(4)]

    nc = bass.Bass("TRN2", target_bir_lowering=False, debug=False)

    def ein(name, shape, dt=F32):
        return nc.dram_tensor(name, shape, dt, kind="ExternalInput").ap()

    x_own = ein("x_own", [TOK, D])
    wqk_in = ein("wqk", [4 * D, 128])
    wv_in = ein("wv", [2 * D, 128])
    wp_in = ein("wpt", [2 * D, D])
    wfr_in = ein("wfr", [D, D])
    wfi_in = ein("wfi", [D, D])
    bias_in = ein("fbias", [1, D])
    mtab_s = ein("mtab_s", [D, D])
    mtab_c = ein("mtab_c", [D, D])
    mtabc_s = ein("mtabc_s", [D, 128])
    mtabc_c = ein("mtabc_c", [D, 128])
    kq_in = ein("kq128", [128, 512])
    sel_in = ein("seltab", [32, 2048])
    ones_in = ein("ones32", [1, T])

    out_own = nc.dram_tensor("out_own", [TOK, D], F32, kind="ExternalOutput").ap()

    xnc = nc.dram_tensor("xnc", [D, TOK], BF16)
    xng = nc.dram_tensor("xng", [8 * D, TOK], BF16)
    mpc = nc.dram_tensor("mpc", [128, D], BF16)
    mg = nc.dram_tensor("mg", [D, D], BF16)
    kbq = nc.dram_tensor("kbq", [24, T], BF16)
    qaug = nc.dram_tensor("qaug", [4 * 70, T], BF16)
    gdr = nc.dram_tensor("gdr", [TOK, D], BF16)
    y_in = nc.dram_tensor("y_in", [2048, TOK], BF16)
    y_out = nc.dram_tensor("y_out", [2048, TOK], BF16)
    d_in = nc.dram_tensor("d_in", [32, TOK], F32)
    d_out = nc.dram_tensor("d_out", [32, TOK], F32)

    dbg = {}
    if DEBUG:
        for nm, src in (("dbg_xnc", [D, TOK]), ("dbg_mg", [D, D]),
                        ("dbg_q", [4 * 70, T]), ("dbg_g", [TOK, D]),
                        ("dbg_y", [2048, TOK])):
            dbg[nm] = nc.dram_tensor(nm, src, BF16, kind="ExternalOutput").ap()
        dbg["dbg_d"] = nc.dram_tensor("dbg_d", [32, TOK], F32, kind="ExternalOutput").ap()
        dbg["dbg_mpc"] = nc.dram_tensor("dbg_mpc", [128, D], BF16, kind="ExternalOutput").ap()
        dbg["dbg_ka"] = nc.dram_tensor("dbg_ka", [70, T], BF16, kind="ExternalOutput").ap()
        dbg["dbg_yin"] = nc.dram_tensor("dbg_yin", [2048, TOK], BF16, kind="ExternalOutput").ap()
        dbg["dbg_bsb"] = nc.dram_tensor("dbg_bsb", [128, 2048], BF16, kind="ExternalOutput").ap()
        dbg["dbg_wr"] = nc.dram_tensor("dbg_wr", [128, 8 * D], BF16, kind="ExternalOutput").ap()
        dbg["dbg_ccol"] = nc.dram_tensor("dbg_ccol", [128, 2 * 1024], BF16, kind="ExternalOutput").ap()
        dbg["dbg_csb"] = nc.dram_tensor("dbg_csb", [128, 8 * D], BF16, kind="ExternalOutput").ap()
        dbg["dbg_din"] = nc.dram_tensor("dbg_din", [32, TOK], F32, kind="ExternalOutput").ap()
        dbg["dbg_wr2"] = nc.dram_tensor("dbg_wr2", [128, 8 * D], BF16, kind="ExternalOutput").ap()
        dbg["dbg_ka3"] = nc.dram_tensor("dbg_ka3", [70, T], BF16, kind="ExternalOutput").ap()
        dbg["dbg_kbq"] = nc.dram_tensor("dbg_kbq", [24, T], BF16, kind="ExternalOutput").ap()
        dbg["dbg_wqk"] = nc.dram_tensor("dbg_wqk", [128, 4 * 1024], BF16, kind="ExternalOutput").ap()
        dbg["dbg_wv"] = nc.dram_tensor("dbg_wv", [128, 2 * 1024], BF16, kind="ExternalOutput").ap()
        dbg["dbg_p3"] = nc.dram_tensor("dbg_p3", [128, 1536], BF16, kind="ExternalOutput").ap()
        dbg["dbg_q0"] = nc.dram_tensor("dbg_q0", [70, 512], BF16, kind="ExternalOutput").ap()

    pl = Plan()
    es = contextlib.ExitStack()

    def mksem(name):
        return Sem(es.enter_context(nc.semaphore(name)), name)

    with es:
        s_ld = mksem("s_ld")
        s_x = mksem("s_x")
        s_xn = mksem("s_xn")
        s_m = mksem("s_m")
        s_xs = mksem("s_xs")
        s_q = mksem("s_q")
        s_q3 = mksem("s_q3")
        s_sp = mksem("s_sp")
        s_out = mksem("s_out")
        c_cc = mksem("c_cc")
        v_g = mksem("v_g")
        g_gp = mksem("g_gp")
        s_spy = mksem("s_spy")
        s_wld = mksem("s_wld")
        v_ev = mksem("v_ev")
        a_m = mksem("a_m")
        t_m = mksem("t_m")
        t_qk = mksem("t_qk")
        a_e = mksem("a_e")
        t_pv = mksem("t_pv")

        ident = es.enter_context(nc.sbuf_tensor("ident", [128, 128], BF16))
        maskt = es.enter_context(nc.sbuf_tensor("maskt", [128, 4 * 512], F32))
        mask01 = es.enter_context(nc.sbuf_tensor("mask01", [128, 4 * 512], BF16))
        shv = es.enter_context(nc.sbuf_tensor("shv", [128, 1], F32))
        epsv = es.enter_context(nc.sbuf_tensor("epsv", [128, 1], F32))
        ones1 = es.enter_context(nc.sbuf_tensor("ones1", [1, 128], BF16))
        vscr = es.enter_context(nc.sbuf_tensor("vscr", [1, 16], F32))
        sscr = es.enter_context(nc.sbuf_tensor("sscr", [1, 16], F32))
        gscr = es.enter_context(nc.sbuf_tensor("gscr", [1, 16], F32))

        def vfl(sem):
            pl.op("vector", lambda e: e.tensor_copy(vscr[:, 0:8], vscr[:, 8:16]), sem=sem)

        def sfl(sem):
            pl.op("scalar", lambda e: e.copy(sscr[:, 0:8], sscr[:, 8:16]), sem=sem)

        def tfl(ap, sem):
            pl.op("tensor", lambda e, o=ap: e.matmul(
                o, ident[:, 0:128], ident[:, 0:128], start=True, stop=True), sem=sem)

        def g_consts(g):
            g.memset(ident[:, :], 0.0)
            g.affine_select(out=ident[:, :], in_=ident[:, :],
                            compare_op=ALU.not_equal, fill=1.0, base=0,
                            pattern=[[-1, 128]], channel_multiplier=1)
            g.memset(maskt[:, :], 0.0)
            for v in range(4):
                g.affine_select(out=maskt[:, 512 * v:512 * (v + 1)],
                                in_=maskt[:, 512 * v:512 * (v + 1)],
                                compare_op=ALU.is_ge, fill=-100000.0,
                                base=-128 * v,
                                pattern=[[1, 512]], channel_multiplier=-1)
            g.memset(mask01[:, :], 3.3895e38)
            for v in range(4):
                g.affine_select(out=mask01[:, 512 * v:512 * (v + 1)],
                                in_=mask01[:, 512 * v:512 * (v + 1)],
                                compare_op=ALU.is_ge, fill=0.0,
                                base=-128 * v,
                                pattern=[[1, 512]], channel_multiplier=-1)
            g.memset(shv[:, :], EXPSHIFT)
            g.memset(epsv[:, :], 1e-5)
            return g.memset(ones1[:, :], 1.0)
        pl.op("gpsimd", g_consts)
        pl.op("gpsimd", lambda e: e.memset(gscr[:, :], 0.0), sem=g_gp)
        GCONST = g_gp.n

        esA = contextlib.ExitStack()
        with esA:
            def sbA(name, shape, dt):
                return esA.enter_context(nc.sbuf_tensor(name, shape, dt))

            xt = [sbA(f"xt{i}", [128, D], F32) for i in range(2)]
            sqt = sbA("sqt", [128, D], F32)
            xnb = [sbA(f"xnb{i}", [128, D], BF16) for i in range(2)]
            st1 = sbA("st1", [128, NT], F32)
            st2 = sbA("st2", [128, NT], F32)
            nmv = sbA("nmv", [128, NT], F32)
            m2v = sbA("m2v", [128, NT], F32)
            vav = sbA("vav", [128, NT], F32)
            sdv = sbA("sdv", [128, NT], F32)
            rsv = sbA("rsv", [128, NT], F32)
            stg = [sbA(f"stg{c}", [128, TOK], BF16) for c in range(8)]
            tp = [esA.enter_context(nc.psum_tensor(f"tp{i}", [128, D], BF16))
                  for i in range(2)]
            psA = esA.enter_context(nc.psum_tensor("psA", [128, 2048], F32))
            psB = esA.enter_context(nc.psum_tensor("psB", [128, 128], F32))
            ptt = esA.enter_context(nc.psum_tensor("ptt", [128, 128], BF16))

            a_sq = mksem("a_sq")
            v_xn = mksem("v_xn")
            t_tr = mksem("t_tr")
            v_st = mksem("v_st")

            kq_sb = sbA("kq_sb", [128, 512], F32)
            pl.op("sync", lambda e: e.dma_start(out=kq_sb[:, :], in_=kq_in[:, :]),
                  sem=s_ld, k=16)
            KQLD = s_ld.n

            for t in range(NT if not NOLN else 0):
                b = t % 2
                if t >= 2:
                    pl.w("gpsimd", v_xn, t - 1)
                    pl.w("gpsimd", a_sq, t - 1)
                pl.op("gpsimd", lambda e, o=xt[b][:, :], i=x_own[128 * t:128 * (t + 1), :]:
                      e.dma_start(out=o, in_=i), sem=s_x, k=16)
                pl.w("scalar", s_x, 16 * (t + 1))
                if t >= 1:
                    pl.w("scalar", v_g)
                pl.op("scalar", lambda e, i=xt[b][:, :]:
                      e.activation(sqt[:, :], i, AF.Square))
                sfl(a_sq)
                pl.w("vector", s_x, 16 * (t + 1))
                pl.op("vector", lambda e, o=st1[:, t:t + 1], i=xt[b][:, :]:
                      e.reduce_sum(o, i, axis=AX.X))
                pl.w("vector", a_sq, t + 1)
                pl.op("vector", lambda e, o=st2[:, t:t + 1]:
                      e.reduce_sum(o, sqt[:, :], axis=AX.X), sem=v_g)
                pl.op("vector", lambda e, o=nmv[:, t:t + 1], i=st1[:, t:t + 1]:
                      e.tensor_scalar(o, i, -1.0 / D, None, ALU.mult), sem=v_g)
                pl.w("vector", v_g)
                pl.op("vector", lambda e, o=m2v[:, t:t + 1], i=nmv[:, t:t + 1]:
                      e.tensor_tensor(o, i, i, ALU.mult))
                pl.op("vector", lambda e, o=vav[:, t:t + 1], i=st2[:, t:t + 1]:
                      e.tensor_scalar(o, i, 1.0 / D, None, ALU.mult), sem=v_g)
                pl.w("vector", v_g)
                pl.op("vector", lambda e, o=vav[:, t:t + 1], i=m2v[:, t:t + 1]:
                      e.tensor_tensor(o, o, i, ALU.subtract), sem=v_g)
                pl.w("scalar", v_g)
                pl.op("scalar", lambda e, o=sdv[:, t:t + 1], i=vav[:, t:t + 1]:
                      e.activation(o, i, AF.Sqrt, bias=epsv[:, 0:1]))
                sfl(a_m)
                pl.w("vector", a_m)
                pl.op("vector", lambda e, o=rsv[:, t:t + 1], i=sdv[:, t:t + 1]:
                      e.reciprocal(o, i), sem=v_g)
                pl.w("vector", v_g)
                if t >= 2:
                    pl.w("vector", t_tr, t - 1)
                pl.op("vector", lambda e, o=xnb[b][:, :], i=xt[b][:, :],
                      s1=nmv[:, t:t + 1], s2=rsv[:, t:t + 1]:
                      e.tensor_scalar(o, i, s1, s2, ALU.add, ALU.mult))
                vfl(v_xn)
                pl.w("tensor", v_xn, t + 1)
                pl.w("tensor", g_gp, GCONST)
                if t >= 2:
                    pl.w("tensor", v_st, t - 1)
                for c in range(8):
                    pl.op("tensor", lambda e, o=tp[b][:, 128 * c:128 * (c + 1)],
                          i=xnb[b][:, 128 * c:128 * (c + 1)]:
                          e.transpose(o, i, ident[:, :]))
                tfl(psA[0:128, 0:128], t_tr)
                pl.w("vector", t_tr, t + 1)
                for c in range(8):
                    pl.op("vector", lambda e, o=stg[c][:, 128 * t:128 * (t + 1)],
                          i=tp[b][:, 128 * c:128 * (c + 1)]:
                          e.tensor_copy(o, i))
                vfl(v_st)

            pl.w("sync", v_st, v_st.n)
            for c in range(8 if not NOLN else 0):
                pl.op("sync", lambda e, o=xnc.ap()[128 * c:128 * (c + 1), :], i=stg[c][:, :]:
                      e.dma_start(out=o, in_=i), sem=s_xn, k=16)
            wr_sb = sbA("wr_sb", [128, 8 * D], BF16)
            wi_sb = sbA("wi_sb", [128, 8 * D], BF16)
            c_sb = sbA("c_sb", [128, 8 * D], BF16)
            ss_sb = sbA("ss_sb", [128, 8 * D], BF16)
            ccol = sbA("ccol", [128, 8 * 128], BF16)
            scol = sbA("scol", [128, 8 * 128], BF16)
            wtmp = [sbA(f"wtmp{i}", [128, D], F32) for i in range(2)]
            btmp = [sbA(f"btmp{i}", [128, 128], F32) for i in range(2)]
            b_sb = sbA("b_sb", [128, 8 * 256], BF16)
            sintmp = sbA("sintmp", [128, D], F32)
            mtp = sbA("mtp", [128, 128], BF16)
            mstage = sbA("mstage", [128, D], BF16)
            s_mw = mksem("s_mw")
            a_mm = mksem("a_mm")
            v_mw = mksem("v_mw")

            SC = 2.0 * math.pi / 1024.0
            wload = []
            for c in range(8):
                wload.append((wfr_in[128 * c:128 * (c + 1), :], wr_sb, D * c, "v"))
                wload.append((mtabc_c[128 * c:128 * (c + 1), :], ccol, 128 * c, "a"))
            for c in range(8):
                wload.append((wfi_in[128 * c:128 * (c + 1), :], wi_sb, D * c, "v"))
                wload.append((mtabc_s[128 * c:128 * (c + 1), :], scol, 128 * c, "a"))
            for c in range(8):
                wload.append((mtab_c[128 * c:128 * (c + 1), :], c_sb, D * c, "a"))
                wload.append((mtab_s[128 * c:128 * (c + 1), :], ss_sb, D * c, "a"))
            pl.w("sync", v_st, v_st.n)
            pl.w("sync", s_xn, s_xn.n)
            kqh = sbA("kqh", [128, 512], BF16)
            kqm = sbA("kqm", [128, 512], BF16)
            kql = sbA("kql", [128, 512], BF16)
            kqr = sbA("kqr", [128, 512], F32)
            kqr2 = sbA("kqr2", [128, 512], F32)
            pl.w("vector", s_ld, KQLD)
            for opf in () if NOKQ else (
                lambda e: e.tensor_copy(kqh[:, :], kq_sb[:, :]),
                lambda e: e.tensor_tensor(kqr[:, :], kq_sb[:, :], kqh[:, :], ALU.subtract),
                lambda e: e.tensor_copy(kqm[:, :], kqr[:, :]),
                lambda e: e.tensor_tensor(kqr2[:, :], kqr[:, :], kqm[:, :], ALU.subtract),
                lambda e: e.tensor_copy(kql[:, :], kqr2[:, :]),
            ):
                pl.op("vector", opf, sem=v_g)
                pl.w("vector", v_g)
            vfl(v_g)
            KQ_DONE = v_g.n
            pl.w("sync", v_g, KQ_DONE)
            for blk in range(8 if not NOKQ else 0):
                for i, src in enumerate((kqh, kqm, kql)):
                    pl.op("sync", lambda e,
                          o=kbq.ap()[3 * blk + i:3 * blk + i + 1, :]
                          .rearrange("o (p j) -> (o p) j", p=128),
                          s=src[:, 64 * blk:64 * (blk + 1)]:
                          e.dma_start(out=o, in_=s), sem=s_spy, k=16)
            KBQ_DMAS = s_spy.n
            pl.w("gpsimd", s_spy, KBQ_DMAS)
            if not (NOQAUG or NOKQ):
                for s in range(4):
                    for i in range(3):
                        pl.op("gpsimd", lambda e, o=qaug.ap()[70 * s + 64 + i:70 * s + 65 + i, :],
                              i2=ones_in[0:1, :]:
                              e.dma_start(out=o, in_=i2), sem=s_sp, k=16)
                    for i in range(3):
                        pl.op("gpsimd", lambda e, o=qaug.ap()[70 * s + 67 + i:70 * s + 68 + i, :],
                              i2=kbq.ap()[3 * (4 + s) + i:3 * (4 + s) + i + 1, :]:
                              e.dma_start(out=o, in_=i2), sem=s_sp, k=16)
            QAUG_ONES = s_sp.n
            slot_guard = [None, None]
            for li, (src, dst, off, engk) in enumerate(wload):
                bb = li % 2
                wide = dst in (wr_sb, wi_sb, c_sb, ss_sb)
                buf = wtmp[bb] if wide else btmp[bb]
                if engk == "v":
                    pl.op("gpsimd", lambda e, o=dst[:, off:off + D], i=src:
                          e.dma_start(out=o, in_=i), sem=s_wld, k=16)
                    continue
                g = slot_guard[bb]
                if g is not None:
                    pl.w("sync", g[0], g[1])
                pl.op("sync", lambda e, o=buf[:, 0:(D if wide else 128)], i=src:
                      e.dma_start(out=o, in_=i), sem=s_mw, k=16)
                if True:
                    pl.w("scalar", s_mw, s_mw.n)
                    pl.op("scalar", lambda e, o=sintmp[:, 0:(D if wide else 128)],
                          i=buf[:, 0:(D if wide else 128)]:
                          e.activation(o, i, AF.Sin, scale=SC))
                    sfl(a_mm)
                    slot_guard[bb] = (a_mm, a_mm.n)
                    pl.w("vector", a_mm, a_mm.n)
                    pl.op("vector", lambda e, o=dst[:, off:off + (D if wide else 128)],
                          i=sintmp[:, 0:(D if wide else 128)]:
                          e.tensor_copy(o, i))
                    vfl(v_mw)
                    pl.w("scalar", v_mw, v_mw.n)
            W_CASTS_V = v_mw.n
            W_CASTS_A = a_mm.n

            for sm_ in (s_x, s_ld, s_xn, s_m, s_mw, s_wld, s_sp, s_spy):
                pl.w("gpsimd", sm_, sm_.n)
            if not NOCC:
                pl.op("gpsimd", lambda e: e.collective_compute(
                    "AllGather", ALU.bypass,
                    replica_groups=[[0, 1, 2, 3], [4, 5, 6, 7]],
                    ins=[xnc.ap().opt()], outs=[xng.ap()[0:4 * D, :].opt()]), sem=c_cc)
            CC_XN = c_cc.n

            pl.w("tensor", v_mw, W_CASTS_V)
            pl.w("tensor", s_wld, s_wld.n)
            pl.w("tensor", a_mm, W_CASTS_A)
            evac_guard = 0
            for oc in range(8):
                if oc >= 1:
                    pl.w("tensor", v_g, evac_guard)
                for wi_, wsb in ((0, wr_sb), (1, wi_sb)):
                    for ci, csb in ((0, ccol), (1, scol)):
                        reg = 512 * (2 * wi_ + ci)
                        for kc in range(8):
                            pl.op("tensor", lambda e,
                                  o=psA[:, reg:reg + 128],
                                  l=wsb[:, D * kc + 128 * oc: D * kc + 128 * (oc + 1)],
                                  r=csb[:, 128 * kc:128 * (kc + 1)],
                                  stt=(kc == 0), sp=(kc == 7):
                                  e.matmul(o, l, r, start=stt, stop=sp))
                tfl(psB[0:128, 0:128], t_m)
                pl.w("scalar", t_m)
                pl.op("scalar", lambda e, o=btmp[0][:, :], i=psA[:, 1536:1664]:
                      e.activation(o, i, AF.Copy, scale=1.0 / 1024.0))
                pl.op("scalar", lambda e, o=btmp[1][:, :], i=psA[:, 1024:1152]:
                      e.activation(o, i, AF.Copy, scale=-1.0 / 1024.0))
                sfl(a_mm)
                pl.w("vector", a_mm)
                pl.op("vector", lambda e, o=b_sb[:, 256 * oc:256 * oc + 128],
                      i=psA[:, 0:128], tt=btmp[0][:, :]:
                      e.scalar_tensor_tensor(o, i, 1.0 / 1024.0, tt, ALU.mult, ALU.add))
                pl.op("vector", lambda e, o=b_sb[:, 256 * oc + 128:256 * oc + 256],
                      i=psA[:, 512:640], tt=btmp[1][:, :]:
                      e.scalar_tensor_tensor(o, i, 1.0 / 1024.0, tt, ALU.mult, ALU.add))
                vfl(v_g)
                evac_guard = v_g.n
            B_DONE = v_g.n

            mw_guard = v_mw.n
            for oc in range(8):
                pl.w("tensor", v_g, B_DONE)
                if oc >= 1:
                    pl.w("tensor", v_mw, mw_guard)
                for cc in range(8):
                    pl.op("tensor", lambda e, o=psB[:, :],
                          l=c_sb[:, D * cc + 128 * oc: D * cc + 128 * (oc + 1)],
                          r=b_sb[:, 256 * cc:256 * cc + 128], stt=(cc == 0):
                          e.matmul(o, l, r, start=stt, stop=False))
                for cc in range(8):
                    pl.op("tensor", lambda e, o=psB[:, :],
                          l=ss_sb[:, D * cc + 128 * oc: D * cc + 128 * (oc + 1)],
                          r=b_sb[:, 256 * cc + 128:256 * cc + 256], sp=(cc == 7):
                          e.matmul(o, l, r, start=False, stop=sp))
                tfl(psA[0:128, 0:128], t_m)
                pl.w("vector", t_m)
                pl.op("vector", lambda e, o=mtp[:, :], i=psB[:, :]:
                      e.tensor_copy(o, i))
                vfl(v_g)
                pl.w("tensor", v_g)
                pl.op("tensor", lambda e, o=ptt[:, :], i=mtp[:, :]:
                      e.transpose(o, i, ident[:, :]))
                tfl(psA[0:128, 0:128], t_m)
                pl.w("vector", t_m)
                pl.op("vector", lambda e, o=mstage[:, 128 * oc:128 * (oc + 1)], i=ptt[:, :]:
                      e.tensor_copy(o, i))
                vfl(v_mw)
                mw_guard = v_mw.n
            pl.w("sync", v_mw, v_mw.n)
            pl.w("sync", c_cc, CC_XN)
            pl.op("sync", lambda e: e.dma_start(out=mpc.ap()[:, :], in_=mstage[:, :]),
                  sem=s_m, k=16)
            if DEBUG and MIDDUMPS:
                pl.op("sync", lambda e: e.dma_start(out=dbg["dbg_bsb"][:, :], in_=b_sb[:, :]),
                      sem=s_m, k=16)
                pl.op("sync", lambda e: e.dma_start(out=dbg["dbg_wr"][:, :], in_=wr_sb[:, :]),
                      sem=s_m, k=16)
                pl.op("sync", lambda e: e.dma_start(out=dbg["dbg_ccol"][:, 0:1024], in_=ccol[:, :]),
                      sem=s_m, k=16)
                pl.op("sync", lambda e: e.dma_start(out=dbg["dbg_ccol"][:, 1024:2048], in_=scol[:, :]),
                      sem=s_m, k=16)
                pl.op("sync", lambda e: e.dma_start(out=dbg["dbg_csb"][:, :], in_=c_sb[:, :]),
                      sem=s_m, k=16)
            pl.w("gpsimd", c_cc, CC_XN)
            for sm_ in (s_m, s_spy, s_sp, s_ld):
                pl.w("gpsimd", sm_, sm_.n)
            if not NOCC:
                pl.op("gpsimd", lambda e: e.collective_compute(
                    "AllGather", ALU.bypass, replica_groups=[list(range(8))],
                    ins=[mpc.ap().opt()], outs=[mg.ap().opt()]), sem=c_cc)
            CC_M = c_cc.n
            pl.w("gpsimd", c_cc, c_cc.n)
            if DEBUG and PHASES < 2:
                pl.op("gpsimd", lambda e: e.dma_start(out=dbg["dbg_wr2"][:, :], in_=wr_sb[:, :]),
                      sem=s_out, k=16)
                pl.w("gpsimd", s_out, s_out.n)

        PA_SM = s_m.n
        PA_SXN = s_xn.n
        PA_SSP = s_sp.n
        PA_SSPY = s_spy.n
        if PHASES < 2:
            pl.w("sync", s_sp, s_sp.n)
            pl.w("sync", s_m, s_m.n)
            pl.w("sync", c_cc, c_cc.n)
            if DEBUG:
                for src, dst in ((xnc, dbg["dbg_xnc"]), (mg, dbg["dbg_mg"]),
                                 (qaug, dbg["dbg_q"]), (mpc, dbg["dbg_mpc"])):
                    pl.op("gpsimd", lambda e, o=dst[:, :], i=src.ap()[:, :]:
                          e.dma_start(out=o, in_=i), sem=s_out, k=16)
            pl.w("gpsimd", s_out, s_out.n)
            return _emit(nc, pl)
        wp_sb = es.enter_context(nc.sbuf_tensor("wp_sb", [128, 16 * D], BF16))
        wqk_sb = es.enter_context(nc.sbuf_tensor("wqk_sb", [128, 4 * 1024], BF16))
        wv_sb = es.enter_context(nc.sbuf_tensor("wv_sb", [128, 2 * 1024], BF16))
        brow = es.enter_context(nc.sbuf_tensor("brow", [1, D], BF16))

        def wload2(src_rows, dst, dst_off):
            pl.op("gpsimd", lambda e,
                  o=dst[:, dst_off:dst_off + D].rearrange("p (c m) -> p c m", m=128),
                  i=src_rows.rearrange("(c p) m -> p c m", p=128):
                  e.dma_start(out=o, in_=i), sem=s_wld, k=16)

        for s in range(4):
            wload2(wqk_in[D * s:D * (s + 1), :], wqk_sb, 1024 * s)
        for p in range(2):
            wload2(wv_in[D * p:D * (p + 1), :], wv_sb, 1024 * p)
        for c in range(16):
            pl.op("gpsimd", lambda e, o=wp_sb[:, D * c:D * (c + 1)],
                  i=wp_in[128 * c:128 * (c + 1), :]:
                  e.dma_start(out=o, in_=i), sem=s_wld, k=16)
        pl.op("gpsimd", lambda e: e.dma_start(out=brow[:, :], in_=bias_in[:, :]),
              sem=s_wld, k=16)
        WEIGHTS_OK_W = s_wld.n
        esBC = contextlib.ExitStack()
        kaug = [esBC.enter_context(nc.sbuf_tensor(f"kaug{s}", [70, T], BF16))
                for s in range(4)]
        vaug = [esBC.enter_context(nc.sbuf_tensor(f"vaug{s}", [128, 64 * 65], BF16))
                for s in range(4)]

        esB = contextlib.ExitStack()
        with esB:
            def sbB(name, shape, dt):
                return esB.enter_context(nc.sbuf_tensor(name, shape, dt))

            xs = [sbB(f"xs{i}", [128, 8 * 512], BF16) for i in range(2)]
            m_sb = sbB("m_sb", [128, 8 * D], BF16)
            qstage = [sbB(f"qstage{i}", [64, 512], BF16) for i in range(2)]
            gtile = [sbB(f"gtile{i}", [128, D], BF16) for i in range(2)]
            psQK = [esB.enter_context(nc.psum_tensor(f"psQK{i}", [128, 512], F32))
                    for i in range(2)]
            psV = [esB.enter_context(nc.psum_tensor(f"psV{i}", [128, 128], F32))
                   for i in range(2)]
            psDum = esB.enter_context(nc.psum_tensor("psDum", [128, 128], F32))

            pl.w("gpsimd", c_cc, CC_M)
            pl.w("gpsimd", s_m, PA_SM)
            pl.w("gpsimd", s_xn, PA_SXN)
            pl.w("gpsimd", s_sp, PA_SSP)
            pl.w("gpsimd", s_spy, PA_SSPY)
            for s in range(4):
                pl.op("gpsimd", lambda e, o=kaug[s][64:70, :]:
                      e.memset(o, 1.0), sem=g_gp)
            for s in range(4):
                for i in range(3):
                    pl.op("gpsimd", lambda e, o=kaug[s][64 + i:65 + i, :],
                          i2=kbq.ap()[3 * s + i:3 * s + i + 1, :]:
                          e.dma_start(out=o, in_=i2), sem=s_sp, k=16)
            KB_ROWS = s_sp.n

            pl.w("sync", c_cc, CC_M)
            pl.w("sync", s_m, PA_SM)
            pl.w("sync", s_xn, PA_SXN)
            pl.w("sync", s_sp, PA_SSP)
            pl.w("sync", s_spy, PA_SSPY)

            pl.w("sync", c_cc, CC_M)
            for c in range(8):
                pl.op("sync", lambda e, o=m_sb[:, D * c:D * (c + 1)],
                      i=mg.ap()[128 * c:128 * (c + 1), :]:
                      e.dma_start(out=o, in_=i), sem=s_m, k=16)
            MSB_OK = s_m.n

            for s in range(4):
                pl.op("gpsimd", lambda e, o=vaug[s][:, :]: e.memset(o, 1.0), sem=g_gp)
            VONES = g_gp.n

            pl.w("sync", c_cc, CC_XN)
            if DEBUG and MIDDUMPS:
                pl.w("sync", v_g, WEIGHTS_OK)
                pl.op("sync", lambda e: e.dma_start(out=dbg["dbg_wqk"][:, :], in_=wqk_sb[:, :]),
                      sem=s_ld, k=16)
                pl.op("sync", lambda e: e.dma_start(out=dbg["dbg_wv"][:, :], in_=wv_sb[:, :]),
                      sem=s_ld, k=16)
                pl.w("sync", s_ld, s_ld.n)
            pl.w("tensor", s_wld, WEIGHTS_OK_W)
            pl.w("tensor", g_gp, VONES)
            pl.w("tensor", s_sp, KB_ROWS)

            psqk_guard = [0, 0]
            psv_guard = [0, 0]
            qstage_guard = [0, 0]
            blk_ev = {}
            for blk in range(16):
                bb = blk % 2
                rb, qt = blk // 4, blk % 4
                if blk >= 2:
                    pl.w("sync", v_ev, blk_ev[blk - 2])
                for c in range(8):
                    pl.op("sync", lambda e, o=xs[bb][:, 512 * c:512 * (c + 1)],
                          i=xng.ap()[1024 * rb + 128 * c:1024 * rb + 128 * (c + 1),
                                     512 * qt:512 * (qt + 1)]:
                          e.dma_start(out=o, in_=i), sem=s_xs, k=16)
                pl.w("tensor", s_xs, s_xs.n)
                vgroups = [(p, sub) for p in range(2) for sub in range(4)]
                for s in range(4):
                    pb = s % 2
                    pl.w("tensor", v_ev, psqk_guard[pb])
                    for c in range(8):
                        pl.op("tensor", lambda e, o=psQK[pb][:, :],
                              l=wqk_sb[:, 1024 * s + 128 * c:1024 * s + 128 * (c + 1)],
                              r=xs[bb][:, 512 * c:512 * (c + 1)],
                              stt=(c == 0), sp=(c == 7):
                              e.matmul(o, l, r, start=stt, stop=sp))
                    tfl(psDum[:, :], t_m)
                    qkd = t_m.n
                    for p, sub in vgroups[2 * s:2 * s + 2]:
                        vb = (p * 4 + sub) % 2
                        jj = 4 * blk + sub
                        pl.w("tensor", v_ev, psv_guard[vb])
                        for c in range(8):
                            pl.op("tensor", lambda e, o=psV[vb][:, :],
                                  l=xs[bb][:, 512 * c + 128 * sub:512 * c + 128 * (sub + 1)],
                                  r=wv_sb[:, 1024 * p + 128 * c:1024 * p + 128 * (c + 1)],
                                  stt=(c == 0), sp=(c == 7):
                                  e.matmul(o, l, r, start=stt, stop=sp))
                        tfl(psDum[:, :], t_m)
                        vd = t_m.n
                        pl.w("vector", t_m, vd)
                        pl.op("vector", lambda e,
                              o=vaug[2 * p][:, 65 * jj:65 * jj + 64],
                              i=psV[vb][:, 0:64]:
                              e.tensor_copy(o, i))
                        pl.op("vector", lambda e,
                              o=vaug[2 * p + 1][:, 65 * jj:65 * jj + 64],
                              i=psV[vb][:, 64:128]:
                              e.tensor_copy(o, i))
                        vfl(v_ev)
                        psv_guard[vb] = v_ev.n
                    pl.w("vector", t_m, qkd)
                    pl.w("vector", s_q, qstage_guard[pb])
                    pl.op("vector", lambda e, o=qstage[pb][:, :], i=psQK[pb][0:64, :]:
                          e.tensor_copy(o, i))
                    pl.op("vector", lambda e,
                          o=kaug[s][0:64, 512 * blk:512 * (blk + 1)],
                          i=psQK[pb][64:128, :]:
                          e.tensor_copy(o, i))
                    vfl(v_ev)
                    psqk_guard[pb] = v_ev.n
                    pl.w("gpsimd", v_ev, v_ev.n)
                    pl.op("gpsimd", lambda e,
                          o=qaug.ap()[70 * s:70 * s + 64, 512 * blk:512 * (blk + 1)],
                          i=qstage[pb][:, :]:
                          e.dma_start(out=o, in_=i), sem=s_q, k=16)
                    qstage_guard[pb] = s_q.n
                blk_ev[blk] = v_ev.n
            vfl(v_ev)
            P2_T = t_m.n
            P2_EV = v_ev.n
            P2_Q = s_q.n

            pl.w("tensor", s_m, MSB_OK)
            pl.w("sync", v_ev, P2_EV)
            ffn_mm = {}
            ffn_silu = {}
            ffn_spill = {}
            for fb in range(4):
                bb = fb % 2
                if fb >= 2:
                    pl.w("sync", t_m, ffn_mm[fb - 2])
                for c in range(8):
                    pl.op("sync", lambda e, o=xs[bb][:, 512 * c:512 * (c + 1)],
                          i=xnc.ap()[128 * c:128 * (c + 1), 512 * fb:512 * (fb + 1)]:
                          e.dma_start(out=o, in_=i), sem=s_xs, k=16)
                pl.w("tensor", s_xs, s_xs.n)
                for sub in range(4):
                    tt_ = 4 * fb + sub
                    gb = tt_ % 2
                    pl.w("tensor", a_m, ffn_silu.get(tt_ - 1, 0))
                    for half in range(2):
                        for c in range(8):
                            pl.op("tensor", lambda e, o=psQK[half][:, :],
                                  l=xs[bb][:, 512 * c + 128 * sub:512 * c + 128 * (sub + 1)],
                                  r=m_sb[:, D * c + 512 * half:D * c + 512 * (half + 1)],
                                  stt=(c == 0):
                                  e.matmul(o, l, r, start=stt, stop=False))
                        pl.op("tensor", lambda e, o=psQK[half][:, :],
                              r=brow[0:1, 512 * half:512 * (half + 1)]:
                              e.matmul(o, ones1[0:1, :], r, start=False, stop=True))
                    tfl(psDum[:, :], t_m)
                    ffn_mm[fb] = t_m.n
                    pl.w("scalar", t_m, t_m.n)
                    pl.w("scalar", s_sp, ffn_spill.get(tt_ - 2, 0))
                    for half in range(2):
                        pl.op("scalar", lambda e, o=gtile[gb][:, 512 * half:512 * (half + 1)],
                              i=psQK[half][:, :]:
                              e.activation(o, i, AF.Silu))
                    sfl(a_m)
                    ffn_silu[tt_] = a_m.n
                    pl.w("gpsimd", a_m, ffn_silu[tt_])
                    pl.op("gpsimd", lambda e, o=gdr.ap()[128 * tt_:128 * (tt_ + 1), :],
                          i=gtile[gb][:, :]:
                          e.dma_start(out=o, in_=i), sem=s_sp, k=16)
                    ffn_spill[tt_] = s_sp.n
            FFN_SP = s_sp.n

        if PHASES < 3:
            pl.w("sync", s_sp, s_sp.n)
            pl.w("sync", s_q, s_q.n)
            if DEBUG:
                for src, dst in ((xnc, dbg["dbg_xnc"]), (mg, dbg["dbg_mg"]),
                                 (qaug, dbg["dbg_q"]), (gdr, dbg["dbg_g"]),
                                 (mpc, dbg["dbg_mpc"])):
                    pl.op("gpsimd", lambda e, o=dst[:, :], i=src.ap()[:, :]:
                          e.dma_start(out=o, in_=i), sem=s_out, k=16)
            pl.w("gpsimd", s_out, s_out.n)
            return _emit(nc, pl)
        esC = contextlib.ExitStack()
        with esC:
            def sbC(name, shape, dt):
                return esC.enter_context(nc.sbuf_tensor(name, shape, dt))

            qsb = [sbC(f"qsb{i}", [70, 512], BF16) for i in range(2)]
            p3sb = [sbC(f"p3sb{i}", [128, 1536], BF16) for i in range(2)]
            ystage = sbC("ystage", [64, 512], BF16)
            dstage = sbC("dstage", [1, 512], F32)
            s3 = [esC.enter_context(nc.psum_tensor(f"s3_{i}", [128, 1536], F32))
                  for i in range(2)]
            yps = [esC.enter_context(nc.psum_tensor(f"yps{i}", [128, 512], F32))
                   for i in range(2)]

            pl.w("tensor", s_q, P2_Q)
            pl.w("tensor", v_ev, P2_EV)
            pl.w("tensor", s_sp, QAUG_ONES)
            pl.w("tensor", s_sp, KB_ROWS)
            pl.w("gpsimd", s_q, P2_Q)
            pl.w("gpsimd", s_sp, QAUG_ONES)

            if DEBUG and MIDDUMPS:
                pl.op("gpsimd", lambda e: e.dma_start(out=dbg["dbg_ka"][:, :], in_=kaug[0][:, :]),
                      sem=s_sp, k=16)
                pl.op("gpsimd", lambda e: e.dma_start(out=dbg["dbg_ka3"][:, :], in_=kaug[3][:, :]),
                      sem=s_sp, k=16)
                pl.op("gpsimd", lambda e: e.dma_start(out=dbg["dbg_kbq"][:, :], in_=kbq.ap()[:, :]),
                      sem=s_sp, k=16)
                pl.w("gpsimd", s_sp, s_sp.n)
            pairs = [(s, qb) for s in range(4) for qb in range(NQB)]
            exp_hist = []
            pv_hist = []
            qload = {}
            qsb_guard = [0, 0]
            yev_guard = [0, 0]
            sp_guard = [0]
            tri_idx = 0
            pend = None
            flushq = None

            def emit_qload(idx):
                s_, qb_ = pairs[idx]
                qbb_ = qb_ % 2
                pl.w("gpsimd", t_qk, qsb_guard[qbb_])
                pl.op("gpsimd", lambda e, o=qsb[qbb_][:, :],
                      i=qaug.ap()[70 * s_:70 * (s_ + 1), 512 * qb_:512 * (qb_ + 1)]:
                      e.dma_start(out=o, in_=i), sem=s_q3, k=16)
                qload[idx] = s_q3.n

            def emit_pv(p):
                tri_, tb_, stt_, last_, exp_, sel_, s_, qbb_ = p
                pl.w("tensor", a_e, exp_)
                if sel_:
                    pl.w("tensor", v_g, sel_)
                tw_ = len(tri_)
                for i, j in enumerate(tri_):
                    pl.op("tensor", lambda e, o=yps[qbb_][0:65, :],
                          l=vaug[s_][:, 65 * j:65 * (j + 1)],
                          r=p3sb[tb_][:, 512 * i:512 * (i + 1)],
                          stt=(stt_ and i == 0),
                          sp=(last_ and i == tw_ - 1):
                          e.matmul(o, l, r, start=stt, stop=sp),
                          sem=(t_pv if i == tw_ - 1 else None))
                pv_hist.append(t_pv.n)

            def emit_flush(f):
                s_, qb_, qbb_, tbl_ = f
                tfl(s3[tbl_][0:128, 0:128], t_pv)
                PVD = t_pv.n
                pl.w("vector", t_pv, PVD)
                pl.w("vector", s_sp, sp_guard[0])
                pl.op("vector", lambda e, o=ystage[:, :], i=yps[qbb_][0:64, :]:
                      e.tensor_copy(o, i))
                pl.op("vector", lambda e, o=dstage[:, :], i=yps[qbb_][64:65, :]:
                      e.tensor_copy(o, i))
                vfl(v_ev)
                yev_guard[qbb_] = v_ev.n
                pl.w("gpsimd", v_ev, v_ev.n)
                co = 512 * (qb_ % 4)
                for dst in (qb_ // 4, qb_ // 4 + 4):
                    ro = 256 * dst + 64 * s_
                    pl.op("gpsimd", lambda e, o=y_in.ap()[ro:ro + 64, co:co + 512],
                          i=ystage[:, :]:
                          e.dma_start(out=o, in_=i), sem=s_sp, k=16)
                    dr = 4 * dst + s_
                    pl.op("gpsimd", lambda e, o=d_in.ap()[dr:dr + 1, co:co + 512],
                          i=dstage[:, :]:
                          e.dma_start(out=o, in_=i), sem=s_sp, k=16)
                sp_guard[0] = s_sp.n

            emit_qload(0)
            for idx, (s, qb) in enumerate(pairs):
                chunks = sched[s][qb]
                qbb = qb % 2
                ntri = (len(chunks) + 2) // 3
                first = True

                for ti in range(ntri):
                    tri = chunks[3 * ti:3 * ti + 3]
                    tb = tri_idx % 2
                    tw = len(tri)
                    if tri_idx >= 2:
                        pl.w("tensor", a_e, exp_hist[tri_idx - 2])
                    pl.w("tensor", s_q3, qload[idx])
                    if first:
                        pl.w("tensor", v_ev, yev_guard[qbb])
                    for i, j in enumerate(tri):
                        pl.op("tensor", lambda e, o=s3[tb][:, 512 * i:512 * (i + 1)],
                              l=kaug[s][:, 128 * j:128 * (j + 1)],
                              r=qsb[qbb][:, :]:
                              e.matmul(o, l, r, start=True, stop=True),
                              sem=(t_qk if i == tw - 1 else None))
                    QKD = t_qk.n
                    has_diag = any(j >= 4 * qb for j in tri)
                    pl.w("scalar", t_qk, QKD)
                    if tri_idx >= 2:
                        pl.w("scalar", t_pv, pv_hist[tri_idx - 2])
                    pl.op("scalar", lambda e, o=p3sb[tb][:, 0:512 * tw],
                          i=s3[tb][:, 0:512 * tw]:
                          e.activation(o, i, AF.Exp, bias=shv[:, 0:1], scale=0.125))
                    sfl(a_e)
                    exp_hist.append(a_e.n)
                    sel_done = 0
                    if has_diag:
                        pl.w("vector", a_e, exp_hist[-1])
                        for i, j in enumerate(tri):
                            if j >= 4 * qb:
                                v = j - 4 * qb
                                pl.op("vector", lambda e,
                                      o=p3sb[tb][:, 512 * i:512 * (i + 1)],
                                      m=mask01[:, 512 * v:512 * (v + 1)]:
                                      e.tensor_tensor(o, o, m, ALU.min))
                        vfl(v_g)
                        sel_done = v_g.n
                    if DEBUG and idx == 49 and ti == 0:
                        pl.w("gpsimd", a_e, a_e.n)
                        pl.op("gpsimd", lambda e, src=p3sb[tb]:
                              e.dma_start(out=dbg["dbg_p3"][:, :], in_=src[:, :]),
                              sem=s_sp, k=16)
                        pl.op("gpsimd", lambda e, src=qsb[qbb]:
                              e.dma_start(out=dbg["dbg_q0"][:, :], in_=src[:, :]),
                              sem=s_sp, k=16)
                        DBGP3 = s_sp.n
                        pl.w("scalar", s_sp, DBGP3)
                    if ti == 0 and idx + 1 < len(pairs):
                        emit_qload(idx + 1)
                    if pend is not None:
                        emit_pv(pend)
                    if flushq is not None:
                        emit_flush(flushq)
                        flushq = None
                    pend = (tri, tb, first, ti == ntri - 1, exp_hist[-1],
                            sel_done, s, qbb)
                    first = False
                    tri_idx += 1
                qsb_guard[qbb] = t_qk.n
                flushq = (s, qb, qbb, (tri_idx - 1) % 2)
            emit_pv(pend)
            emit_flush(flushq)
            P3_SP = s_sp.n
            for sm_ in (s_x, s_ld, s_xn, s_m, s_sp, s_spy, s_xs, s_q, s_q3, s_out):
                pl.w("gpsimd", sm_, sm_.n)
            with allow_ici_all_to_all():
                pl.op("gpsimd", lambda e: e.collective_compute(
                    "AllToAll", ALU.bypass,
                    replica_groups=[list(range(8))],
                    ins=[y_in.ap().opt()], outs=[y_out.ap().opt()]), sem=c_cc)
                pl.w("gpsimd", c_cc, c_cc.n)
                pl.op("gpsimd", lambda e: e.collective_compute(
                    "AllToAll", ALU.bypass,
                    replica_groups=[list(range(8))],
                    ins=[d_in.ap().opt()], outs=[d_out.ap().opt()]), sem=c_cc)
            CC_Y = c_cc.n
            pl.w("gpsimd", c_cc, CC_Y)
        esBC.close()
        if PHASES < 4:
            pl.w("sync", s_sp, s_sp.n)
            pl.w("sync", c_cc, c_cc.n)
            if DEBUG:
                for src, dst in ((xnc, dbg["dbg_xnc"]), (mg, dbg["dbg_mg"]),
                                 (qaug, dbg["dbg_q"]), (gdr, dbg["dbg_g"]),
                                 (y_out, dbg["dbg_y"]), (d_out, dbg["dbg_d"]),
                                 (mpc, dbg["dbg_mpc"]), (y_in, dbg["dbg_yin"]),
                                 (d_in, dbg["dbg_din"])):
                    pl.op("gpsimd", lambda e, o=dst[:, :], i=src.ap()[:, :]:
                          e.dma_start(out=o, in_=i), sem=s_out, k=16)
            pl.w("gpsimd", s_out, s_out.n)
            return _emit(nc, pl)

        esD = contextlib.ExitStack()
        with esD:
            def sbD(name, shape, dt):
                return esD.enter_context(nc.sbuf_tensor(name, shape, dt))

            den_sb = sbD("den_sb", [32, TOK], F32)
            rden = sbD("rden", [32, TOK], F32)
            sel_sb = sbD("sel_sb", [32, 2048], F32)
            ysb = [sbD(f"ysb{i}", [128, TOK], BF16) for i in range(2)]
            ynorm = sbD("ynorm", [128, 16 * TOK], BF16)
            gx = [sbD(f"gx{i}", [128, D], BF16) for i in range(2)]
            xres = [sbD(f"xres{i}", [128, D], F32) for i in range(2)]
            tsg = sbD("tsg", [128, D], F32)
            osb = [sbD(f"osb{i}", [128, D], F32) for i in range(2)]
            psDum4 = esD.enter_context(nc.psum_tensor("psDum4", [128, 128], F32))

            pl.w("sync", c_cc, CC_Y)
            pl.op("sync", lambda e: e.dma_start(out=sel_sb[:, :], in_=sel_in[:, :]),
                  sem=s_ld, k=16)
            pl.op("sync", lambda e: e.dma_start(out=den_sb[:, :], in_=d_out.ap()[:, :]),
                  sem=s_ld, k=16)
            pl.w("vector", s_ld, s_ld.n)
            pl.op("vector", lambda e: e.reciprocal(rden[:, :], den_sb[:, :]))
            vfl(v_g)
            RDEN = v_g.n

            esD1 = contextlib.ExitStack()
            bcah = [esD1.enter_context(nc.psum_tensor(f"bcah{i}", [128, 1024], F32))
                    for i in range(2)]
            ysb_ld = [0, 0]
            ymul_done = [0, 0]
            hmul_done = [0, 0]
            for c in range(16):
                cb = c % 2
                if c < 2:
                    pl.op("sync", lambda e, o=ysb[cb][:, :],
                          i=y_out.ap()[128 * c:128 * (c + 1), :]:
                          e.dma_start(out=o, in_=i), sem=s_ld, k=16)
                    ysb_ld[cb] = s_ld.n
                for h in range(2):
                    hb = (2 * c + h) % 2
                    pl.w("tensor", s_ld, ysb_ld[cb])
                    pl.w("tensor", v_g, RDEN)
                    pl.w("tensor", v_g, hmul_done[hb])
                    for q in range(2):
                        qtr = 2 * h + q
                        pl.op("tensor", lambda e, o=bcah[hb][:, 512 * q:512 * (q + 1)],
                              l=sel_sb[:, 128 * c:128 * (c + 1)],
                              r=rden[:, 512 * qtr:512 * (qtr + 1)]:
                              e.matmul(o, l, r, start=True, stop=True))
                    tfl(psDum4[:, :], t_m)
                    pl.w("vector", t_m, t_m.n)
                    pl.op("vector", lambda e,
                          o=ynorm[:, TOK * c + 1024 * h:TOK * c + 1024 * (h + 1)],
                          a=ysb[cb][:, 1024 * h:1024 * (h + 1)], b2=bcah[hb][:, :]:
                          e.tensor_tensor(o, a, b2, ALU.mult))
                    vfl(v_g)
                    hmul_done[hb] = v_g.n
                ymul_done[cb] = v_g.n
                if c + 2 < 16:
                    pl.w("sync", v_g, ymul_done[cb])
                    pl.op("sync", lambda e, o=ysb[cb][:, :],
                          i=y_out.ap()[128 * (c + 2):128 * (c + 3), :]:
                          e.dma_start(out=o, in_=i), sem=s_ld, k=16)
                    ysb_ld[cb] = s_ld.n
            YN = v_g.n
            esD1.close()
            psS = [esD.enter_context(nc.psum_tensor(f"psS{i}", [128, 512], F32))
                   for i in range(4)]

            pl.w("tensor", v_g, YN)
            gx_ld = [0, 0]
            fin_done = [0, 0]
            for t in range(2):
                tb = t % 2
                pl.op("sync", lambda e, o=gx[tb][:, :],
                      i=gdr.ap()[128 * t:128 * (t + 1), :]:
                      e.dma_start(out=o, in_=i), sem=s_ld, k=16)
                pl.op("sync", lambda e, o=xres[tb][:, :],
                      i=x_own[128 * t:128 * (t + 1), :]:
                      e.dma_start(out=o, in_=i), sem=s_ld, k=16)
                gx_ld[tb] = s_ld.n
            for t in range(NT):
                tb = t % 2
                for half in range(2):
                    pk = 2 * tb + half
                    if t >= 2:
                        pl.w("tensor", v_g, fin_done[tb])
                    for c in range(16):
                        pl.op("tensor", lambda e, o=psS[pk][:, :],
                              l=ynorm[:, TOK * c + 128 * t:TOK * c + 128 * (t + 1)],
                              r=wp_sb[:, D * c + 512 * half:D * c + 512 * (half + 1)],
                              stt=(c == 0), sp=(c == 15):
                              e.matmul(o, l, r, start=stt, stop=sp))
                tfl(psDum4[:, :], t_m)
                SD = t_m.n
                pl.w("vector", t_m, SD)
                pl.w("vector", s_ld, gx_ld[tb])
                pl.w("vector", s_sp, FFN_SP)
                if t >= 2:
                    pl.w("vector", s_out, 16 * (t - 1))
                for half in range(2):
                    pk = 2 * tb + half
                    pl.op("vector", lambda e, o=tsg[:, 512 * half:512 * (half + 1)],
                          a=psS[pk][:, :], b2=gx[tb][:, 512 * half:512 * (half + 1)]:
                          e.tensor_tensor(o, a, b2, ALU.mult), sem=v_g)
                pl.w("vector", v_g, v_g.n)
                pl.op("vector", lambda e, o=osb[tb][:, :], a=tsg[:, :], b2=xres[tb][:, :]:
                      e.tensor_tensor(o, a, b2, ALU.add))
                vfl(v_g)
                fin_done[tb] = v_g.n
                pl.w("sync", v_g, fin_done[tb])
                pl.op("sync", lambda e, o=out_own[128 * t:128 * (t + 1), :], i=osb[tb][:, :]:
                      e.dma_start(out=o, in_=i), sem=s_out, k=16)
                if t + 2 < NT:
                    pl.op("sync", lambda e, o=gx[tb][:, :],
                          i=gdr.ap()[128 * (t + 2):128 * (t + 3), :]:
                          e.dma_start(out=o, in_=i), sem=s_ld, k=16)
                    pl.op("sync", lambda e, o=xres[tb][:, :],
                          i=x_own[128 * (t + 2):128 * (t + 3), :]:
                          e.dma_start(out=o, in_=i), sem=s_ld, k=16)
                    gx_ld[tb] = s_ld.n

            if DEBUG:
                pl.w("gpsimd", s_out, s_out.n)
                pl.w("gpsimd", c_cc, CC_Y)
                for src, dst in ((xnc, dbg["dbg_xnc"]), (mg, dbg["dbg_mg"]),
                                 (qaug, dbg["dbg_q"]), (gdr, dbg["dbg_g"]),
                                 (y_out, dbg["dbg_y"]), (d_out, dbg["dbg_d"]),
                                 (mpc, dbg["dbg_mpc"]), (y_in, dbg["dbg_yin"]),
                                 (d_in, dbg["dbg_din"])):
                    pl.op("gpsimd", lambda e, o=dst[:, :], i=src.ap()[:, :]:
                          e.dma_start(out=o, in_=i), sem=s_out, k=16)

            pl.w("sync", s_out, s_out.n)

        return _emit(nc, pl)


def _emit(nc, pl):
    with allow_ici_all_to_all(), nc.Block() as block:
        @block.sync
        def _(e):
            for f in pl.q["sync"]:
                f(e)

        @block.vector
        def _(e):
            for f in pl.q["vector"]:
                f(e)

        @block.scalar
        def _(e):
            for f in pl.q["scalar"]:
                f(e)

        @block.tensor
        def _(e):
            for f in pl.q["tensor"]:
                f(e)

        @block.gpsimd
        def _(e):
            for f in pl.q["gpsimd"]:
                f(e)

    return nc



def host_inputs(L0_attn_x, L0_attn_w_attn, L0_attn_w_proj,
                L0_ffn_w_freq_real, L0_ffn_w_freq_imag, L0_ffn_bias):
    x = np.asarray(L0_attn_x, dtype=np.float32)
    wa = np.ascontiguousarray(np.asarray(L0_attn_w_attn, dtype=np.float32))
    wp = np.asarray(L0_attn_w_proj, dtype=np.float32)
    wfr = np.ascontiguousarray(np.asarray(L0_ffn_w_freq_real, dtype=np.float32))
    wfi = np.ascontiguousarray(np.asarray(L0_ffn_w_freq_imag, dtype=np.float32))
    fb = np.ascontiguousarray(np.asarray(L0_ffn_bias, dtype=np.float32).reshape(1, D))

    slots, Ls, sl = head_slots()
    perm = np.zeros(D, dtype=np.int64)
    for g in range(4):
        for s in range(4):
            h = slots[s][g]
            perm[256 * g + 64 * s: 256 * g + 64 * (s + 1)] = np.arange(64 * h, 64 * (h + 1))
    wpt1 = wp.T[perm, :]
    wpt = np.ascontiguousarray(np.concatenate([wpt1, wpt1], axis=0))

    jk = (np.arange(D, dtype=np.int64)[:, None] * np.arange(D, dtype=np.int64)[None, :])
    msin = ((((jk) % D) + 512) % D - 512).astype(np.float32)
    mcos = ((((jk + 256) % D) + 512) % D - 512).astype(np.float32)

    karr = (np.arange(128, dtype=np.float64)[:, None] * 64 +
            np.arange(64, dtype=np.float64)[None, :])


    ones32 = np.ones((1, T), dtype=np.float32)

    in_maps = []
    for core in range(NC):
        b, g = core // 4, core % 4
        heads = [slots[s][g] for s in range(4)]
        wqk = np.concatenate([
            np.ascontiguousarray(
                np.concatenate([wa[64 * h:64 * (h + 1), :],
                                wa[D + 64 * h:D + 64 * (h + 1), :]], axis=0).T)
            for h in heads], axis=0)
        wv = np.concatenate([
            np.ascontiguousarray(
                np.concatenate(
                    [wa[2 * D + 64 * heads[2 * p]:2 * D + 64 * (heads[2 * p] + 1), :],
                     wa[2 * D + 64 * heads[2 * p + 1]:2 * D + 64 * (heads[2 * p + 1] + 1), :]],
                    axis=0).T)
            for p in range(2)], axis=0)
        selt = np.zeros((32, 2048), dtype=np.float32)
        gl = np.arange(2048)
        hb = gl // 64
        ownrow = (hb < 16) if b == 0 else (hb >= 16)
        selt[hb[ownrow], gl[ownrow]] = 1.0
        kq = np.zeros((128, 512), dtype=np.float32)
        for s in range(4):
            ss = float(sl[heads[s]])
            kq[:, 64 * s:64 * (s + 1)] = (8.0 * ss * karr).astype(np.float32)
            kq[:, 256 + 64 * s:256 + 64 * (s + 1)] = (-8.0 * ss * karr).astype(np.float32)
        in_maps.append({
            "x_own": np.ascontiguousarray(x[b, 2048 * g:2048 * (g + 1), :]),
            "wqk": wqk, "wv": wv, "wpt": wpt,
            "wfr": wfr, "wfi": wfi, "fbias": fb,
            "mtab_s": msin, "mtab_c": mcos,
            "mtabc_s": np.ascontiguousarray(msin[:, 128 * core:128 * (core + 1)]),
            "mtabc_c": np.ascontiguousarray(mcos[:, 128 * core:128 * (core + 1)]),
            "kq128": kq, "seltab": selt, "ones32": ones32,
        })
    return in_maps


_NC_CACHE = [None]


def kernel(L0_attn_x, L0_attn_w_attn, L0_attn_w_proj,
           L0_ffn_w_freq_real, L0_ffn_w_freq_imag, L0_ffn_bias):
    in_maps = host_inputs(L0_attn_x, L0_attn_w_attn, L0_attn_w_proj,
                          L0_ffn_w_freq_real, L0_ffn_w_freq_imag, L0_ffn_bias)
    if _NC_CACHE[0] is None:
        _NC_CACHE[0] = build_nc()
    res = run_bass_kernel_spmd(_NC_CACHE[0], in_maps, core_ids=list(range(NC)))
    out = np.zeros((B, T, D), dtype=np.float32)
    for core in range(NC):
        b, g = core // 4, core % 4
        out[b, 2048 * g:2048 * (g + 1), :] = res.results[core]["out_own"]
    if DEBUG:
        kernel.last = res
    return out
